# version 01
# baseline (speedup 1.0000x reference)
import numpy as np
import ml_dtypes
from contextlib import ExitStack
import concourse.bass as bass
import concourse.mybir as mybir
from concourse.bass_utils import run_bass_kernel_spmd

F32 = mybir.dt.float32
BF16 = mybir.dt.bfloat16
I8 = mybir.dt.int8
AF = mybir.ActivationFunctionType
ALU = mybir.AluOpType
AX = mybir.AxisListType
BF16NP = ml_dtypes.bfloat16

D = 2048
NFC = 16
TOK = 1024
NEG = -1e30
ENGS = ("sync", "gpsimd", "vector", "tensor", "scalar")


class Reg:
    __slots__ = ("w", "rs", "name")

    def __init__(self, name=""):
        self.w = None
        self.rs = []
        self.name = name


class Rec:
    __slots__ = ("eng", "fn", "deps", "dma", "small", "need_inc", "semval", "wdeps", "semkey", "waitlist")

    def __init__(self, eng, fn, deps, dma, small):
        self.eng, self.fn, self.deps, self.dma, self.small = eng, fn, deps, dma, small
        self.need_inc = False
        self.semval = None


class K:
    def __init__(self):
        self.nc = bass.Bass("TRN2", target_bir_lowering=False, debug=False)
        self.recs = []
        self.ctx = ExitStack()

    def sb(self, name, shape, dt, ctx=None):
        return (ctx or self.ctx).enter_context(self.nc.sbuf_tensor(name, shape, dt))

    def ps(self, name, shape, dt=F32, ctx=None):
        return (ctx or self.ctx).enter_context(self.nc.psum_tensor(name, shape, dt))

    def din(self, name, shape, dt):
        return self.nc.dram_tensor(name, shape, dt, kind="ExternalInput").ap()

    def dout(self, name, shape, dt):
        return self.nc.dram_tensor(name, shape, dt, kind="ExternalOutput").ap()

    def dint(self, name, shape, dt):
        return self.nc.dram_tensor(name, shape, dt, kind="Internal").ap()

    def op(self, eng, fn, r=(), w=(), small=False, dma=False):
        deps = set()
        for reg in r:
            if reg.w is not None:
                deps.add(reg.w)
        for reg in w:
            if reg.w is not None:
                deps.add(reg.w)
            deps.update(reg.rs)
        rec = Rec(eng, fn, deps, dma, small)
        self.recs.append(rec)
        for reg in r:
            reg.rs.append(rec)
        for reg in w:
            reg.w = rec
            reg.rs = []
        return rec

    def dma(self, eng, out, in_, r=(), w=(), **kw):
        return self.op(eng, lambda e, o=out, i=in_, kw=kw:
                       e.dma_start(out=o, in_=i, **kw), r=r, w=w, dma=True)

    def barrier(self):
        lasts, dma_last, dcnt = {}, {}, {}
        for rec in self.recs:
            lasts[rec.eng] = rec
            if rec.dma:
                i = dcnt.get(rec.eng, 0)
                dma_last[(rec.eng, i % K.NDS)] = rec
                dcnt[rec.eng] = i + 1
        for eng in ENGS:
            deps = {r2 for e2, r2 in lasts.items() if e2 != eng}
            deps.update(dma_last.values())
            rec = Rec(eng, lambda e: e.nop(), deps, False, True)
            self.recs.append(rec)

    NDS = 16

    def build(self):
        nc = self.nc
        dcnt = {}
        for rec in self.recs:
            if rec.dma:
                i = dcnt.get(rec.eng, 0)
                rec.semval = (rec.eng, i % K.NDS)
                dcnt[rec.eng] = i + 1
        for rec in self.recs:
            rec.wdeps = []
            for dep in rec.deps:
                dep.need_inc = True
                rec.wdeps.append(dep)
        sems, cnt = {}, {}
        for eng in ENGS:
            sems[(eng, "c")] = self.ctx.enter_context(nc.semaphore(f"s_{eng}_c"))
            cnt[(eng, "c")] = 0
            for i in range(K.NDS):
                sems[(eng, "d", i)] = self.ctx.enter_context(
                    nc.semaphore(f"s_{eng}_d{i}"))
                cnt[(eng, "d", i)] = 0
        for rec in self.recs:
            rec.waitlist = []
            for dep in rec.wdeps:
                if dep.dma:
                    key3 = (dep.eng, "d", dep.semkey)
                    rec.waitlist.append((key3, None, dep))
                else:
                    rec.waitlist.append(((dep.eng, "c"), None, dep))
            rec.waitlist = [(key3, (cnt[key3] if key3[1] == "d" else dep.semval), dep)
                            for (key3, _, dep) in rec.waitlist]
            if rec.dma:
                key3 = (rec.eng, "d", rec.semval[1])
                cnt[key3] += 16
                rec.semkey = rec.semval[1]
                rec.semval = cnt[key3]
            elif rec.need_inc:
                cnt[(rec.eng, "c")] += 1
                rec.semval = cnt[(rec.eng, "c")]

        def emit_for(e, recs):
            hwm = {}
            for rec in recs:
                for key3, val, dep in sorted(rec.waitlist, key=lambda t: (str(t[0]), t[1])):
                    if hwm.get(key3, 0) >= val:
                        continue
                    hwm[key3] = val
                    e.wait_ge(sems[key3], val)
                ins = rec.fn(e)
                if rec.dma:
                    ins.then_inc(sems[(rec.eng, "d", rec.semkey)], 16)
                elif rec.need_inc:
                    ins.then_inc(sems[(rec.eng, "c")], 1)

        perengine = {e: [r2 for r2 in self.recs if r2.eng == e] for e in ENGS}
        with nc.Block() as block:
            for eng in ENGS:
                if perengine[eng]:
                    getattr(block, eng)(lambda e, recs=perengine[eng]: emit_for(e, recs))
        return nc

    def run(self, in_maps, core_ids=None, **kw):
        with self.ctx:
            nc = self.build()
        return run_bass_kernel_spmd(nc, in_maps,
                                    core_ids=core_ids or list(range(8)), **kw)


def fsl(t, c, ntok, lo=None, hi=None):
    if lo is None:
        return t[:, c * ntok:(c + 1) * ntok]
    return t[:, c * ntok + lo:c * ntok + hi]


def rearr_feat(dram_ap, p=128):
    return dram_ap.rearrange("(c p) t -> p c t", p=p)


class Consts:
    def __init__(self, k):
        self.ones_col = k.sb("ones_col", [128, 1], F32)
        self.ones_row = k.sb("ones_row", [1, 128], F32)
        self.reg = Reg("consts")
        k.op("gpsimd", lambda g, a=self.ones_col: g.memset(a[:], 1.0),
             w=[self.reg], small=True)
        k.op("gpsimd", lambda g, a=self.ones_row: g.memset(a[:], 1.0),
             w=[self.reg], small=True)


def emit_layernorm(k, sc, C, x_sb, x_regs, ln_sb, ln_regs, ntok, tag):
    nspl = (ntok + 511) // 512
    sq = [k.sb(f"lnsq{i}_{tag}", [128, ntok], F32, ctx=sc) for i in range(2)]
    sqr = [Reg(f"sq{i}") for i in range(2)]
    tmp = [k.sb(f"lntmp{i}_{tag}", [128, ntok], F32, ctx=sc) for i in range(1)]
    tmpr = [Reg(f"lntmp{i}") for i in range(1)]
    srow = k.sb(f"lnrow_{tag}", [1, 4 * ntok], F32, ctx=sc)
    ps_sum = k.ps(f"lnpsum_{tag}", [1, ntok], F32, ctx=sc)
    ps_sq = k.ps(f"lnpssq_{tag}", [1, ntok], F32, ctx=sc)
    ps_bc = k.ps(f"lnpsbc_{tag}", [128, 2 * ntok], F32, ctx=sc)
    mean_r, ex2_r = srow[:, 0:ntok], srow[:, ntok:2 * ntok]
    var_r, rstd_r = srow[:, 2 * ntok:3 * ntok], srow[:, 3 * ntok:4 * ntok]
    r_mean, r_ex2, r_var, r_rstd = Reg("mean"), Reg("ex2"), Reg("var"), Reg("rstd")
    r_pssum, r_pssq, r_psbc = Reg(), Reg(), Reg()

    for c in range(NFC):
        b, br = sq[c % 2], sqr[c % 2]
        k.op("scalar", lambda e, o=b, i=fsl(x_sb, c, ntok):
             e.activation(o[:], i, AF.Square), r=[x_regs[c]], w=[br])
        for j in range(nspl):
            sl = slice(j * 512, min((j + 1) * 512, ntok))
            k.op("tensor", lambda e, p=ps_sum[:, sl], o=C.ones_col[:],
                 i=fsl(x_sb, c, ntok)[:, sl], st=(c == 0), sp=(c == NFC - 1):
                 e.matmul(p, o, i, start=st, stop=sp),
                 r=[x_regs[c], C.reg],
                 w=([r_pssum] if (c == NFC - 1 and j == nspl - 1) or c == 0 else []),
                 small=True)
            k.op("tensor", lambda e, p=ps_sq[:, sl], o=C.ones_col[:], i=b[:, sl],
                 st=(c == 0), sp=(c == NFC - 1):
                 e.matmul(p, o, i, start=st, stop=sp),
                 r=[br, C.reg],
                 w=([r_pssq] if (c == NFC - 1 and j == nspl - 1) or c == 0 else []),
                 small=True)
    k.op("vector", lambda v, o=mean_r, i=ps_sum: v.tensor_scalar_mul(o, i[:], 1.0 / D),
         r=[r_pssum], w=[r_mean], small=True)
    k.op("vector", lambda v, o=ex2_r, i=ps_sq: v.tensor_scalar_mul(o, i[:], 1.0 / D),
         r=[r_pssq], w=[r_ex2], small=True)
    k.op("vector", lambda v, o=var_r, a=mean_r: v.tensor_mul(o, a, a),
         r=[r_mean], w=[r_var], small=True)
    k.op("vector", lambda v, o=var_r, a=ex2_r: v.tensor_sub(o, a, o),
         r=[r_ex2, r_var], w=[r_var], small=True)
    k.op("vector", lambda v, o=var_r: v.tensor_scalar_add(o, o, 1e-5),
         r=[r_var], w=[r_var], small=True)
    k.op("scalar", lambda e, o=rstd_r, i=var_r: e.activation(o, i, AF.Ln),
         r=[r_var], w=[r_rstd], small=True)
    k.op("scalar", lambda e, o=rstd_r, i=rstd_r: e.activation(o, i, AF.Exp, scale=-0.5),
         r=[r_rstd], w=[r_rstd], small=True)
    k.op("vector", lambda v, o=mean_r, a=rstd_r: v.tensor_mul(o, o, a),
         r=[r_rstd, r_mean], w=[r_mean], small=True)
    for j in range(nspl):
        sl = slice(j * 512, min((j + 1) * 512, ntok))
        k.op("tensor", lambda e, p=ps_bc[:, 0:ntok][:, sl], o=C.ones_row[:], i=rstd_r[:, sl]:
             e.matmul(p, o, i, start=True, stop=True),
             r=[r_rstd, C.reg], w=[r_psbc], small=True)
        k.op("tensor", lambda e, p=ps_bc[:, ntok:][:, sl], o=C.ones_row[:], i=mean_r[:, sl]:
             e.matmul(p, o, i, start=True, stop=True),
             r=[r_mean, C.reg], w=[r_psbc], small=True)
    for c in range(NFC):
        t, tr = tmp[0], tmpr[0]
        k.op("vector", lambda v, o=t, i=fsl(x_sb, c, ntok), b2=ps_bc[:, 0:ntok]:
             v.tensor_mul(o[:], i, b2[:]), r=[x_regs[c], r_psbc], w=[tr])
        k.op("vector", lambda v, o=fsl(ln_sb, c, ntok), i=t, b2=ps_bc[:, ntok:]:
             v.tensor_tensor(o, i[:], b2[:], ALU.subtract),
             r=[tr, r_psbc], w=[ln_regs[c]])



def build_ffn0(with_partials):
    k = K()
    x_d = k.din("x_t", [D, TOK], F32)
    if with_partials:
        pa_d = k.din("pa", [D, TOK], F32)
        pb_d = k.din("pb", [D, TOK], F32)
    wg_d = k.din("wg", [D, 8192], I8)
    wu_d = k.din("wu", [D, 8192], I8)
    wd_d = k.din("wdn", [8192, D], I8)
    sg_d = k.din("sg2", [128, 64], F32)
    su_d = k.din("su2", [128, 64], F32)
    sd_d = k.din("sd2", [128, 16], F32)
    xo_d = k.dout("xo", [D, TOK], F32)
    x2_d = k.dint("x2", [D, TOK], F32) if with_partials else x_d

    C = Consts(k)
    scales = k.sb("scales", [128, 144], F32)
    r_sc = Reg("scales")
    ln = k.sb("ln", [128, NFC * TOK], BF16)
    r_ln = [Reg(f"ln{c}") for c in range(NFC)]
    gate = [k.sb(f"gate{i}", [128, 512], BF16) for i in range(4)]
    r_gate = [Reg(f"g{i}") for i in range(4)]
    xres = [k.sb(f"xres{i}", [128, TOK], F32) for i in range(2)]
    r_xres = [Reg(f"xres{i}") for i in range(2)]
    xout = [k.sb(f"xout{i}", [128, 512], F32) for i in range(4)]
    r_xout = [Reg(f"xo{i}") for i in range(4)]
    r_xod = Reg("xo_dram")
    r_x2d = Reg("x2_dram")

    WBLK = 128
    NWB = 8192 // WBLK
    wgb = [k.sb(f"wgb{i}", [128, NFC * WBLK], BF16) for i in range(2)]
    wub = [k.sb(f"wub{i}", [128, NFC * WBLK], BF16) for i in range(2)]
    r_wgb = [Reg(f"wgb{i}") for i in range(2)]
    r_wub = [Reg(f"wub{i}") for i in range(2)]
    OBLK = 128
    NOB = D // OBLK
    wdb = [k.sb(f"wdb{i}", [128, 32 * OBLK], BF16) for i in range(2)]
    r_wdb = [Reg(f"wdb{i}") for i in range(2)]

    wg_src = wg_d.rearrange("(c p) i -> p c i", p=128)
    wu_src = wu_d.rearrange("(c p) i -> p c i", p=128)
    wd_src = wd_d.rearrange("(ic p) o -> p ic o", p=128)

    def wgu_dma(b):
        buf = b % 2
        isl = slice(b * WBLK, (b + 1) * WBLK)
        k.dma("gpsimd", wgb[buf][:].rearrange("p (c i) -> p c i", c=NFC),
              wg_src[:, :, isl], w=[r_wgb[buf]])
        k.dma("gpsimd", wub[buf][:].rearrange("p (c i) -> p c i", c=NFC),
              wu_src[:, :, isl], w=[r_wub[buf]])

    def wd_dma(wave, ob):
        buf = ob % 2
        osl = slice(ob * OBLK, (ob + 1) * OBLK)
        icsl = slice(wave * 32, wave * 32 + 32)
        k.dma("gpsimd", wdb[buf][:].rearrange("p (ic o) -> p ic o", ic=32),
              wd_src[:, icsl, osl], w=[r_wdb[buf]])

    xsc = ExitStack()
    x_sb = k.sb("x", [128, NFC * TOK], F32, ctx=xsc)
    r_x = [Reg(f"x{c}") for c in range(NFC)]
    xsrc = rearr_feat(x_d)
    if with_partials:
        pstg = [k.sb(f"pstg{i}", [128, TOK], F32, ctx=xsc) for i in range(2)]
        r_pstg = [Reg(f"pstg{i}") for i in range(2)]
        pasrc, pbsrc = rearr_feat(pa_d), rearr_feat(pb_d)
        x2dst = rearr_feat(x2_d)
        for c in range(NFC):
            k.dma("sync", x_sb[:, c * TOK:(c + 1) * TOK], xsrc[:, c, :], w=[r_x[c]])
            sa, sb_ = 0, 1
            k.dma("sync", pstg[sa][:], pasrc[:, c, :], w=[r_pstg[sa]])
            k.dma("sync", pstg[sb_][:], pbsrc[:, c, :], w=[r_pstg[sb_]])
            k.op("vector", lambda v, o=fsl(x_sb, c, TOK), a=pstg[sa]:
                 v.tensor_add(o, o, a[:]),
                 r=[r_pstg[sa], r_x[c]], w=[r_x[c]])
            k.op("vector", lambda v, o=fsl(x_sb, c, TOK), b2=pstg[sb_]:
                 v.tensor_add(o, o, b2[:]),
                 r=[r_pstg[sb_], r_x[c]], w=[r_x[c]])
            k.dma("sync", x2dst[:, c, :], x_sb[:, c * TOK:(c + 1) * TOK],
                  r=[r_x[c]], w=[r_x2d])
    else:
        for c in range(NFC):
            k.dma("sync", x_sb[:, c * TOK:(c + 1) * TOK], xsrc[:, c, :], w=[r_x[c]])
    k.dma("sync", scales[:, 0:64], sg_d, w=[r_sc])
    k.dma("sync", scales[:, 64:128], su_d, w=[r_sc])
    k.dma("sync", scales[:, 128:144], sd_d, w=[r_sc])
    k.op("vector", lambda v, o=scales[:, 128:144]: v.tensor_scalar_mul(o, o, 0.5),
         r=[r_sc], w=[r_sc], small=True)

    lnsc = ExitStack()
    emit_layernorm(k, lnsc, C, x_sb, r_x, ln, r_ln, TOK, "f")
    lnsc.close()
    xsc.close()
    k.barrier()

    ps_g = [k.ps(f"psg{i}", [128, 512], F32) for i in range(2)]
    ps_u = [k.ps(f"psu{i}", [128, 512], F32) for i in range(2)]
    ps_d = [k.ps(f"psd{i}", [128, 512], F32) for i in range(2)]
    r_psg = [Reg(f"psg{i}") for i in range(2)]
    r_psu = [Reg(f"psu{i}") for i in range(2)]
    r_psd = [Reg(f"psd{i}") for i in range(2)]


    h = k.sb("h", [128, 32 * TOK], BF16)
    r_h = [Reg(f"h{i}") for i in range(64)]
    acc = k.sb("acc", [128, NFC * TOK], BF16)
    r_acc = [Reg(f"acc{i}") for i in range(32)]

    wgu_dma(0)
    wgu_dma(1)
    GUPB = WBLK // 128
    for wave in range(2):
        for ic in range(32):
            icg = wave * 32 + ic
            b, wo = divmod(icg, GUPB)
            wo *= 128
            buf = b % 2
            if icg % GUPB == 0 and b >= 1 and b + 1 < NWB:
                wgu_dma(b + 1)
            for tt in range(2):
                pg, rpg = ps_g[tt], r_psg[tt]
                for c in range(NFC):
                    k.op("tensor", lambda e, p=pg,
                         wt=wgb[buf][:, c * WBLK + wo:c * WBLK + wo + 128],
                         i=fsl(ln, c, TOK, tt * 512, (tt + 1) * 512),
                         st=(c == 0), sp=(c == NFC - 1):
                         e.matmul(p[:], wt, i, start=st, stop=sp),
                         r=[r_wgb[buf], r_ln[c]],
                         w=([rpg] if c in (0, NFC - 1) else []))
                g = (2 * icg + tt) % 4
                k.op("scalar", lambda e, o=gate[g], i=pg, s1=scales[:, icg:icg + 1]:
                     e.activation(o[:], i[:], AF.Silu, scale=s1),
                     r=[rpg, r_sc], w=[r_gate[g]])
                pu, rpu = ps_u[tt], r_psu[tt]
                for c in range(NFC):
                    k.op("tensor", lambda e, p=pu,
                         wt=wub[buf][:, c * WBLK + wo:c * WBLK + wo + 128],
                         i=fsl(ln, c, TOK, tt * 512, (tt + 1) * 512),
                         st=(c == 0), sp=(c == NFC - 1):
                         e.matmul(p[:], wt, i, start=st, stop=sp),
                         r=[r_wub[buf], r_ln[c]],
                         w=([rpu] if c in (0, NFC - 1) else []))
                k.op("vector", lambda v, o=fsl(h, ic, TOK, tt * 512, (tt + 1) * 512),
                     i=pu, s1=scales[:, 64 + icg:65 + icg], gg=gate[g]:
                     v.scalar_tensor_tensor(o, i[:], s1, gg[:], ALU.mult, ALU.mult),
                     r=[rpu, r_sc, r_gate[g]], w=[r_h[2 * ic + tt]])
        if wave == 0:
            wd_dma(0, 0)
            wd_dma(0, 1)
        for ob in range(NOB):
            buf = ob % 2
            g = wave * NOB + ob
            if g >= 1 and g + 1 < 2 * NOB:
                wd_dma((g + 1) // NOB, (g + 1) % NOB)
            for oc2 in range(OBLK // 128):
                oc = ob * (OBLK // 128) + oc2
                for tt in range(2):
                    pd, rpd = ps_d[tt], r_psd[tt]
                    for ic in range(32):
                        k.op("tensor", lambda e, p=pd,
                             wt=wdb[buf][:, ic * OBLK + oc2 * 128:ic * OBLK + oc2 * 128 + 128],
                             i=fsl(h, ic, TOK, tt * 512, (tt + 1) * 512),
                             st=(ic == 0), sp=(ic == 31):
                             e.matmul(p[:], wt, i, start=st, stop=sp),
                             r=[r_wdb[buf], r_h[2 * ic + tt]],
                             w=([rpd] if ic in (0, 31) else []))
                    asl = fsl(acc, oc, TOK, tt * 512, (tt + 1) * 512)
                    if wave == 0:
                        k.op("vector", lambda v, o=asl, i=pd, s1=scales[:, 128 + oc:129 + oc]:
                             v.tensor_scalar_mul(o, i[:], s1),
                             r=[rpd, r_sc], w=[r_acc[2 * oc + tt]])
                    else:
                        if tt == 0:
                            xr = xres[oc % 2]
                            k.dma("sync", xr[:], rearr_feat(x2_d)[:, oc, :],
                                  r=[r_x2d], w=[r_xres[oc % 2]])
                        j = (2 * oc + tt) % 4
                        k.op("vector", lambda v, o=xout[j], i=pd,
                             s1=scales[:, 128 + oc:129 + oc], a=asl:
                             v.scalar_tensor_tensor(o[:], i[:], s1, a, ALU.mult, ALU.add),
                             r=[rpd, r_sc, r_acc[2 * oc + tt]], w=[r_xout[j]])
                        k.op("vector", lambda v, o=xout[j],
                             a=xres[oc % 2][:, tt * 512:(tt + 1) * 512]:
                             v.tensor_add(o[:], o[:], a),
                             r=[r_xres[oc % 2], r_xout[j]], w=[r_xout[j]])
                        k.dma("sync", rearr_feat(xo_d)[:, oc, tt * 512:(tt + 1) * 512],
                              xout[j][:], r=[r_xout[j]], w=[r_xod])
    k.op("sync", lambda e: e.nop(), r=[r_xod])
    return k



def _shard_x(x_full):
    xt = x_full.reshape(8192, D)
    return [np.ascontiguousarray(xt[c * TOK:(c + 1) * TOK].T) for c in range(8)]


def _unshard_x(parts):
    out = np.empty((8192, D), np.float32)
    for c, p in enumerate(parts):
        out[c * TOK:(c + 1) * TOK] = np.asarray(p).T
    return out.reshape(4, 2048, D)


def _scale2d(s, nch):
    return np.ascontiguousarray(np.asarray(s, np.float32).reshape(nch, 128).T)


def run_ffn0(xs, inputs, partials=None):
    k = build_ffn0(partials is not None)
    shared = dict(
        wg=np.asarray(inputs["L0_ffn_w_gate"]),
        wu=np.asarray(inputs["L0_ffn_w_up"]),
        wdn=np.asarray(inputs["L0_ffn_w_down"]),
        sg2=_scale2d(inputs["L0_ffn_s_gate"], 64),
        su2=_scale2d(inputs["L0_ffn_s_up"], 64),
        sd2=_scale2d(inputs["L0_ffn_s_down"], 16),
    )
    in_maps = []
    for c in range(8):
        m = dict(shared)
        m["x_t"] = xs[c]
        if partials is not None:
            m["pa"], m["pb"] = partials[0][c], partials[1][c]
        in_maps.append(m)
    res = k.run(in_maps)
    return [res.results[c]["xo"] for c in range(8)]


def kernel(**inputs):
    xs = _shard_x(np.asarray(inputs["L0_attn_x"], np.float32))
    xs = run_ffn0(xs, inputs)
    pas, pbs = run_attn0(xs, inputs)
    xs = run_ffn0(xs, inputs, partials=(pas, pbs))
    xs = run_moe(xs, inputs)
    pas, pbs = run_attn1(xs, inputs)
    xs = run_moe(xs, inputs, partials=(pas, pbs))
    return _unshard_x(xs)



def build_attn0(stages=99, nheads=8, sub=99):
    k = K()
    T = 2048
    NH = 8
    x_d = k.din("xseq", [D, T], F32)
    wqt_d = k.din("wqt", [D, NH * 128], F32)
    wkt_d = k.din("wkt", [D, NH * 128], F32)
    wvt_d = k.din("wvt", [D, NH * 128], F32)
    wot_d = k.din("wot", [NH * 128, D], F32)
    gq_d = k.din("gq", [128, 1], F32)
    gk_d = k.din("gk", [128, 1], F32)
    po_d = k.dout("po", [D, T], F32)

    C = Consts(k)
    ocbf = k.sb("ocbf", [128, 1], BF16)
    orbf = k.sb("orbf", [1, 128], BF16)
    i_bf = k.sb("i_bf", [128, 128], BF16)
    i_f32 = k.sb("i_f32", [128, 128], F32)
    dmask = k.sb("dmask", [128, 128], BF16)
    tmask = k.sb("tmask", [128, 4 * 512], BF16)
    gqk = k.sb("gqk", [128, 2], F32)
    rC = Reg("aconst")

    def gop(fn):
        k.op("gpsimd", fn, w=[rC], small=True)
    gop(lambda g: g.memset(ocbf[:], 1.0))
    gop(lambda g: g.memset(orbf[:], 1.0))
    gop(lambda g: g.memset(i_bf[:], 0.0))
    gop(lambda g: g.affine_select(out=i_bf[:], in_=i_bf[:], compare_op=ALU.not_equal,
                                  fill=1.0, base=0, pattern=[[-1, 128]], channel_multiplier=1))
    gop(lambda g: g.memset(i_f32[:], 0.0))
    gop(lambda g: g.affine_select(out=i_f32[:], in_=i_f32[:], compare_op=ALU.not_equal,
                                  fill=1.0, base=0, pattern=[[-1, 128]], channel_multiplier=1))
    gop(lambda g: g.memset(dmask[:], 0.0))
    gop(lambda g: g.affine_select(out=dmask[:], in_=dmask[:], compare_op=ALU.is_ge,
                                  fill=NEG, base=0, pattern=[[-1, 128]], channel_multiplier=1))
    for j in range(4):
        gop(lambda g, j=j: g.memset(tmask[:, j * 512:(j + 1) * 512], 0.0))
        if j > 0:
            gop(lambda g, j=j: g.affine_select(
                out=tmask[:, j * 512:(j + 1) * 512], in_=tmask[:, j * 512:(j + 1) * 512],
                compare_op=ALU.is_ge, fill=NEG, base=-j * 128,
                pattern=[[1, 512]], channel_multiplier=0))
        gop(lambda g, j=j: g.affine_select(
            out=tmask[:, j * 512 + j * 128:j * 512 + (j + 1) * 128],
            in_=tmask[:, j * 512 + j * 128:j * 512 + (j + 1) * 128],
            compare_op=ALU.is_ge, fill=NEG, base=0,
            pattern=[[1, 128]], channel_multiplier=-1))
    k.dma("sync", gqk[:, 0:1], gq_d, w=[rC])
    k.dma("sync", gqk[:, 1:2], gk_d, w=[rC])

    ob = k.sb("ob", [128, NH * T], BF16)
    r_ob = [Reg(f"ob{i}") for i in range(NH * 4)]

    main = ExitStack()
    ln = k.sb("lnA", [128, NFC * T], BF16, ctx=main)
    r_ln = [Reg(f"ln{i}") for i in range(NFC * 2)]

    lnsc = ExitStack()
    stg = [k.sb(f"stgA{i}", [128, TOK], BF16, ctx=lnsc) for i in range(2)]
    r_stg = [Reg(f"stg{i}") for i in range(2)]
    sqb = [k.sb(f"sqA{i}", [128, TOK], F32, ctx=lnsc) for i in range(2)]
    r_sqb = [Reg(f"sq{i}") for i in range(2)]
    srow = k.sb("rowA", [1, 4 * TOK], F32, ctx=lnsc)
    ps_sum = k.ps("psLs", [1, TOK], F32, ctx=lnsc)
    ps_sq = k.ps("psLq", [1, TOK], F32, ctx=lnsc)
    ps_bc = k.ps("psLb", [128, 2 * TOK], F32, ctx=lnsc)
    mean_r, ex2_r = srow[:, 0:TOK], srow[:, TOK:2 * TOK]
    var_r, rstd_r = srow[:, 2 * TOK:3 * TOK], srow[:, 3 * TOK:4 * TOK]
    xsrc = rearr_feat(x_d)
    for half in range(2):
        r_mean, r_var, r_ex2, r_rstd = Reg(), Reg(), Reg(), Reg()
        r_pssum, r_pssq, r_psbc = Reg(), Reg(), Reg()
        tsl = slice(half * TOK, (half + 1) * TOK)
        for c in range(NFC):
            s, rs = stg[c % 2], r_stg[c % 2]
            k.dma("gpsimd", s[:], xsrc[:, c, tsl], w=[rs])
            b, rb = sqb[c % 2], r_sqb[c % 2]
            k.op("scalar", lambda e, o=b, i=s: e.activation(o[:], i[:], AF.Square),
                 r=[rs], w=[rb])
            for j in range(2):
                sl = slice(j * 512, (j + 1) * 512)
                k.op("tensor", lambda e, p=ps_sum[:, sl], o=ocbf[:], i=s[:, sl],
                     st=(c == 0), sp=(c == NFC - 1):
                     e.matmul(p, o, i, start=st, stop=sp, skip_group_check=True),
                     r=[rs, rC], w=([r_pssum] if c in (0, NFC - 1) else []), small=True)
                k.op("tensor", lambda e, p=ps_sq[:, sl], o=C.ones_col[:], i=b[:, sl],
                     st=(c == 0), sp=(c == NFC - 1):
                     e.matmul(p, o, i, start=st, stop=sp, skip_group_check=True),
                     r=[rb, C.reg], w=([r_pssq] if c in (0, NFC - 1) else []), small=True)
        k.op("vector", lambda v, o=mean_r, i=ps_sum: v.tensor_scalar_mul(o, i[:], 1.0 / D),
             r=[r_pssum], w=[r_mean], small=True)
        k.op("vector", lambda v, o=ex2_r, i=ps_sq: v.tensor_scalar_mul(o, i[:], 1.0 / D),
             r=[r_pssq], w=[r_ex2], small=True)
        k.op("vector", lambda v, o=var_r, a=mean_r: v.tensor_mul(o, a, a),
             r=[r_mean], w=[r_var], small=True)
        k.op("vector", lambda v, o=var_r, a=ex2_r: v.tensor_sub(o, a, o),
             r=[r_ex2, r_var], w=[r_var], small=True)
        k.op("vector", lambda v, o=var_r: v.tensor_scalar_add(o, o, 1e-5),
             r=[r_var], w=[r_var], small=True)
        k.op("scalar", lambda e, o=rstd_r, i=var_r: e.activation(o, i, AF.Ln),
             r=[r_var], w=[r_rstd], small=True)
        k.op("scalar", lambda e, o=rstd_r, i=rstd_r: e.activation(o, i, AF.Exp, scale=-0.5),
             r=[r_rstd], w=[r_rstd], small=True)
        k.op("vector", lambda v, o=mean_r, a=rstd_r: v.tensor_mul(o, o, a),
             r=[r_rstd, r_mean], w=[r_mean], small=True)
        for j in range(2):
            sl = slice(j * 512, (j + 1) * 512)
            k.op("tensor", lambda e, p=ps_bc[:, 0:TOK][:, sl], o=C.ones_row[:], i=rstd_r[:, sl]:
                 e.matmul(p, o, i, start=True, stop=True),
                 r=[r_rstd, C.reg], w=[r_psbc], small=True)
            k.op("tensor", lambda e, p=ps_bc[:, TOK:][:, sl], o=C.ones_row[:], i=mean_r[:, sl]:
                 e.matmul(p, o, i, start=True, stop=True),
                 r=[r_mean, C.reg], w=[r_psbc], small=True)
        for c in range(NFC):
            s, rs = stg[c % 2], r_stg[c % 2]
            k.dma("gpsimd", s[:], xsrc[:, c, tsl], w=[rs])
            b, rb = sqb[c % 2], r_sqb[c % 2]
            k.op("vector", lambda v, o=b, i=s, b2=ps_bc[:, 0:TOK]:
                 v.tensor_mul(o[:], i[:], b2[:]), r=[rs, r_psbc], w=[rb])
            k.op("vector", lambda v, o=ln[:, c * T + half * TOK:c * T + (half + 1) * TOK],
                 i=b, b2=ps_bc[:, TOK:]:
                 v.tensor_tensor(o, i[:], b2[:], ALU.subtract),
                 r=[rb, r_psbc], w=[r_ln[2 * c + half]])
    lnsc.close()
    k.barrier()
    if stages <= 1:
        main.close()
        return k

    p512 = [k.ps(f"p512_{i}", [128, 512], F32) for i in range(4)]
    rp512 = [Reg(f"p512_{i}") for i in range(4)]
    prow = [k.ps(f"prow{i}", [1, 512], F32) for i in range(2)]
    rprow = [Reg(f"prow{i}") for i in range(2)]
    ps_l = k.ps("ps_l", [1, 512], F32)
    rps_l = Reg("ps_l")
    ps_o = k.ps("ps_o", [128, 512], F32)
    rps_o = Reg("ps_o")

    hsc = ExitStack()
    qg = k.sb("qg", [128, T], BF16, ctx=hsc)
    knh = k.sb("knh", [128, T], BF16, ctx=hsc)
    vth = k.sb("vth", [128, T], BF16, ctx=hsc)
    r_qg, r_kn, r_vt = Reg("qg"), Reg("kn"), Reg("vt")
    wbh = [k.sb(f"wbh{i}", [128, NFC * 128], BF16, ctx=hsc) for i in range(2)]
    r_wbh = [Reg(f"wbh{i}") for i in range(2)]
    sq2 = [k.sb(f"sq2_{i}", [128, 512], F32, ctx=hsc) for i in range(2)]
    r_sq2 = [Reg(f"sq2{i}") for i in range(2)]
    rowq = k.sb("rowq", [1, 2 * T], F32, ctx=hsc)
    mpart = k.sb("mpart", [128, 8], F32, ctx=hsc)
    r_mpart = Reg("mpart")
    negm = k.sb("negm", [1, T], F32, ctx=hsc)
    negmb = k.sb("negmb", [1, T], BF16, ctx=hsc)
    r_negm, r_negmb = Reg("negm"), Reg("negmb")
    expb = [k.sb(f"expb{i}", [128, 512], BF16, ctx=hsc) for i in range(2)]
    r_expb = [Reg(f"expb{i}") for i in range(2)]
    rlrow = k.sb("rlrow", [1, 512], F32, ctx=hsc)
    r_rlrow = Reg("rlrow")
    rlb = k.sb("rlb", [128, 512], F32, ctx=hsc)
    r_rlb = Reg("rlb")

    wq_src = wqt_d.rearrange("(c p) o -> p c o", p=128)
    wk_src = wkt_d.rearrange("(c p) o -> p c o", p=128)
    wv_src = wvt_d.rearrange("(c p) o -> p c o", p=128)

    def lnr(c):
        return [r_ln[2 * c], r_ln[2 * c + 1]]

    wload = 0
    for h in range(min(NH, nheads)):
        bq, bk = wload % 2, (wload + 1) % 2
        wload += 2
        k.dma("gpsimd", wbh[bq][:].rearrange("p (c o) -> p c o", c=NFC),
              wq_src[:, :, h * 128:(h + 1) * 128], w=[r_wbh[bq]])
        k.dma("gpsimd", wbh[bk][:].rearrange("p (c o) -> p c o", c=NFC),
              wk_src[:, :, h * 128:(h + 1) * 128], w=[r_wbh[bk]])
        ssq_q, ssq_k = rowq[:, 0:T], rowq[:, T:2 * T]
        r_ssq, r_row = Reg("ssq"), Reg("wrow")
        for wi, (buf, dst, rdst, gcol, roff) in enumerate(
                ((bq, qg, r_qg, gqk[:, 0:1], 0), (bk, knh, r_kn, gqk[:, 1:2], T))):
            for tt in range(4):
                pp, rp = p512[tt % 2], rp512[tt % 2]
                for c in range(NFC):
                    k.op("tensor", lambda e, p=pp, w=wbh[buf][:, c * 128:(c + 1) * 128],
                         i=ln[:, c * T + tt * 512:c * T + (tt + 1) * 512],
                         st=(c == 0), sp=(c == NFC - 1):
                         e.matmul(p[:], w, i, start=st, stop=sp, skip_group_check=True),
                         r=[r_wbh[buf]] + lnr(c), w=([rp] if c in (0, NFC - 1) else []))
                b2, rb2 = sq2[tt % 2], r_sq2[tt % 2]
                k.op("scalar", lambda e, o=b2, i=pp: e.activation(o[:], i[:], AF.Square),
                     r=[rp], w=[rb2, rp])
                k.op("vector", lambda v, o=dst[:, tt * 512:(tt + 1) * 512], i=pp, s1=gcol:
                     v.tensor_scalar_mul(o, i[:], s1), r=[rp, rC], w=[rdst])
                pr, rr = prow[tt % 2], rprow[tt % 2]
                k.op("tensor", lambda e, p=pr, o=C.ones_col[:], i=b2:
                     e.matmul(p[:], o, i[:], start=True, stop=True),
                     r=[rb2, C.reg], w=[rr], small=True)
                k.op("vector", lambda v, o=rowq[:, roff + tt * 512:roff + (tt + 1) * 512],
                     i=pr: v.tensor_scalar_mul(o, i[:], 1.0 / 128),
                     r=[rr], w=[r_ssq], small=True)
        for wi, roff in ((0, 0), (1, T)):
            sl0 = slice(roff, roff + T)
            k.op("vector", lambda v, o=rowq[:, sl0]: v.tensor_scalar_add(o, o, 1e-5),
                 r=[r_ssq], w=[r_ssq], small=True)
            k.op("scalar", lambda e, o=rowq[:, sl0], i=rowq[:, sl0]: e.activation(o, i, AF.Ln),
                 r=[r_ssq], w=[r_ssq], small=True)
            k.op("scalar", lambda e, o=rowq[:, sl0], i=rowq[:, sl0], s=-0.5:
                 e.activation(o, i, AF.Exp, scale=s), r=[r_ssq], w=[r_ssq], small=True)
            if wi == 0:
                k.op("vector", lambda v, o=rowq[:, sl0]:
                     v.tensor_scalar_mul(o, o, float(128 ** -0.5)),
                     r=[r_ssq], w=[r_ssq], small=True)
        for wi, (dst, rdst, roff) in enumerate(((qg, r_qg, 0), (knh, r_kn, T))):
            for tt in range(4):
                pb2, rb3 = p512[2 + tt % 2], rp512[2 + tt % 2]
                k.op("tensor", lambda e, p=pb2, o=C.ones_row[:],
                     i=rowq[:, roff + tt * 512:roff + (tt + 1) * 512]:
                     e.matmul(p[:], o, i, start=True, stop=True),
                     r=[r_ssq, C.reg], w=[rb3], small=True)
                k.op("vector", lambda v, o=dst[:, tt * 512:(tt + 1) * 512], i=pb2:
                     v.tensor_mul(o, o, i[:]), r=[rb3, rdst], w=[rdst])
        bv = wload % 2
        wload += 1
        k.dma("gpsimd", wbh[bv][:].rearrange("p (c o) -> p c o", c=NFC),
              wv_src[:, :, h * 128:(h + 1) * 128], w=[r_wbh[bv]])
        for tc in range(16):
            pp, rp = p512[tc % 2], rp512[tc % 2]
            for c in range(NFC):
                k.op("tensor", lambda e, p=pp[:, 0:128],
                     w=ln[:, c * T + tc * 128:c * T + (tc + 1) * 128],
                     i=wbh[bv][:, c * 128:(c + 1) * 128],
                     st=(c == 0), sp=(c == NFC - 1):
                     e.matmul(p, w, i, start=st, stop=sp, skip_group_check=True),
                     r=[r_wbh[bv]] + lnr(c), w=([rp] if c in (0, NFC - 1) else []))
            k.op("vector", lambda v, o=vth[:, tc * 128:(tc + 1) * 128], i=pp[:, 0:128]:
                 v.tensor_copy(o, i), r=[rp], w=[r_vt])
        if stages <= 2:
            continue
        for qc in range(16):
            ntile = qc // 4 + 1
            for kt in range(ntile):
                wdt = 512 if kt < qc // 4 else (qc % 4 + 1) * 128
                diag = (kt == qc // 4)
                pp, rp = p512[kt % 2], rp512[kt % 2]
                k.op("tensor", lambda e, p=pp[:, 0:wdt], w=qg[:, qc * 128:(qc + 1) * 128],
                     i=knh[:, kt * 512:kt * 512 + wdt], sp=(not diag):
                     e.matmul(p, w, i, start=True, stop=sp, skip_group_check=True),
                     r=[r_qg, r_kn], w=[rp])
                if diag:
                    k.op("tensor", lambda e, p=pp[:, wdt - 128:wdt], o=i_bf[:], i=dmask[:]:
                         e.matmul(p, o, i, start=False, stop=True, skip_group_check=True),
                         r=[rC], w=[rp])
            for kt in range(ntile):
                wdt = 512 if kt < qc // 4 else (qc % 4 + 1) * 128
                pp, rp = p512[kt % 2], rp512[kt % 2]
                k.op("vector", lambda v, o=mpart[:, kt:kt + 1], i=pp[:, 0:wdt]:
                     v.reduce_max(o, i, axis=AX.X), r=[rp], w=[r_mpart], small=True)
            k.op("vector", lambda v, o=mpart[:, 4:5], i=mpart[:, 0:ntile]:
                 v.reduce_max(o, i, axis=AX.X),
                 r=[r_mpart], w=[r_mpart], small=True)
            k.op("vector", lambda v, o=mpart[:, 4:5]: v.tensor_scalar_mul(o, o, -1.0),
                 r=[r_mpart], w=[r_mpart], small=True)
            pr, rr = prow[0], rprow[0]
            k.op("tensor", lambda e, p=pr[:, 0:128], o=mpart[:, 4:5], i=i_f32[:]:
                 e.matmul(p, o, i, start=True, stop=True),
                 r=[r_mpart, rC], w=[rr], small=True)
            k.op("vector", lambda v, o=negm[:, qc * 128:(qc + 1) * 128], i=pr[:, 0:128]:
                 v.tensor_copy(o, i), r=[rr], w=[r_negm], small=True)
        k.op("vector", lambda v, o=negmb[:], i=negm: v.tensor_copy(o[:], i[:]),
             r=[r_negm], w=[r_negmb], small=True)
        if stages <= 3:
            continue
        for qt in range(4):
            nkc = qt * 4 + 4
            for kc in range(nkc):
                pp, rp = p512[kc % 2], rp512[kc % 2]
                k.op("tensor", lambda e, p=pp, w=knh[:, kc * 128:(kc + 1) * 128],
                     i=qg[:, qt * 512:(qt + 1) * 512]:
                     e.matmul(p[:], w, i, start=True, stop=False, skip_group_check=True),
                     r=[r_qg, r_kn], w=[rp])
                k.op("tensor", lambda e, p=pp, o=orbf[:],
                     i=negmb[:, qt * 512:(qt + 1) * 512], last=(kc < qt * 4):
                     e.matmul(p[:], o, i, start=False, stop=last, skip_group_check=True),
                     r=[r_negmb], w=[rp])
                if kc >= qt * 4:
                    j = kc - qt * 4
                    k.op("tensor", lambda e, p=pp, o=i_bf[:], i=tmask[:, j * 512:(j + 1) * 512]:
                         e.matmul(p[:], o, i, start=False, stop=True, skip_group_check=True),
                         r=[rC], w=[rp])
                eb, re = expb[kc % 2], r_expb[kc % 2]
                k.op("scalar", lambda e, o=eb, i=pp: e.activation(o[:], i[:], AF.Exp),
                     r=[rp], w=[re, rp])
                k.op("tensor", lambda e, p=ps_l, o=ocbf[:], i=eb,
                     st=(kc == 0), sp=(kc == nkc - 1):
                     e.matmul(p[:], o, i[:], start=st, stop=sp, skip_group_check=True),
                     r=[re, rC], w=([rps_l] if kc in (0, nkc - 1) else []), small=True)
                k.op("tensor", lambda e, p=ps_o, w=vth[:, kc * 128:(kc + 1) * 128], i=eb,
                     st=(kc == 0), sp=(kc == nkc - 1):
                     e.matmul(p[:], w, i[:], start=st, stop=sp, skip_group_check=True),
                     r=[re, r_vt], w=([rps_o] if kc in (0, nkc - 1) else []))
            k.op("scalar", lambda e, o=rlrow[:], i=ps_l: e.activation(o, i[:], AF.Ln),
                 r=[rps_l], w=[r_rlrow, rps_l], small=True)
            k.op("scalar", lambda e, o=rlrow[:], i=rlrow: e.activation(o, i[:], AF.Exp, scale=-1.0),
                 r=[r_rlrow], w=[r_rlrow], small=True)
            pb3, rb3 = p512[2], rp512[2]
            k.op("tensor", lambda e, p=pb3, o=C.ones_row[:], i=rlrow[:]:
                 e.matmul(p[:], o, i, start=True, stop=True),
                 r=[r_rlrow, C.reg], w=[rb3], small=True)
            k.op("vector", lambda v, o=rlb, i=pb3: v.tensor_copy(o[:], i[:]),
                 r=[rb3], w=[r_rlb])
            k.op("vector", lambda v, o=ob[:, h * T + qt * 512:h * T + (qt + 1) * 512],
                 i=ps_o, b2=rlb: v.tensor_mul(o, i[:], b2[:]),
                 r=[rps_o, r_rlb], w=[r_ob[h * 4 + qt]])
    hsc.close()
    k.barrier()
    if stages <= 4:
        main.close()
        return k

    osc = ExitStack()
    wo_sb = k.sb("wo", [128, NH * D], BF16, ctx=osc)
    r_wo = Reg("wo")
    postg = [k.sb(f"postg{i}", [128, 512], F32, ctx=osc) for i in range(2)]
    r_postg = [Reg(f"postg{i}") for i in range(2)]
    r_pod = Reg("po_dram")
    wo_src = wot_d.rearrange("(hh p) o -> p hh o", p=128)
    for hh in range(NH):
        k.dma("gpsimd", wo_sb[:, hh * D:(hh + 1) * D].rearrange("p (a o) -> p a o", a=1),
              wo_src[:, hh:hh + 1, :], w=[r_wo])
    podst = rearr_feat(po_d)
    NHH = min(NH, nheads)
    for oc in range(NFC):
        for tt in range(4):
            pp, rp = p512[tt % 2], rp512[tt % 2]
            for hh in range(NHH):
                k.op("tensor", lambda e, p=pp, w=wo_sb[:, hh * D + oc * 128:hh * D + (oc + 1) * 128],
                     i=ob[:, hh * T + tt * 512:hh * T + (tt + 1) * 512],
                     st=(hh == 0), sp=(hh == NHH - 1):
                     e.matmul(p[:], w, i, start=st, stop=sp, skip_group_check=True),
                     r=[r_wo, r_ob[hh * 4 + tt]], w=([rp] if hh in (0, NHH - 1) else []))
            g2 = (2 * oc + tt) % 2
            k.op("vector", lambda v, o=postg[g2], i=pp: v.tensor_copy(o[:], i[:]),
                 r=[rp], w=[r_postg[g2]])
            k.dma("sync", podst[:, oc, tt * 512:(tt + 1) * 512], postg[g2][:],
                  r=[r_postg[g2]], w=[r_pod])
    k.op("sync", lambda e: e.nop(), r=[r_pod])
    osc.close()
    main.close()
    return k


def run_attn0(xs, inputs):
    wq = np.asarray(inputs["L0_attn_Wq"], np.float32)
    wk = np.asarray(inputs["L0_attn_Wk"], np.float32)
    wv = np.asarray(inputs["L0_attn_Wv"], np.float32)
    wo = np.asarray(inputs["L0_attn_Wo"], np.float32)
    wqt = np.ascontiguousarray(wq.T)
    wkt = np.ascontiguousarray(wk.T)
    wvt = np.ascontiguousarray(wv.T)
    wot = np.ascontiguousarray(wo.T)
    gq = np.ascontiguousarray(np.asarray(inputs["L0_attn_gq"], np.float32).reshape(128, 1))
    gk = np.ascontiguousarray(np.asarray(inputs["L0_attn_gk"], np.float32).reshape(128, 1))
    k = build_attn0()
    in_maps = []
    for c in range(8):
        s, hg = c // 2, c % 2
        xseq = np.ascontiguousarray(np.concatenate([xs[2 * s], xs[2 * s + 1]], axis=1))
        cols = slice(hg * 1024, (hg + 1) * 1024)
        in_maps.append(dict(
            xseq=xseq,
            wqt=np.ascontiguousarray(wqt[:, cols]),
            wkt=np.ascontiguousarray(wkt[:, cols]),
            wvt=np.ascontiguousarray(wvt[:, cols]),
            wot=np.ascontiguousarray(wot[cols, :]),
            gq=gq, gk=gk,
        ))
    res = k.run(in_maps)
    pos = [np.asarray(res.results[c]["po"]) for c in range(8)]
    pas, pbs = [], []
    for c in range(8):
        s, h = c // 2, c % 2
        tsl = slice(h * 1024, (h + 1) * 1024)
        pas.append(np.ascontiguousarray(pos[2 * s][:, tsl]))
        pbs.append(np.ascontiguousarray(pos[2 * s + 1][:, tsl]))
    return pas, pbs



def build_moe(with_partials):
    k = K()
    x_d = k.din("x_t", [D, TOK], F32)
    if with_partials:
        pa_d = k.din("pa", [D, TOK], F32)
        pb_d = k.din("pb", [D, TOK], F32)
    rw_d = k.din("rw", [D, 8], F32)
    gu_d = k.din("guw", [8, D, 2048], F32)
    dw_d = k.din("dww", [8, 1024, D], F32)
    sg_d = k.din("sgw", [D, 1024], F32)
    su_d = k.din("suw", [D, 1024], F32)
    sd_d = k.din("sdw", [1024, D], F32)
    xo_d = k.dout("xo", [D, TOK], F32)
    x2_d = k.dint("x2", [D, TOK], F32) if with_partials else x_d

    C = Consts(k)
    i_bf = k.sb("i_bfM", [128, 128], BF16)
    rC = Reg("mconst")
    k.op("gpsimd", lambda g: g.memset(i_bf[:], 0.0), w=[rC], small=True)
    k.op("gpsimd", lambda g: g.affine_select(out=i_bf[:], in_=i_bf[:],
         compare_op=ALU.not_equal, fill=1.0, base=0, pattern=[[-1, 128]],
         channel_multiplier=1), w=[rC], small=True)

    ln = k.sb("lnM", [128, NFC * TOK], BF16)
    r_ln = [Reg(f"ln{c}") for c in range(NFC)]
    probs = k.sb("probs", [128, 8 * 8], F32)
    r_probs = Reg("probs")
    gtbuf = [k.sb(f"gtM{i}", [128, 512], BF16) for i in range(2)]
    r_gt = [Reg(f"gt{i}") for i in range(2)]
    xres = [k.sb(f"xresM{i}", [128, TOK], F32) for i in range(2)]
    r_xres = [Reg(f"xres{i}") for i in range(2)]
    xout = [k.sb(f"xoutM{i}", [128, 512], F32) for i in range(4)]
    r_xout = [Reg(f"xo{i}") for i in range(4)]
    r_xod = Reg("xo_dram")
    r_x2d = Reg("x2_dram")

    xsc = ExitStack()
    x_sb = k.sb("xM", [128, NFC * TOK], F32, ctx=xsc)
    r_x = [Reg(f"x{c}") for c in range(NFC)]
    xsrc = rearr_feat(x_d)
    if with_partials:
        pstg = [k.sb(f"pstgM{i}", [128, TOK], F32, ctx=xsc) for i in range(2)]
        r_pstg = [Reg(f"pstg{i}") for i in range(2)]
        pasrc, pbsrc = rearr_feat(pa_d), rearr_feat(pb_d)
        x2dst = rearr_feat(x2_d)
        for c in range(NFC):
            k.dma("sync", x_sb[:, c * TOK:(c + 1) * TOK], xsrc[:, c, :], w=[r_x[c]])
            k.dma("sync", pstg[0][:], pasrc[:, c, :], w=[r_pstg[0]])
            k.dma("sync", pstg[1][:], pbsrc[:, c, :], w=[r_pstg[1]])
            k.op("vector", lambda v, o=fsl(x_sb, c, TOK), a=pstg[0]:
                 v.tensor_add(o, o, a[:]), r=[r_pstg[0], r_x[c]], w=[r_x[c]])
            k.op("vector", lambda v, o=fsl(x_sb, c, TOK), b2=pstg[1]:
                 v.tensor_add(o, o, b2[:]), r=[r_pstg[1], r_x[c]], w=[r_x[c]])
            k.dma("sync", x2dst[:, c, :], x_sb[:, c * TOK:(c + 1) * TOK],
                  r=[r_x[c]], w=[r_x2d])
    else:
        for c in range(NFC):
            k.dma("sync", x_sb[:, c * TOK:(c + 1) * TOK], xsrc[:, c, :], w=[r_x[c]])

    lnsc = ExitStack()
    emit_layernorm(k, lnsc, C, x_sb, r_x, ln, r_ln, TOK, "m")
    lnsc.close()
    xsc.close()
    k.barrier()

    acc = k.sb("accM", [128, NFC * TOK], F32)
    r_acc = [Reg(f"acc{i}") for i in range(32)]
    htok = k.sb("htok", [128, 8 * 1024], BF16)
    r_htok = [Reg(f"ht{i}") for i in range(8)]
    hT = k.sb("hT", [128, 8 * 1024], BF16)
    r_hT = [Reg(f"hT{i}") for i in range(16)]
    wgu = [k.sb(f"wguM{i}", [128, NFC * 512], BF16) for i in range(2)]
    r_wgu = [Reg(f"wgu{i}") for i in range(2)]
    wdn = [k.sb(f"wdnM{i}", [128, 8 * 256], BF16) for i in range(2)]
    r_wdn = [Reg(f"wdn{i}") for i in range(2)]

    ps_z = [k.ps(f"pszM{i}", [128, 512], F32) for i in range(2)]
    r_psz = [Reg(f"psz{i}") for i in range(2)]
    ps_t = [k.ps(f"pstM{i}", [128, 512], BF16) for i in range(2)]
    r_pst = [Reg(f"pst{i}") for i in range(2)]
    ps_d = [k.ps(f"psdM{i}", [128, 512], F32) for i in range(2)]
    r_psd = [Reg(f"psd{i}") for i in range(2)]
    ps_r = k.ps("psrM", [128, 8], F32)
    r_psr = Reg("psr")

    rw_sb = k.sb("rwM", [128, NFC * 8], BF16)
    r_rw = Reg("rw")
    k.dma("gpsimd", rw_sb[:].rearrange("p (c e) -> p c e", c=NFC),
          rw_d.rearrange("(c p) e -> p c e", p=128), w=[r_rw])
    lgt = k.sb("lgtM", [128, 4 * 8], F32)
    r_lgt = Reg("lgt")
    for tc in range(8):
        for c in range(NFC):
            k.op("tensor", lambda e, p=ps_r, w=fsl(ln, c, TOK, tc * 128, (tc + 1) * 128),
                 i=rw_sb[:, c * 8:(c + 1) * 8], st=(c == 0), sp=(c == NFC - 1):
                 e.matmul(p[:], w, i, start=st, stop=sp, skip_group_check=True),
                 r=[r_rw] + [r_ln[c]], w=([r_psr] if c in (0, NFC - 1) else []))
        lg, m1, m2, msk = lgt[:, 0:8], lgt[:, 8:9], lgt[:, 9:10], lgt[:, 16:24]
        k.op("vector", lambda v, o=lg, i=ps_r: v.tensor_copy(o, i[:]),
             r=[r_psr], w=[r_lgt], small=True)
        k.op("vector", lambda v, o=m1, i=lg: v.reduce_max(o, i, axis=AX.X),
             r=[r_lgt], w=[r_lgt], small=True)
        k.op("vector", lambda v, o=msk, i=lg, s=m1: v.tensor_scalar(o, i, s, None, ALU.is_ge),
             r=[r_lgt], w=[r_lgt], small=True)
        k.op("vector", lambda v, o=msk, i=msk, a=lg:
             v.scalar_tensor_tensor(o, i, NEG, a, ALU.mult, ALU.add),
             r=[r_lgt], w=[r_lgt], small=True)
        k.op("vector", lambda v, o=m2, i=msk: v.reduce_max(o, i, axis=AX.X),
             r=[r_lgt], w=[r_lgt], small=True)
        k.op("scalar", lambda e, o=msk, i=lg: e.activation(o, i, AF.Sigmoid),
             r=[r_lgt], w=[r_lgt], small=True)
        k.op("vector", lambda v, o=lg, i=lg, s=m2: v.tensor_scalar(o, i, s, None, ALU.is_ge),
             r=[r_lgt], w=[r_lgt], small=True)
        k.op("vector", lambda v, o=probs[:, tc * 8:(tc + 1) * 8], i=lg, a=msk:
             v.tensor_mul(o, i, a), r=[r_lgt], w=[r_probs], small=True)

    gu_src = gu_d.rearrange("e (c p) i -> e p c i", p=128)
    dw_src = dw_d.rearrange("e (ic p) o -> e p ic o", p=128)
    sg_src = sg_d.rearrange("(c p) i -> p c i", p=128)
    su_src = su_d.rearrange("(c p) i -> p c i", p=128)
    sd_src = sd_d.rearrange("(ic p) o -> p ic o", p=128)

    for e in range(9):
        for pair in range(2):
            gsl = pair * 512
            if e == 0:
                k.dma("gpsimd", wgu[0][:].rearrange("p (c i) -> p c i", c=NFC),
                      sg_src[:, :, gsl:gsl + 512], w=[r_wgu[0]])
                k.dma("gpsimd", wgu[1][:].rearrange("p (c i) -> p c i", c=NFC),
                      su_src[:, :, gsl:gsl + 512], w=[r_wgu[1]])
            else:
                k.dma("gpsimd", wgu[0][:].rearrange("p (c i) -> p c i", c=NFC),
                      gu_src[e - 1][:, :, gsl:gsl + 512], w=[r_wgu[0]])
                k.dma("gpsimd", wgu[1][:].rearrange("p (c i) -> p c i", c=NFC),
                      gu_src[e - 1][:, :, 1024 + gsl:1024 + gsl + 512], w=[r_wgu[1]])
            for tc in range(8):
                pz, rz = ps_z[0], r_psz[0]
                for c in range(NFC):
                    k.op("tensor", lambda e2, p=pz, w=fsl(ln, c, TOK, tc * 128, (tc + 1) * 128),
                         i=wgu[0][:, c * 512:(c + 1) * 512],
                         st=(c == 0), sp=(c == NFC - 1):
                         e2.matmul(p[:], w, i, start=st, stop=sp, skip_group_check=True),
                         r=[r_wgu[0], r_ln[c]], w=([rz] if c in (0, NFC - 1) else []))
                g2 = tc % 2
                if e == 0:
                    k.op("scalar", lambda e2, o=gtbuf[g2], i=pz:
                         e2.activation(o[:], i[:], AF.Silu), r=[rz], w=[r_gt[g2], rz])
                else:
                    k.op("scalar", lambda e2, o=gtbuf[g2], i=pz,
                         s1=probs[:, tc * 8 + (e - 1):tc * 8 + e]:
                         e2.activation(o[:], i[:], AF.Silu, scale=s1),
                         r=[rz, r_probs], w=[r_gt[g2], rz])
                pu, ru = ps_z[1], r_psz[1]
                for c in range(NFC):
                    k.op("tensor", lambda e2, p=pu, w=fsl(ln, c, TOK, tc * 128, (tc + 1) * 128),
                         i=wgu[1][:, c * 512:(c + 1) * 512],
                         st=(c == 0), sp=(c == NFC - 1):
                         e2.matmul(p[:], w, i, start=st, stop=sp, skip_group_check=True),
                         r=[r_wgu[1], r_ln[c]], w=([ru] if c in (0, NFC - 1) else []))
                if e == 0:
                    k.op("vector", lambda v, o=htok[:, tc * 1024 + gsl:tc * 1024 + gsl + 512],
                         i=pu, g3=gtbuf[g2]: v.tensor_tensor(o, i[:], g3[:], ALU.mult),
                         r=[ru, r_gt[g2]], w=[r_htok[tc]])
                else:
                    k.op("vector", lambda v, o=htok[:, tc * 1024 + gsl:tc * 1024 + gsl + 512],
                         i=pu, s1=probs[:, tc * 8 + (e - 1):tc * 8 + e], g3=gtbuf[g2]:
                         v.scalar_tensor_tensor(o, i[:], s1, g3[:], ALU.mult, ALU.mult),
                         r=[ru, r_probs, r_gt[g2]], w=[r_htok[tc]])
        for tc in range(8):
            for grp in range(2):
                pt, rt = ps_t[grp], r_pst[grp]
                for s4 in range(4):
                    ic = grp * 4 + s4
                    k.op("tensor", lambda e2, p=pt[:, s4 * 128:(s4 + 1) * 128],
                         i=htok[:, tc * 1024 + ic * 128:tc * 1024 + (ic + 1) * 128],
                         ii=i_bf[:]: e2.transpose(p, i, ii),
                         r=[r_htok[tc], rC], w=[rt])
                dst = hT[:].rearrange("p (ic t) -> p ic t", ic=8)
                k.op("vector", lambda v, o=dst[:, grp * 4:(grp + 1) * 4, tc * 128:(tc + 1) * 128],
                     i=pt[:].rearrange("p (s t) -> p s t", s=4): v.tensor_copy(o, i),
                     r=[rt], w=[r_hT[ic2 * 2 + tc // 4] for ic2 in range(grp * 4, grp * 4 + 4)])
        for ob in range(8):
            buf = ob % 2
            osl = slice(ob * 256, (ob + 1) * 256)
            if e == 0:
                k.dma("gpsimd", wdn[buf][:].rearrange("p (ic o) -> p ic o", ic=8),
                      sd_src[:, :, osl], w=[r_wdn[buf]])
            else:
                k.dma("gpsimd", wdn[buf][:].rearrange("p (ic o) -> p ic o", ic=8),
                      dw_src[e - 1][:, :, osl], w=[r_wdn[buf]])
            for oc2 in range(2):
                oc = ob * 2 + oc2
                for tt in range(2):
                    pd, rd = ps_d[tt], r_psd[tt]
                    for ic in range(8):
                        k.op("tensor", lambda e2, p=pd,
                             w=wdn[buf][:, ic * 256 + oc2 * 128:ic * 256 + (oc2 + 1) * 128],
                             i=hT[:].rearrange("p (ic2 t) -> p ic2 t", ic2=8)[:, ic, tt * 512:(tt + 1) * 512],
                             st=(ic == 0), sp=(ic == 7):
                             e2.matmul(p[:], w, i, start=st, stop=sp, skip_group_check=True),
                             r=[r_wdn[buf], r_hT[ic * 2 + tt]],
                             w=([rd] if ic in (0, 7) else []))
                    asl = fsl(acc, oc, TOK, tt * 512, (tt + 1) * 512)
                    if e == 0:
                        k.op("vector", lambda v, o=asl, i=pd: v.tensor_copy(o, i[:]),
                             r=[rd], w=[r_acc[2 * oc + tt]])
                    else:
                        k.op("vector", lambda v, o=asl, i=pd: v.tensor_add(o, o, i[:]),
                             r=[rd, r_acc[2 * oc + tt]], w=[r_acc[2 * oc + tt]])
    for oc in range(NFC):
        xr = xres[oc % 2]
        k.dma("sync", xr[:], rearr_feat(x2_d)[:, oc, :], r=[r_x2d], w=[r_xres[oc % 2]])
        for tt in range(2):
            j = (2 * oc + tt) % 4
            k.op("vector", lambda v, o=xout[j], i=fsl(acc, oc, TOK, tt * 512, (tt + 1) * 512),
                 a=xr[:, tt * 512:(tt + 1) * 512]:
                 v.scalar_tensor_tensor(o[:], i, 0.5, a, ALU.mult, ALU.add),
                 r=[r_acc[2 * oc + tt], r_xres[oc % 2]], w=[r_xout[j]])
            k.dma("sync", rearr_feat(xo_d)[:, oc, tt * 512:(tt + 1) * 512],
                  xout[j][:], r=[r_xout[j]], w=[r_xod])
    k.op("sync", lambda e: e.nop(), r=[r_xod])
    return k


def run_moe(xs, inputs, partials=None):
    k = build_moe(partials is not None)
    shared = dict(
        rw=np.ascontiguousarray(np.asarray(inputs["L1_ffn_router_w"], np.float32)),
        guw=np.ascontiguousarray(np.asarray(inputs["L1_ffn_gate_up_w"], np.float32)),
        dww=np.ascontiguousarray(np.asarray(inputs["L1_ffn_down_w"], np.float32)),
        sgw=np.ascontiguousarray(np.asarray(inputs["L1_ffn_shared_gate_w"], np.float32)),
        suw=np.ascontiguousarray(np.asarray(inputs["L1_ffn_shared_up_w"], np.float32)),
        sdw=np.ascontiguousarray(np.asarray(inputs["L1_ffn_shared_down_w"], np.float32)),
    )
    in_maps = []
    for c in range(8):
        m = dict(shared)
        m["x_t"] = xs[c]
        if partials is not None:
            m["pa"], m["pb"] = partials[0][c], partials[1][c]
        in_maps.append(m)
    res = k.run(in_maps)
    return [res.results[c]["xo"] for c in range(8)]



def build_attn1(nheads=6, nchunks=16):
    k = K()
    T = 2048
    NH = nheads
    DK, DV = 128, 256
    NQC = NH
    NVC = NH * 2
    CP = 2056
    PAD = 8
    x_d = k.din("xseq", [D, T], F32)
    wq_d = k.din("wq", [D, NH * DK], F32)
    wk_d = k.din("wk", [D, NH * DK], F32)
    wv_d = k.din("wv", [D, NH * DV], F32)
    wg_d = k.din("wgg", [D, NH * DV], F32)
    wb_d = k.din("wbb", [D, NH], F32)
    wa_d = k.din("waa", [D, NH], F32)
    wo_d = k.din("wo", [NH * DV, D], F32)
    cwq_d = k.din("cwq", [128, NQC * 4], F32)
    cwk_d = k.din("cwk", [128, NQC * 4], F32)
    cwv_d = k.din("cwv", [128, NVC * 4], F32)
    aldt_d = k.din("aldt", [NH, 2], F32)
    nw_d = k.din("normw", [1, DV], F32)
    po_d = k.dout("po", [D, T], F32)
    gg_d = k.dint("ggscr", [T, NH * DV], BF16)
    onT_d = k.dint("onTscr", [NH * DV, T], BF16)

    C = Consts(k)
    ocbf = k.sb("ocbf1", [128, 1], BF16)
    orbf = k.sb("orbf1", [1, 128], BF16)
    i_bf = k.sb("i_bf1", [128, 128], BF16)
    i_f32 = k.sb("i_f321", [128, 128], F32)
    ltones = k.sb("ltones", [128, 128], F32)
    mstrict = k.sb("mstrict", [128, 128], F32)
    mincl = k.sb("mincl", [128, 128], F32)
    rC = Reg("a1const")

    def gop(fn):
        k.op("gpsimd", fn, w=[rC], small=True)
    gop(lambda g: g.memset(ocbf[:], 1.0))
    gop(lambda g: g.memset(orbf[:], 1.0))
    gop(lambda g: g.memset(i_bf[:], 0.0))
    gop(lambda g: g.affine_select(out=i_bf[:], in_=i_bf[:], compare_op=ALU.not_equal,
                                  fill=1.0, base=0, pattern=[[-1, 128]], channel_multiplier=1))
    gop(lambda g: g.memset(i_f32[:], 0.0))
    gop(lambda g: g.affine_select(out=i_f32[:], in_=i_f32[:], compare_op=ALU.not_equal,
                                  fill=1.0, base=0, pattern=[[-1, 128]], channel_multiplier=1))
    gop(lambda g: g.memset(ltones[:], 1.0))
    gop(lambda g: g.affine_select(out=ltones[:], in_=ltones[:], compare_op=ALU.is_ge,
                                  fill=0.0, base=0, pattern=[[1, 128]], channel_multiplier=-1))
    gop(lambda g: g.memset(mstrict[:], 0.0))
    gop(lambda g: g.affine_select(out=mstrict[:], in_=mstrict[:], compare_op=ALU.is_gt,
                                  fill=NEG, base=0, pattern=[[-1, 128]], channel_multiplier=1))
    gop(lambda g: g.memset(mincl[:], 0.0))
    gop(lambda g: g.affine_select(out=mincl[:], in_=mincl[:], compare_op=ALU.is_ge,
                                  fill=NEG, base=0, pattern=[[1, 128]], channel_multiplier=-1))

    qpre = k.sb("qpre", [128, NQC * CP], BF16)
    kpre = k.sb("kpre", [128, NQC * CP], BF16)
    vpre = k.sb("vpre", [128, NVC * CP], BF16)
    r_qpre = [Reg(f"qpre{i}") for i in range(NQC)]
    r_kpre = [Reg(f"kpre{i}") for i in range(NQC)]
    r_vpre = [Reg(f"vpre{i}") for i in range(NVC)]
    b_raw = k.sb("b_raw", [NH, T], F32)
    a_raw = k.sb("a_raw", [NH, T], F32)
    g_rows = k.sb("g_rows", [NH, T], F32)
    beta_rows = b_raw
    r_braw, r_araw, r_grows = Reg(), Reg(), Reg()
    r_beta = r_braw
    aldt = k.sb("aldt_sb", [NH, 2], F32)
    negexpA = k.sb("negexpA", [NH, 1], F32)
    r_aldt = Reg("aldt")
    nwb = k.sb("nwb", [128, DV], F32)
    r_nwb = Reg("nwb")

    for c in range(NQC):
        k.op("gpsimd", lambda g, o=qpre[:, c * CP:c * CP + PAD]: g.memset(o, 0.0),
             w=[r_qpre[c]], small=True)
        k.op("gpsimd", lambda g, o=kpre[:, c * CP:c * CP + PAD]: g.memset(o, 0.0),
             w=[r_kpre[c]], small=True)
    for c in range(NVC):
        k.op("gpsimd", lambda g, o=vpre[:, c * CP:c * CP + PAD]: g.memset(o, 0.0),
             w=[r_vpre[c]], small=True)
    k.dma("sync", aldt[:], aldt_d, w=[r_aldt])

    main = ExitStack()
    ln = k.sb("lnA1", [128, NFC * TOK], BF16, ctx=main)
    r_ln = [Reg(f"ln{i}") for i in range(NFC)]
    lnsc = ExitStack()
    stg = [k.sb(f"stgA1{i}", [128, TOK], BF16, ctx=lnsc) for i in range(2)]
    r_stg = [Reg(f"stg{i}") for i in range(2)]
    sqb = [k.sb(f"sqA1{i}", [128, TOK], F32, ctx=lnsc) for i in range(2)]
    r_sqb = [Reg(f"sq{i}") for i in range(2)]
    srow = k.sb("rowA1", [1, 3 * TOK], F32, ctx=lnsc)
    P8 = [k.ps(f"P8_{i}", [128, 512], F32, ctx=lnsc) for i in range(8)]
    rP8 = [Reg(f"P8_{i}") for i in range(8)]
    wbh = [k.sb(f"wbh1{i}", [128, NFC * 128], BF16, ctx=lnsc) for i in range(2)]
    r_wbh = [Reg(f"wbh{i}") for i in range(2)]
    ps_p = [P8[0], P8[1]]
    r_psp = [rP8[0], rP8[1]]
    ps_ba = P8[2]
    r_psba = rP8[2]
    wgb = [k.sb(f"wgb1{i}", [128, NFC * 256], BF16, ctx=lnsc) for i in range(2)]
    r_wgb = [Reg(f"wgb{i}") for i in range(2)]
    ggst = [k.sb(f"ggst{i}", [128, 256], BF16, ctx=lnsc) for i in range(2)]
    r_ggst = [Reg(f"ggst{i}") for i in range(2)]
    r_ggd = Reg("ggd")
    mean_r, ex2_r = srow[:, 0:TOK], srow[:, TOK:2 * TOK]
    work_r = srow[:, 2 * TOK:3 * TOK]
    xsrc = rearr_feat(x_d)
    srcs = dict(q=wq_d.rearrange("(c p) o -> p c o", p=128),
                k2=wk_d.rearrange("(c p) o -> p c o", p=128),
                v=wv_d.rearrange("(c p) o -> p c o", p=128))
    wg_src = wg_d.rearrange("(c p) o -> p c o", p=128)
    gg_dst = gg_d.rearrange("(tc p) o -> p tc o", p=128)
    targets = [("q", qpre, r_qpre, NQC), ("k2", kpre, r_kpre, NQC), ("v", vpre, r_vpre, NVC)]
    NGS = (NH * DV) // 256
    wi = 0
    for half in range(2):
        r_mean, r_work, r_ex2 = Reg(), Reg(), Reg()
        tsl = slice(half * TOK, (half + 1) * TOK)
        for c in range(NFC):
            s, rs = stg[c % 2], r_stg[c % 2]
            k.dma("gpsimd", s[:], xsrc[:, c, tsl], w=[rs])
            b, rb = sqb[c % 2], r_sqb[c % 2]
            k.op("scalar", lambda e, o=b, i=s: e.activation(o[:], i[:], AF.Square),
                 r=[rs], w=[rb])
            for j in range(2):
                sl = slice(j * 512, (j + 1) * 512)
                k.op("tensor", lambda e, p=P8[0 + j][0:1, :], o=ocbf[:], i=s[:, sl],
                     st=(c == 0), sp=(c == NFC - 1):
                     e.matmul(p, o, i, start=st, stop=sp, skip_group_check=True),
                     r=[rs, rC], w=([rP8[0 + j]] if c in (0, NFC - 1) else []), small=True)
                k.op("tensor", lambda e, p=P8[2 + j][0:1, :], o=C.ones_col[:], i=b[:, sl],
                     st=(c == 0), sp=(c == NFC - 1):
                     e.matmul(p, o, i, start=st, stop=sp, skip_group_check=True),
                     r=[rb, C.reg], w=([rP8[2 + j]] if c in (0, NFC - 1) else []), small=True)
        for j in range(2):
            sl = slice(j * 512, (j + 1) * 512)
            k.op("vector", lambda v, o=mean_r[:, sl], i=P8[0 + j][0:1, :]:
                 v.tensor_scalar_mul(o, i, 1.0 / D), r=[rP8[0 + j]], w=[r_mean], small=True)
            k.op("vector", lambda v, o=ex2_r[:, sl], i=P8[2 + j][0:1, :]:
                 v.tensor_scalar_mul(o, i, 1.0 / D), r=[rP8[2 + j]], w=[r_ex2], small=True)
        k.op("vector", lambda v, o=work_r, a=mean_r: v.tensor_mul(o, a, a),
             r=[r_mean], w=[r_work], small=True)
        k.op("vector", lambda v, o=work_r, a=ex2_r: v.tensor_sub(o, a, o),
             r=[r_ex2, r_work], w=[r_work], small=True)
        k.op("vector", lambda v, o=work_r: v.tensor_scalar_add(o, o, 1e-5),
             r=[r_work], w=[r_work], small=True)
        k.op("scalar", lambda e, o=work_r, i=work_r: e.activation(o, i, AF.Ln),
             r=[r_work], w=[r_work], small=True)
        k.op("scalar", lambda e, o=work_r, i=work_r: e.activation(o, i, AF.Exp, scale=-0.5),
             r=[r_work], w=[r_work], small=True)
        k.op("vector", lambda v, o=mean_r, a=work_r: v.tensor_mul(o, o, a),
             r=[r_work, r_mean], w=[r_mean], small=True)
        for j in range(2):
            sl = slice(j * 512, (j + 1) * 512)
            k.op("tensor", lambda e, p=P8[4 + j][:], o=C.ones_row[:], i=work_r[:, sl]:
                 e.matmul(p, o, i, start=True, stop=True),
                 r=[r_work, C.reg], w=[rP8[4 + j]], small=True)
            k.op("tensor", lambda e, p=P8[6 + j][:], o=C.ones_row[:], i=mean_r[:, sl]:
                 e.matmul(p, o, i, start=True, stop=True),
                 r=[r_mean, C.reg], w=[rP8[6 + j]], small=True)
        for c in range(NFC):
            s, rs = stg[c % 2], r_stg[c % 2]
            k.dma("gpsimd", s[:], xsrc[:, c, tsl], w=[rs])
            b, rb = sqb[c % 2], r_sqb[c % 2]
            for j in range(2):
                sl = slice(j * 512, (j + 1) * 512)
                k.op("vector", lambda v, o=b[:, sl], i=s[:, sl], b2=P8[4 + j]:
                     v.tensor_mul(o, i, b2[:]), r=[rs, rP8[4 + j]], w=[rb])
                k.op("vector", lambda v, o=fsl(ln, c, TOK)[:, sl], i=b[:, sl], b2=P8[6 + j]:
                     v.tensor_tensor(o, i, b2[:], ALU.subtract),
                     r=[rb, rP8[6 + j]], w=[r_ln[c]])
        for name, dst, rdst, nch in targets:
            for ch in range(nch):
                buf = wi % 2
                wi += 1
                k.dma("gpsimd", wbh[buf][:].rearrange("p (c o) -> p c o", c=NFC),
                      srcs[name][:, :, ch * 128:(ch + 1) * 128], w=[r_wbh[buf]])
                for tt2 in range(2):
                    tt = half * 2 + tt2
                    pp, rp = ps_p[tt2 % 2], r_psp[tt2 % 2]
                    for c in range(NFC):
                        k.op("tensor", lambda e, p=pp, w=wbh[buf][:, c * 128:(c + 1) * 128],
                             i=fsl(ln, c, TOK, tt2 * 512, (tt2 + 1) * 512),
                             st=(c == 0), sp=(c == NFC - 1):
                             e.matmul(p[:], w, i, start=st, stop=sp, skip_group_check=True),
                             r=[r_wbh[buf], r_ln[c]], w=([rp] if c in (0, NFC - 1) else []))
                    k.op("vector", lambda v, o=dst[:, ch * CP + PAD + tt * 512:ch * CP + PAD + (tt + 1) * 512],
                         i=pp: v.tensor_copy(o, i[:]), r=[rp], w=[rdst[ch]])
        for name, wsrc_d, dst, rdst in (("b", wb_d, b_raw, r_braw), ("a", wa_d, a_raw, r_araw)):
            buf = wi % 2
            wi += 1
            k.dma("gpsimd", wbh[buf][:, 0:NFC * NH].rearrange("p (c o) -> p c o", c=NFC),
                  wsrc_d.rearrange("(c p) o -> p c o", p=128), w=[r_wbh[buf]])
            for tt2 in range(2):
                tt = half * 2 + tt2
                for c in range(NFC):
                    k.op("tensor", lambda e, p=ps_ba[0:NH, :], w=wbh[buf][:, c * NH:(c + 1) * NH],
                         i=fsl(ln, c, TOK, tt2 * 512, (tt2 + 1) * 512),
                         st=(c == 0), sp=(c == NFC - 1):
                         e.matmul(p, w, i, start=st, stop=sp, skip_group_check=True),
                         r=[r_wbh[buf], r_ln[c]], w=([r_psba] if c in (0, NFC - 1) else []))
                k.op("vector", lambda v, o=dst[:, tt * 512:(tt + 1) * 512], i=ps_ba[0:NH, :]:
                     v.tensor_copy(o, i), r=[r_psba], w=[rdst], small=True)
        for ns in range(NGS):
            buf = ns % 2
            k.dma("gpsimd", wgb[buf][:].rearrange("p (c o) -> p c o", c=NFC),
                  wg_src[:, :, ns * 256:(ns + 1) * 256], w=[r_wgb[buf]])
            for tc2 in range(8):
                tc = half * 8 + tc2
                pp, rp = ps_p[tc2 % 2], r_psp[tc2 % 2]
                for c in range(NFC):
                    k.op("tensor", lambda e, p=pp[:, 0:256], w=fsl(ln, c, TOK, tc2 * 128, (tc2 + 1) * 128),
                         i=wgb[buf][:, c * 256:(c + 1) * 256],
                         st=(c == 0), sp=(c == NFC - 1):
                         e.matmul(p, w, i, start=st, stop=sp, skip_group_check=True),
                         r=[r_wgb[buf], r_ln[c]], w=([rp] if c in (0, NFC - 1) else []))
                sg2 = tc2 % 2
                k.op("vector", lambda v, o=ggst[sg2], i=pp[:, 0:256]: v.tensor_copy(o[:], i),
                     r=[rp], w=[r_ggst[sg2]])
                k.dma("sync", gg_dst[:, tc, ns * 256:(ns + 1) * 256], ggst[sg2][:],
                      r=[r_ggst[sg2]], w=[r_ggd])
    lnsc.close()

    main.close()
    k.barrier()
    return _attn1_tail(k, locals())


def _attn1_tail(k, L):
    g = lambda n: L[n]
    NH, T, DK, DV, CP = g("NH"), g("T"), g("DK"), g("DV"), g("CP")
    PAD = g("PAD")
    NQC, NVC = g("NQC"), g("NVC")
    nchunks = g("nchunks")
    qpre, kpre, vpre = g("qpre"), g("kpre"), g("vpre")
    r_qpre, r_kpre, r_vpre = g("r_qpre"), g("r_kpre"), g("r_vpre")
    b_raw, a_raw, g_rows, beta_rows = g("b_raw"), g("a_raw"), g("g_rows"), g("beta_rows")
    r_braw, r_araw, r_grows, r_beta = g("r_braw"), g("r_araw"), g("r_grows"), g("r_beta")
    aldt, negexpA, r_aldt = g("aldt"), g("negexpA"), g("r_aldt")
    nwb, r_nwb = g("nwb"), g("r_nwb")
    C, rC = g("C"), g("rC")
    ocbf, orbf, i_bf, i_f32 = g("ocbf"), g("orbf"), g("i_bf"), g("i_f32")
    ltones, mstrict, mincl = g("ltones"), g("mstrict"), g("mincl")
    cwq_d, cwk_d, cwv_d, nw_d = g("cwq_d"), g("cwk_d"), g("cwv_d"), g("nw_d")
    wo_d, po_d, gg_d, onT_d = g("wo_d"), g("po_d"), g("gg_d"), g("onT_d")

    k.op("scalar", lambda e, o=beta_rows[:], i=b_raw[:]: e.activation(o, i, AF.Sigmoid),
         r=[r_braw], w=[r_beta], small=True)
    k.op("scalar", lambda e, o=negexpA[:], i=aldt[:, 0:1]: e.activation(o[:], i, AF.Exp),
         r=[r_aldt], w=[r_aldt], small=True)
    k.op("vector", lambda v, o=negexpA[:]: v.tensor_scalar_mul(o[:], o[:], -1.0),
         r=[r_aldt], w=[r_aldt], small=True)
    k.op("vector", lambda v, o=a_raw[:], s=aldt[:, 1:2]: v.tensor_scalar_add(o, o, s),
         r=[r_araw, r_aldt], w=[r_araw], small=True)
    k.op("scalar", lambda e, o=g_rows[:], i=a_raw[:]: e.activation(o, i, AF.Abs),
         r=[r_araw], w=[r_grows], small=True)
    k.op("scalar", lambda e, o=g_rows[:], i=g_rows[:]: e.activation(o, i, AF.Exp, scale=-1.0),
         r=[r_grows], w=[r_grows], small=True)
    k.op("vector", lambda v, o=g_rows[:]: v.tensor_scalar_add(o, o, 1.0),
         r=[r_grows], w=[r_grows], small=True)
    k.op("scalar", lambda e, o=g_rows[:], i=g_rows[:]: e.activation(o, i, AF.Ln),
         r=[r_grows], w=[r_grows], small=True)
    k.op("vector", lambda v, o=a_raw[:], i=a_raw[:]: v.tensor_scalar_max(o, i, 0.0),
         r=[r_araw], w=[r_araw], small=True)
    k.op("vector", lambda v, o=g_rows[:], a=a_raw[:]: v.tensor_add(o, g_rows[:], a),
         r=[r_araw, r_grows], w=[r_grows], small=True)
    k.op("vector", lambda v, o=g_rows[:], s=negexpA[:, 0:1]: v.tensor_scalar_mul(o, g_rows[:], s),
         r=[r_grows, r_aldt], w=[r_grows], small=True)

    csc = ExitStack()
    cw = k.sb("cw1", [128, (2 * NQC + NVC) * 4], F32, ctx=csc)
    r_cw = Reg("cw")
    k.dma("sync", cw[:, 0:NQC * 4], cwq_d, w=[r_cw])
    k.dma("sync", cw[:, NQC * 4:2 * NQC * 4], cwk_d, w=[r_cw])
    k.dma("sync", cw[:, 2 * NQC * 4:], cwv_d, w=[r_cw])
    NDG = (2 * NQC + NVC) * 4
    dg = k.sb("dg1", [128, NDG * 128], BF16, ctx=csc)
    r_dg = Reg("dg")
    for i in range(NDG):
        k.op("vector", lambda v, o=dg[:, i * 128:(i + 1) * 128], a=i_bf[:],
             s=cw[:, i:i + 1]: v.tensor_scalar_mul(o, a, s),
             r=[r_cw, rC], w=[r_dg], small=True)

    qn = k.sb("qn1", [128, NH * 128], BF16, ctx=csc)
    kn = k.sb("kn1", [128, NH * 128], BF16, ctx=csc)
    vcv = k.sb("vcv1", [128, NH * 256], BF16, ctx=csc)
    r_qn, r_kn, r_vcv = Reg("qn"), Reg("kn"), Reg("vcv")
    ktok = k.sb("ktok1", [128, NH * 128], BF16, ctx=csc)
    vtok = k.sb("vtok1", [128, NH * 256], BF16, ctx=csc)
    r_ktok, r_vtok = Reg("ktok"), Reg("vtok")
    gcols = k.sb("gcols1", [128, NH], F32, ctx=csc)
    bcols = k.sb("bcols1", [128, NH], F32, ctx=csc)
    egc = k.sb("egc1", [128, 4 * NH], F32, ctx=csc)
    r_gcols, r_bcols, r_egc = Reg("gcols"), Reg("bcols"), Reg("egc")
    grows0 = k.sb("grows0", [1, 2 * NH * 128], F32, ctx=csc)
    r_grows0 = Reg("grows0")
    bcrow = k.sb("bcrow", [1, NH], F32, ctx=csc)
    r_bcrow = Reg("bcrow")
    dts = k.sb("dts1", [128, NH * 128], BF16, ctx=csc)
    das = k.sb("das1", [128, NH * 128], BF16, ctx=csc)
    r_dts, r_das = Reg("dts"), Reg("das")
    Asb = k.sb("Asb1", [128, NH * 128], BF16, ctx=csc)
    ATsb = k.sb("ATsb1", [128, NH * 128], BF16, ctx=csc)
    Pow = k.sb("Pow1", [128, NH * 128], BF16, ctx=csc)
    PowT = k.sb("PowT1", [128, NH * 128], BF16, ctx=csc)
    PT = k.sb("PT1", [128, NH * 128], BF16, ctx=csc)
    r_A, r_AT, r_Pow, r_PowT, r_PT = Reg("A"), Reg("AT"), Reg("Pow"), Reg("PowT"), Reg("PT")
    Wneg = k.sb("Wneg1", [128, NH * 128], BF16, ctx=csc)
    r_Wneg = Reg("Wneg")
    Kbp = k.sb("Kbp1", [128, NH * 128], BF16, ctx=csc)
    Kde = k.sb("Kde1", [128, NH * 128], BF16, ctx=csc)
    r_Kbp, r_Kde = Reg("Kbp"), Reg("Kde")
    Vb = k.sb("Vb1", [128, NH * 256], BF16, ctx=csc)
    r_Vb = Reg("Vb")
    Usb = k.sb("Usb1", [128, NH * 256], BF16, ctx=csc)
    r_U = Reg("U")
    Attn = k.sb("Attn1", [128, NH * 128], BF16, ctx=csc)
    r_Attn = Reg("Attn")
    Osb = k.sb("Osb1", [128, NH * 256], BF16, ctx=csc)
    r_O = Reg("O")
    S_sb = k.sb("S_sb1", [128, NH * 256], F32, ctx=csc)
    S_bf = k.sb("S_bf1", [128, NH * 256], BF16, ctx=csc)
    r_S, r_Sbf = Reg("S"), Reg("Sbf")
    onst = k.sb("onst1", [128, NH * 256], BF16, ctx=csc)
    r_onst = Reg("onst")
    rowsc = k.sb("rowsc1", [1, NH * 128], F32, ctx=csc)
    r_rowsc = Reg("rowsc")
    rstdc = k.sb("rstdc1", [128, 2 * NH], F32, ctx=csc)
    r_rstdc = Reg("rstdc")
    r_onTd = Reg("onTd")
    r_ggdr = L["r_ggd"]

    PA = [k.ps(f"PA1{i}", [128, 1024], F32, ctx=csc) for i in range(2)]
    r_PA = [Reg(f"PA{i}") for i in range(2)]
    PB = [k.ps(f"PB1{i}", [128, 512], F32, ctx=csc) for i in range(2)]
    r_PB = [Reg(f"PB{i}") for i in range(2)]
    PTr = k.ps("PTr1", [128, 512], BF16, ctx=csc)
    r_PTr = Reg("PTr")
    PRow = k.ps("PRow1", [1, 512], F32, ctx=csc)
    r_PRow = Reg("PRow")

    nwrow = k.sb("nwrow", [1, DV], F32, ctx=csc)
    r_nwrow = Reg("nwrow")
    k.dma("sync", nwrow[:], nw_d, w=[r_nwrow])
    k.op("tensor", lambda e, p=PB[0][:, 0:DV], o=C.ones_row[:], i=nwrow[:]:
         e.matmul(p, o, i, start=True, stop=True), r=[r_nwrow, C.reg], w=[r_PB[0]],
         small=True)
    k.op("vector", lambda v, o=nwb, i=PB[0][:, 0:DV]: v.tensor_copy(o[:], i),
         r=[r_PB[0]], w=[r_nwb], small=True)

    gg_src = gg_d.rearrange("(tc p) o -> p tc o", p=128)
    onT_dst = onT_d.rearrange("(vc p) t -> p vc t", p=128)

    for tc in range(nchunks):
        tsl = slice(tc * 128, (tc + 1) * 128)
        k.dma("sync", onst[:], gg_src[:, tc, :], w=[r_onst])
        for (pre, rpre, dst, rdst, nch, cwoff) in (
                (qpre, r_qpre, qn, r_qn, NQC, 0),
                (kpre, r_kpre, kn, r_kn, NQC, NQC * 4),
                (vpre, r_vpre, vcv, r_vcv, NVC, 2 * NQC * 4)):
            for ch4 in range((nch + 3) // 4):
                pa, rpa = PA[ch4 % 2], r_PA[ch4 % 2]
                nsub = min(4, nch - ch4 * 4)
                for s4 in range(nsub):
                    ch = ch4 * 4 + s4
                    for tap in range(4):
                        k.op("tensor", lambda e, p=pa[:, s4 * 128:(s4 + 1) * 128],
                             w=dg[:, (cwoff + ch * 4 + tap) * 128:(cwoff + ch * 4 + tap + 1) * 128],
                             i=pre[:, ch * CP + PAD + tc * 128 - (3 - tap):ch * CP + PAD + tc * 128 - (3 - tap) + 128],
                             st=(tap == 0), sp=(tap == 3):
                             e.matmul(p, w, i, start=st, stop=sp, skip_group_check=True),
                             r=[r_dg, rpre[ch]], w=([rpa] if tap in (0, 3) else []))
                k.op("scalar", lambda e, o=dst[:, ch4 * 512:ch4 * 512 + nsub * 128],
                     i=pa[:, 0:nsub * 128]: e.activation(o, i[:], AF.Silu),
                     r=[rpa], w=[rdst, rpa])
        for (dst, rdst) in ((qn, r_qn), (kn, r_kn)):
            for h in range(NH):
                k.op("scalar", lambda e, o=das[:, h * 128:(h + 1) * 128],
                     i=dst[:, h * 128:(h + 1) * 128]: e.activation(o, i, AF.Square),
                     r=[rdst], w=[r_das])
            for h in range(NH):
                k.op("tensor", lambda e, p=PRow[:, 0:128], o=ocbf[:],
                     i=das[:, h * 128:(h + 1) * 128]:
                     e.matmul(p, o, i, start=True, stop=True, skip_group_check=True),
                     r=[r_das, rC], w=[r_PRow], small=True)
                k.op("vector", lambda v, o=rowsc[:, h * 128:(h + 1) * 128],
                     i=PRow[:, 0:128]: v.tensor_copy(o, i), r=[r_PRow], w=[r_rowsc],
                     small=True)
            sl0 = slice(0, NH * 128)
            k.op("vector", lambda v, o=rowsc[:, sl0]: v.tensor_scalar_add(o, o, 1e-6),
                 r=[r_rowsc], w=[r_rowsc], small=True)
            k.op("scalar", lambda e, o=rowsc[:, sl0], i=rowsc[:, sl0]: e.activation(o, i, AF.Ln),
                 r=[r_rowsc], w=[r_rowsc], small=True)
            k.op("scalar", lambda e, o=rowsc[:, sl0], i=rowsc[:, sl0]:
                 e.activation(o, i, AF.Exp, scale=-0.5), r=[r_rowsc], w=[r_rowsc], small=True)
            for h in range(NH):
                pb, rpb = PB[h % 2], r_PB[h % 2]
                k.op("tensor", lambda e, p=pb[:, 0:128], o=C.ones_row[:],
                     i=rowsc[:, h * 128:(h + 1) * 128]:
                     e.matmul(p, o, i, start=True, stop=True, skip_group_check=True),
                     r=[r_rowsc, C.reg], w=[rpb], small=True)
                k.op("vector", lambda v, o=dst[:, h * 128:(h + 1) * 128], i=pb[:, 0:128]:
                     v.tensor_mul(o, o, i), r=[rpb, rdst], w=[rdst])
        for h in range(NH):
            k.op("tensor", lambda e, p=PTr[:, 0:128], i=kn[:, h * 128:(h + 1) * 128], ii=i_bf[:]:
                 e.transpose(p, i, ii), r=[r_kn, rC], w=[r_PTr])
            k.op("vector", lambda v, o=ktok[:, h * 128:(h + 1) * 128], i=PTr[:, 0:128]:
                 v.tensor_copy(o, i), r=[r_PTr], w=[r_ktok])
        for h in range(NH):
            for d2 in range(2):
                k.op("tensor", lambda e, p=PTr[:, d2 * 128:(d2 + 1) * 128],
                     i=vcv[:, h * 256 + d2 * 128:h * 256 + (d2 + 1) * 128], ii=i_bf[:]:
                     e.transpose(p, i, ii), r=[r_vcv, rC], w=[r_PTr])
            k.op("vector", lambda v, o=vtok[:, h * 256:(h + 1) * 256], i=PTr[:, 0:256]:
                 v.tensor_copy(o, i), r=[r_PTr], w=[r_vtok])
        k.op("tensor", lambda e, p=PB[0][:, 0:NH], i=g_rows[:, tsl], ii=i_f32[0:NH, 0:NH]:
             e.transpose(p, i, ii), r=[r_grows, rC], w=[r_PB[0]], small=True)
        k.op("vector", lambda v, o=rstdc[:, NH:2 * NH], i=PB[0][:, 0:NH]:
             v.tensor_copy(o, i), r=[r_PB[0]], w=[r_rstdc], small=True)
        k.op("tensor", lambda e, p=PB[1][:, 0:NH], i=beta_rows[:, tsl], ii=i_f32[0:NH, 0:NH]:
             e.transpose(p, i, ii), r=[r_beta, rC], w=[r_PB[1]], small=True)
        k.op("vector", lambda v, o=bcols, i=PB[1][:, 0:NH]:
             v.tensor_copy(o[:], i), r=[r_PB[1]], w=[r_bcols], small=True)
        k.op("tensor", lambda e, p=PB[0][:, 0:NH], o=ltones[:], i=rstdc[:, NH:2 * NH]:
             e.matmul(p, o, i, start=True, stop=True, skip_group_check=True),
             r=[r_rstdc, rC], w=[r_PB[0]], small=True)
        k.op("vector", lambda v, o=gcols, i=PB[0][:, 0:NH]: v.tensor_copy(o[:], i),
             r=[r_PB[0]], w=[r_gcols], small=True)
        k.op("scalar", lambda e, o=egc[:, 0:NH], i=gcols[:]: e.activation(o, i[:], AF.Exp),
             r=[r_gcols], w=[r_egc], small=True)
        k.op("vector", lambda v, o=egc[:, 3 * NH:4 * NH], i=gcols[:]:
             v.tensor_scalar_mul(o, i[:], -1.0), r=[r_gcols], w=[r_egc], small=True)
        for h in range(NH):
            k.op("tensor", lambda e, p=PRow[:, 0:128], o=gcols[:, h:h + 1], i=i_f32[:]:
                 e.matmul(p, o, i, start=True, stop=True, skip_group_check=True),
                 r=[r_gcols, rC], w=[r_PRow], small=True)
            k.op("vector", lambda v, o=grows0[:, h * 128:(h + 1) * 128], i=PRow[:, 0:128]:
                 v.tensor_copy(o, i), r=[r_PRow], w=[r_grows0], small=True)
        k.op("vector", lambda v, o=grows0[:, NH * 128:2 * NH * 128], i=grows0[:, 0:NH * 128]:
             v.tensor_scalar_mul(o, i, -1.0), r=[r_grows0], w=[r_grows0], small=True)
        k.op("scalar", lambda e, o=bcrow[:],
             i=grows0[:, 0:NH * 128].rearrange("a (h t) -> a h t", h=NH)[:, :, 127]:
             e.activation(o[:], i, AF.Exp), r=[r_grows0], w=[r_bcrow], small=True)
        k.op("tensor", lambda e, p=PB[0][:, 0:NH], o=C.ones_row[:], i=bcrow[:]:
             e.matmul(p, o, i, start=True, stop=True, skip_group_check=True),
             r=[r_bcrow, C.reg], w=[r_PB[0]], small=True)
        k.op("vector", lambda v, o=egc[:, 2 * NH:3 * NH], i=PB[0][:, 0:NH]:
             v.tensor_copy(o, i), r=[r_PB[0]], w=[r_egc], small=True)
        k.op("tensor", lambda e, p=PB[1][:, 0:NH], o=C.ones_row[:],
             i=grows0[:, 0:NH * 128].rearrange("a (h t) -> a h t", h=NH)[:, :, 127]:
             e.matmul(p, o, i, start=True, stop=True, skip_group_check=True),
             r=[r_grows0, C.reg], w=[r_PB[1]], small=True)
        k.op("vector", lambda v, o=egc[:, NH:2 * NH], i=gcols[:], b2=PB[1][:, 0:NH]:
             v.tensor_sub(o, b2, i[:]), r=[r_PB[1], r_gcols], w=[r_egc], small=True)
        k.op("scalar", lambda e, o=egc[:, NH:2 * NH], i=egc[:, NH:2 * NH]:
             e.activation(o, i, AF.Exp), r=[r_egc], w=[r_egc], small=True)
        for (dst, rdst, mask, t_on_free) in ((dts, r_dts, mincl, True), (das, r_das, mstrict, False)):
            for h in range(NH):
                pa, rpa = PA[h % 2], r_PA[h % 2]
                sl4 = slice((h % 4) * 128, (h % 4 + 1) * 128) if False else slice(0, 128)
                if t_on_free:
                    k.op("tensor", lambda e, p=pa[:, 0:128],
                         o=grows0[:, NH * 128 + h * 128:NH * 128 + (h + 1) * 128],
                         i=C.ones_row[:].bitcast(F32) if False else None,
                         oo=C.ones_row[:]:
                         e.matmul(p, o, oo, start=True, stop=False, skip_group_check=True),
                         r=[r_grows0, C.reg], w=[rpa], small=True)
                    k.op("tensor", lambda e, p=pa[:, 0:128], o=C.ones_row[:],
                         i=grows0[:, h * 128:(h + 1) * 128]:
                         e.matmul(p, o, i, start=False, stop=False, skip_group_check=True),
                         r=[r_grows0, C.reg], w=[rpa], small=True)
                else:
                    k.op("tensor", lambda e, p=pa[:, 0:128],
                         o=grows0[:, h * 128:(h + 1) * 128], oo=C.ones_row[:]:
                         e.matmul(p, o, oo, start=True, stop=False, skip_group_check=True),
                         r=[r_grows0, C.reg], w=[rpa], small=True)
                    k.op("tensor", lambda e, p=pa[:, 0:128], o=C.ones_row[:],
                         i=grows0[:, NH * 128 + h * 128:NH * 128 + (h + 1) * 128]:
                         e.matmul(p, o, i, start=False, stop=False, skip_group_check=True),
                         r=[r_grows0, C.reg], w=[rpa], small=True)
                k.op("tensor", lambda e, p=pa[:, 0:128], o=i_f32[:], i=mask:
                     e.matmul(p, o, i[:], start=False, stop=True, skip_group_check=True),
                     r=[rC], w=[rpa], small=True)
                k.op("scalar", lambda e, o=dst[:, h * 128:(h + 1) * 128], i=pa[:, 0:128]:
                     e.activation(o, i, AF.Exp), r=[rpa], w=[rdst, rpa])
        for h in range(NH):
            pa, rpa = PA[0], r_PA[0]
            k.op("tensor", lambda e, p=pa[:, (h % 4) * 128:(h % 4 + 1) * 128] if False else pa[:, 0:128],
                 w=kn[:, h * 128:(h + 1) * 128], i=kn[:, h * 128:(h + 1) * 128]:
                 e.matmul(p, w, i, start=True, stop=True, skip_group_check=True),
                 r=[r_kn], w=[rpa])
            k.op("vector", lambda v, o=Asb[:, h * 128:(h + 1) * 128], i=pa[:, 0:128],
                 s1=bcols[:, h:h + 1], a=das[:, h * 128:(h + 1) * 128]:
                 v.scalar_tensor_tensor(o, i, s1, a, ALU.mult, ALU.mult),
                 r=[rpa, r_bcols, r_das], w=[r_A, rpa])
        k.op("vector", lambda v, o=Asb[:]: v.tensor_scalar_mul(o[:], o[:], -1.0),
             r=[r_A], w=[r_A])
        for h in range(NH):
            k.op("tensor", lambda e, p=PTr[:, 0:128], i=Asb[:, h * 128:(h + 1) * 128], ii=i_bf[:]:
                 e.transpose(p, i, ii), r=[r_A, rC], w=[r_PTr])
            k.op("vector", lambda v, o=ATsb[:, h * 128:(h + 1) * 128], i=PTr[:, 0:128]:
                 v.tensor_copy(o, i), r=[r_PTr], w=[r_AT])
        k.op("vector", lambda v, o=PT[:], i=ATsb[:],
             b2=i_bf[:].unsqueeze(1).broadcast_to([128, NH, 128]):
             v.tensor_tensor(o[:].rearrange("p (h t) -> p h t", h=NH), i[:].rearrange("p (h t) -> p h t", h=NH), b2, ALU.add),
             r=[r_AT, rC], w=[r_PT])
        k.op("vector", lambda v, o=Pow[:], i=Asb[:]: v.tensor_copy(o[:], i[:]),
             r=[r_A], w=[r_Pow])
        k.op("vector", lambda v, o=PowT[:], i=ATsb[:]: v.tensor_copy(o[:], i[:]),
             r=[r_AT], w=[r_PowT])
        for rnd in range(6):
            pa, rpa = PA[0], r_PA[0]
            pb2, rpb2 = PA[1], r_PA[1]
            for h in range(NH):
                k.op("tensor", lambda e, p=pa[:, 0:128] if False else pa[:, (h % 8) * 128:(h % 8) * 128 + 128],
                     w=PowT[:, h * 128:(h + 1) * 128], i=Pow[:, h * 128:(h + 1) * 128]:
                     e.matmul(p, w, i, start=True, stop=True, skip_group_check=True),
                     r=[r_Pow, r_PowT], w=[rpa]) if h < 8 else None
            for h in range(NH):
                k.op("tensor", lambda e, p=pb2[:, (h % 8) * 128:(h % 8) * 128 + 128],
                     w=Pow[:, h * 128:(h + 1) * 128], i=PowT[:, h * 128:(h + 1) * 128]:
                     e.matmul(p, w, i, start=True, stop=True, skip_group_check=True),
                     r=[r_Pow, r_PowT], w=[rpb2])
            k.op("vector", lambda v, o=Pow[:, 0:NH * 128], i=pa[:, 0:NH * 128]:
                 v.tensor_copy(o, i), r=[rpa], w=[r_Pow, rpa])
            k.op("vector", lambda v, o=PowT[:, 0:NH * 128], i=pb2[:, 0:NH * 128]:
                 v.tensor_copy(o, i), r=[rpb2], w=[r_PowT, rpb2])
            for h in range(NH):
                k.op("tensor", lambda e, p=pa[:, (h % 8) * 128:(h % 8) * 128 + 128],
                     o=i_bf[:], i=PT[:, h * 128:(h + 1) * 128]:
                     e.matmul(p, o, i, start=True, stop=False, skip_group_check=True),
                     r=[rC, r_PT], w=[rpa])
                k.op("tensor", lambda e, p=pa[:, (h % 8) * 128:(h % 8) * 128 + 128],
                     w=Pow[:, h * 128:(h + 1) * 128], i=PT[:, h * 128:(h + 1) * 128]:
                     e.matmul(p, w, i, start=False, stop=True, skip_group_check=True),
                     r=[r_Pow, r_PT], w=[rpa])
            k.op("vector", lambda v, o=PT[:, 0:NH * 128], i=pa[:, 0:NH * 128]:
                 v.tensor_copy(o, i), r=[rpa], w=[r_PT, rpa])
        k.op("vector", lambda v, o=rstdc[:, NH:2 * NH], i=egc[:, 0:NH], a=bcols[:]:
             v.tensor_mul(o, i, a[:]), r=[r_egc, r_bcols], w=[r_rstdc], small=True)
        for h in range(NH):
            k.op("vector", lambda v, o=Kbp[:, h * 128:(h + 1) * 128],
                 i=ktok[:, h * 128:(h + 1) * 128], s1=rstdc[:, NH + h:NH + h + 1]:
                 v.tensor_scalar_mul(o, i, s1), r=[r_ktok, r_rstdc], w=[r_Kbp])
            k.op("vector", lambda v, o=Kde[:, h * 128:(h + 1) * 128],
                 i=ktok[:, h * 128:(h + 1) * 128], s1=egc[:, NH + h:NH + h + 1]:
                 v.tensor_scalar_mul(o, i, s1), r=[r_ktok, r_egc], w=[r_Kde])
        for h in range(NH):
            pa, rpa = PA[0], r_PA[0]
            k.op("tensor", lambda e, p=pa[:, (h % 8) * 128:(h % 8) * 128 + 128],
                 w=Kbp[:, h * 128:(h + 1) * 128], i=PT[:, h * 128:(h + 1) * 128]:
                 e.matmul(p, w, i, start=True, stop=True, skip_group_check=True),
                 r=[r_Kbp, r_PT], w=[rpa])
        k.op("vector", lambda v, o=Wneg[:, 0:NH * 128], i=PA[0][:, 0:NH * 128]:
             v.tensor_scalar_mul(o, i, -1.0), r=[r_PA[0]], w=[r_Wneg, r_PA[0]])
        for h in range(NH):
            k.op("vector", lambda v, o=Vb[:, h * 256:(h + 1) * 256],
                 i=vtok[:, h * 256:(h + 1) * 256], s1=bcols[:, h:h + 1]:
                 v.tensor_scalar_mul(o, i, s1), r=[r_vtok, r_bcols], w=[r_Vb])
        for h in range(NH):
            pa, rpa = PA[h // 4], r_PA[h // 4]
            psl = pa[:, (h % 4) * 256:(h % 4) * 256 + 256]
            k.op("tensor", lambda e, p=psl, w=PT[:, h * 128:(h + 1) * 128],
                 i=Vb[:, h * 256:(h + 1) * 256], sp=(tc == 0):
                 e.matmul(p, w, i, start=True, stop=sp, skip_group_check=True),
                 r=[r_PT, r_Vb], w=[rpa])
            if tc > 0:
                k.op("tensor", lambda e, p=psl, w=Wneg[:, h * 128:(h + 1) * 128],
                     i=S_bf[:, h * 256:(h + 1) * 256]:
                     e.matmul(p, w, i, start=False, stop=True, skip_group_check=True),
                     r=[r_Wneg, r_Sbf], w=[rpa])
        for j in range(2):
            k.op("vector", lambda v, o=Usb[:, j * 1024:min((j + 1) * 1024, NH * 256)],
                 i=PA[j][:, 0:min(1024, NH * 256 - j * 1024)]:
                 v.tensor_copy(o, i), r=[r_PA[j]], w=[r_U, r_PA[j]])
        for h in range(NH):
            pa, rpa = PA[0], r_PA[0]
            k.op("tensor", lambda e, p=pa[:, (h % 8) * 128:(h % 8) * 128 + 128],
                 w=kn[:, h * 128:(h + 1) * 128], i=qn[:, h * 128:(h + 1) * 128]:
                 e.matmul(p, w, i, start=True, stop=True, skip_group_check=True),
                 r=[r_kn, r_qn], w=[rpa])
        k.op("vector", lambda v, o=Attn[:, 0:NH * 128], i=PA[0][:, 0:NH * 128], a=dts[:, 0:NH * 128]:
             v.tensor_tensor(o, i, a, ALU.mult), r=[r_PA[0], r_dts], w=[r_Attn, r_PA[0]])
        for h in range(NH):
            pa, rpa = PA[h // 4], r_PA[h // 4]
            psl = pa[:, (h % 4) * 256:(h % 4) * 256 + 256]
            k.op("tensor", lambda e, p=psl, w=Attn[:, h * 128:(h + 1) * 128],
                 i=Usb[:, h * 256:(h + 1) * 256]:
                 e.matmul(p, w, i, start=True, stop=True, skip_group_check=True),
                 r=[r_Attn, r_U], w=[rpa])
        for j in range(2):
            k.op("vector", lambda v, o=Osb[:, j * 1024:min((j + 1) * 1024, NH * 256)],
                 i=PA[j][:, 0:min(1024, NH * 256 - j * 1024)]:
                 v.tensor_copy(o, i), r=[r_PA[j]], w=[r_O, r_PA[j]])
        if tc > 0:
            for h in range(NH):
                pb2, rpb2 = PB[h % 2], r_PB[h % 2]
                k.op("tensor", lambda e, p=pb2[:, 0:256], w=qn[:, h * 128:(h + 1) * 128],
                     i=S_bf[:, h * 256:(h + 1) * 256]:
                     e.matmul(p, w, i, start=True, stop=True, skip_group_check=True),
                     r=[r_qn, r_Sbf], w=[rpb2])
                k.op("vector", lambda v, o=Osb[:, h * 256:(h + 1) * 256], i=pb2[:, 0:256],
                     s1=egc[:, h:h + 1], a=None:
                     v.scalar_tensor_tensor(o, i, s1, o, ALU.mult, ALU.add),
                     r=[rpb2, r_egc, r_O], w=[r_O, rpb2])
        for h in range(NH):
            pa, rpa = PA[h // 4], r_PA[h // 4]
            psl = pa[:, (h % 4) * 256:(h % 4) * 256 + 256]
            k.op("tensor", lambda e, p=psl, w=Kde[:, h * 128:(h + 1) * 128],
                 i=Usb[:, h * 256:(h + 1) * 256]:
                 e.matmul(p, w, i, start=True, stop=True, skip_group_check=True),
                 r=[r_Kde, r_U], w=[rpa])
        if tc == 0:
            for j in range(2):
                k.op("vector", lambda v, o=S_sb[:, j * 1024:min((j + 1) * 1024, NH * 256)],
                     i=PA[j][:, 0:min(1024, NH * 256 - j * 1024)]:
                     v.tensor_copy(o, i), r=[r_PA[j]], w=[r_S, r_PA[j]])
        else:
            for h in range(NH):
                pa, rpa = PA[h // 4], r_PA[h // 4]
                psl = pa[:, (h % 4) * 256:(h % 4) * 256 + 256]
                k.op("vector", lambda v, o=S_sb[:, h * 256:(h + 1) * 256], i2=psl,
                     s1=egc[:, 2 * NH + h:2 * NH + h + 1]:
                     v.scalar_tensor_tensor(o, o, s1, i2, ALU.mult, ALU.add),
                     r=[rpa, r_egc, r_S], w=[r_S, rpa])
        if tc < nchunks - 1:
            k.op("vector", lambda v, o=S_bf[:], i=S_sb[:]: v.tensor_copy(o[:], i[:]),
                 r=[r_S], w=[r_Sbf])
        for h in range(NH):
            k.op("scalar", lambda e, o=dts[:, h * 128:(h + 1) * 128],
                 i=Osb[:, h * 256:h * 256 + 128]: e.activation(o, i, AF.Square),
                 r=[r_O], w=[r_dts])
            k.op("scalar", lambda e, o=das[:, h * 128:(h + 1) * 128],
                 i=Osb[:, h * 256 + 128:(h + 1) * 256]: e.activation(o, i, AF.Square),
                 r=[r_O], w=[r_das])
        k.op("vector", lambda v, o=dts[:, 0:NH * 128], a=das[:, 0:NH * 128]:
             v.tensor_add(o, dts[:, 0:NH * 128], a), r=[r_dts, r_das], w=[r_dts])
        for h in range(NH):
            k.op("vector", lambda v, o=rstdc[:, h:h + 1], i=dts[:, h * 128:(h + 1) * 128]:
                 v.reduce_sum(o, i, axis=AX.X), r=[r_dts], w=[r_rstdc], small=True)
        k.op("vector", lambda v, o=rstdc[:, 0:NH]: v.tensor_scalar_mul(o, o, 1.0 / DV),
             r=[r_rstdc], w=[r_rstdc], small=True)
        k.op("vector", lambda v, o=rstdc[:, 0:NH]: v.tensor_scalar_add(o, o, 1e-5),
             r=[r_rstdc], w=[r_rstdc], small=True)
        k.op("scalar", lambda e, o=rstdc[:, 0:NH], i=rstdc[:, 0:NH]: e.activation(o, i, AF.Ln),
             r=[r_rstdc], w=[r_rstdc], small=True)
        k.op("scalar", lambda e, o=rstdc[:, 0:NH], i=rstdc[:, 0:NH]:
             e.activation(o, i, AF.Exp, scale=-0.5), r=[r_rstdc], w=[r_rstdc], small=True)
        k.op("scalar", lambda e, o=onst[:, 0:NH * 256], i=onst[:, 0:NH * 256]:
             e.activation(o, i, AF.Silu), r=[r_onst], w=[r_onst])
        k.op("vector", lambda v, o=onst[:].rearrange("p (h d2) -> p h d2", h=NH),
             i=onst[:].rearrange("p (h d2) -> p h d2", h=NH),
             b2=nwb[:].unsqueeze(1).broadcast_to([128, NH, DV]):
             v.tensor_tensor(o, i, b2, ALU.mult), r=[r_onst, r_nwb], w=[r_onst])
        for h in range(NH):
            k.op("vector", lambda v, o=onst[:, h * 256:(h + 1) * 256],
                 i=Osb[:, h * 256:(h + 1) * 256], s1=rstdc[:, h:h + 1],
                 a=onst[:, h * 256:(h + 1) * 256]:
                 v.scalar_tensor_tensor(o, i, s1, a, ALU.mult, ALU.mult),
                 r=[r_O, r_rstdc, r_onst], w=[r_onst])
        for h2 in range(NH * 2):
            pslot = h2 % 4
            k.op("tensor", lambda e, p=PTr[:, pslot * 128:(pslot + 1) * 128],
                 i=onst[:, h2 * 128:(h2 + 1) * 128], ii=i_bf[:]:
                 e.transpose(p, i, ii), r=[r_onst, rC], w=[r_PTr])
            if pslot == 3 or h2 == NH * 2 - 1:
                base = (h2 // 4) * 4
                n4 = pslot + 1
                k.op("vector", lambda v, o=Usb[:, base * 128:(base + n4) * 128],
                     i=PTr[:, 0:n4 * 128]: v.tensor_copy(o, i),
                     r=[r_PTr], w=[r_U, r_PTr])
        k.dma("sync", onT_dst[:, :, tsl],
              Usb[:].rearrange("p (vc t) -> p vc t", vc=NH * 2), r=[r_U], w=[r_onTd])
    csc.close()
    k.barrier()
    return _attn1_outproj(k, L)


def _attn1_outproj(k, L):
    g = lambda n: L[n]
    NH, T = g("NH"), g("T")
    wo_d, po_d, onT_d = g("wo_d"), g("po_d"), g("onT_d")
    osc = ExitStack()
    wo_sb = k.sb("wo1", [128, (NH * 2) * D], BF16, ctx=osc)
    r_wo = Reg("wo1")
    onb = [k.sb(f"onb1{i}", [128, (NH * 2) * 512], BF16, ctx=osc) for i in range(2)]
    r_onb = [Reg(f"onb{i}") for i in range(2)]
    postg = [k.sb(f"postg1{i}", [128, 512], F32, ctx=osc) for i in range(2)]
    r_postg = [Reg(f"postg{i}") for i in range(2)]
    r_pod = Reg("po_dram")
    ps_o = [k.ps(f"psO1{i}", [128, 512], F32, ctx=osc) for i in range(2)]
    r_pso = [Reg(f"psO{i}") for i in range(2)]
    wo_src = wo_d.rearrange("(vc p) o -> p vc o", p=128)
    for vc in range(NH * 2):
        k.dma("gpsimd", wo_sb[:, vc * D:(vc + 1) * D].rearrange("p (a o) -> p a o", a=1),
              wo_src[:, vc:vc + 1, :], w=[r_wo])
    onsrc = onT_d.rearrange("(vc p) t -> p vc t", p=128)
    podst = rearr_feat(po_d)
    for tt in range(4):
        bo = tt % 2
        k.dma("sync", onb[bo][:].rearrange("p (vc t) -> p vc t", vc=NH * 2),
              onsrc[:, :, tt * 512:(tt + 1) * 512], w=[r_onb[bo]])
        for oc in range(NFC):
            pp, rp = ps_o[oc % 2], r_pso[oc % 2]
            for vc in range(NH * 2):
                k.op("tensor", lambda e, p=pp, w=wo_sb[:, vc * D + oc * 128:vc * D + (oc + 1) * 128],
                     i=onb[bo][:, vc * 512:(vc + 1) * 512],
                     st=(vc == 0), sp=(vc == NH * 2 - 1):
                     e.matmul(p[:], w, i, start=st, stop=sp, skip_group_check=True),
                     r=[r_wo, r_onb[bo]], w=([rp] if vc in (0, NH * 2 - 1) else []))
            g2 = oc % 2
            k.op("vector", lambda v, o=postg[g2], i=pp: v.tensor_copy(o[:], i[:]),
                 r=[rp], w=[r_postg[g2]])
            k.dma("sync", podst[:, oc, tt * 512:(tt + 1) * 512], postg[g2][:],
                  r=[r_postg[g2]], w=[r_pod])
    k.op("sync", lambda e: e.nop(), r=[r_pod])
    osc.close()
    return k


def build_attn1_full(nheads=6, nchunks=16):
    k = build_attn1(nheads=nheads, nchunks=nchunks)
    return k


def run_attn1(xs, inputs):
    wq = np.asarray(inputs["L1_attn_Wq"], np.float32)
    wk = np.asarray(inputs["L1_attn_Wk"], np.float32)
    wv = np.asarray(inputs["L1_attn_Wv"], np.float32)
    wb = np.asarray(inputs["L1_attn_Wb"], np.float32)
    wa = np.asarray(inputs["L1_attn_Wa"], np.float32)
    wg = np.asarray(inputs["L1_attn_Wg"], np.float32)
    wo = np.asarray(inputs["L1_attn_Wo"], np.float32)
    cwq = np.asarray(inputs["L1_attn_conv_wq"], np.float32)
    cwk = np.asarray(inputs["L1_attn_conv_wk"], np.float32)
    cwv = np.asarray(inputs["L1_attn_conv_wv"], np.float32)
    alog = np.asarray(inputs["L1_attn_A_log"], np.float32)
    dtb = np.asarray(inputs["L1_attn_dt_bias"], np.float32)
    nw = np.asarray(inputs["L1_attn_norm_w"], np.float32)
    k = build_attn1_full()
    in_maps = []
    for c in range(8):
        s, hg = c // 2, c % 2
        xseq = np.ascontiguousarray(np.concatenate([xs[2 * s], xs[2 * s + 1]], axis=1))
        hq = slice(hg * 6 * 128, (hg + 1) * 6 * 128)
        hv = slice(hg * 6 * 256, (hg + 1) * 6 * 256)
        hh = slice(hg * 6, (hg + 1) * 6)
        in_maps.append(dict(
            xseq=xseq,
            wq=np.ascontiguousarray(wq[:, hq]),
            wk=np.ascontiguousarray(wk[:, hq]),
            wv=np.ascontiguousarray(wv[:, hv]),
            wgg=np.ascontiguousarray(wg[:, hv]),
            wbb=np.ascontiguousarray(wb[:, hh]),
            waa=np.ascontiguousarray(wa[:, hh]),
            wo=np.ascontiguousarray(wo[hv, :]),
            cwq=np.ascontiguousarray(np.transpose(cwq[hq].reshape(6, 128, 4), (1, 0, 2)).reshape(128, 24)),
            cwk=np.ascontiguousarray(np.transpose(cwk[hq].reshape(6, 128, 4), (1, 0, 2)).reshape(128, 24)),
            cwv=np.ascontiguousarray(np.transpose(cwv[hv].reshape(12, 128, 4), (1, 0, 2)).reshape(128, 48)),
            aldt=np.ascontiguousarray(np.stack([alog[hh], dtb[hh]], axis=1)),
            normw=np.ascontiguousarray(nw.reshape(1, 256)),
        ))
    res = k.run(in_maps)
    pos = [np.asarray(res.results[c]["po"]) for c in range(8)]
    pas, pbs = [], []
    for c in range(8):
        s, h = c // 2, c % 2
        tsl = slice(h * 1024, (h + 1) * 1024)
        pas.append(np.ascontiguousarray(pos[2 * s][:, tsl]))
        pbs.append(np.ascontiguousarray(pos[2 * s + 1][:, tsl]))
    return pas, pbs


# version 14
# speedup vs baseline: 1.0207x; 1.0207x over previous
import numpy as np
import ml_dtypes
from contextlib import ExitStack
import concourse.bass as bass
import concourse.mybir as mybir
from concourse.bass_utils import run_bass_kernel_spmd

F32 = mybir.dt.float32
BF16 = mybir.dt.bfloat16
I8 = mybir.dt.int8
AF = mybir.ActivationFunctionType
ALU = mybir.AluOpType
AX = mybir.AxisListType
BF16NP = ml_dtypes.bfloat16

D = 2048
NFC = 16
TOK = 1024
NEG = -1e30
ENGS = ("sync", "gpsimd", "vector", "tensor", "scalar")


class Reg:
    __slots__ = ("w", "rs", "name")

    def __init__(self, name=""):
        self.w = None
        self.rs = []
        self.name = name


class Rec:
    __slots__ = ("eng", "fn", "deps", "dma", "small", "need_inc", "semval", "wdeps", "semkey", "waitlist")

    def __init__(self, eng, fn, deps, dma, small):
        self.eng, self.fn, self.deps, self.dma, self.small = eng, fn, deps, dma, small
        self.need_inc = False
        self.semval = None


class K:
    def __init__(self):
        self.nc = bass.Bass("TRN2", target_bir_lowering=False, debug=False)
        self.recs = []
        self.ctx = ExitStack()
        self.prefix = ""

    scope = None

    def sb(self, name, shape, dt, ctx=None):
        return (ctx or self.scope or self.ctx).enter_context(
            self.nc.sbuf_tensor(self.prefix + name, shape, dt))

    def ps(self, name, shape, dt=F32, ctx=None):
        return (ctx or self.scope or self.ctx).enter_context(
            self.nc.psum_tensor(self.prefix + name, shape, dt))

    def din(self, name, shape, dt):
        return self.nc.dram_tensor(name, shape, dt, kind="ExternalInput").ap()

    def dout(self, name, shape, dt):
        return self.nc.dram_tensor(name, shape, dt, kind="ExternalOutput").ap()

    def dint(self, name, shape, dt):
        return self.nc.dram_tensor(name, shape, dt, kind="Internal").ap()

    def op(self, eng, fn, r=(), w=(), small=False, dma=False):
        deps = set()
        for reg in r:
            if reg.w is not None:
                deps.add(reg.w)
        for reg in w:
            if reg.w is not None:
                deps.add(reg.w)
            deps.update(reg.rs)
        rec = Rec(eng, fn, deps, dma, small)
        self.recs.append(rec)
        for reg in r:
            reg.rs.append(rec)
        for reg in w:
            reg.w = rec
            reg.rs = []
        return rec

    def dma(self, eng, out, in_, r=(), w=(), **kw):
        return self.op(eng, lambda e, o=out, i=in_, kw=kw:
                       e.dma_start(out=o, in_=i, **kw), r=r, w=w, dma=True)

    def barrier(self):
        lasts, dma_last, dcnt = {}, {}, {}
        for rec in self.recs:
            lasts[rec.eng] = rec
            if rec.dma:
                i = dcnt.get(rec.eng, 0)
                dma_last[(rec.eng, i % K.NDS)] = rec
                dcnt[rec.eng] = i + 1
        for eng in ENGS:
            deps = {r2 for e2, r2 in lasts.items() if e2 != eng}
            deps.update(dma_last.values())
            rec = Rec(eng, lambda e: e.nop(), deps, False, True)
            self.recs.append(rec)

    NDS = 16

    def build(self):
        nc = self.nc
        dcnt = {}
        for rec in self.recs:
            if rec.dma:
                i = dcnt.get(rec.eng, 0)
                rec.semval = (rec.eng, i % K.NDS)
                dcnt[rec.eng] = i + 1
        for rec in self.recs:
            rec.wdeps = []
            for dep in rec.deps:
                dep.need_inc = True
                rec.wdeps.append(dep)
        sems, cnt = {}, {}
        for eng in ENGS:
            sems[(eng, "c")] = self.ctx.enter_context(nc.semaphore(f"s_{eng}_c"))
            cnt[(eng, "c")] = 0
            for i in range(K.NDS):
                sems[(eng, "d", i)] = self.ctx.enter_context(
                    nc.semaphore(f"s_{eng}_d{i}"))
                cnt[(eng, "d", i)] = 0
        for rec in self.recs:
            rec.waitlist = []
            for dep in rec.wdeps:
                if dep.dma:
                    key3 = (dep.eng, "d", dep.semkey)
                    rec.waitlist.append((key3, None, dep))
                else:
                    rec.waitlist.append(((dep.eng, "c"), None, dep))
            rec.waitlist = [(key3, (cnt[key3] if key3[1] == "d" else dep.semval), dep)
                            for (key3, _, dep) in rec.waitlist]
            if rec.dma:
                key3 = (rec.eng, "d", rec.semval[1])
                if cnt[key3] > 0:
                    rec.waitlist.append((key3, cnt[key3], None))
                cnt[key3] += 16
                rec.semkey = rec.semval[1]
                rec.semval = cnt[key3]
            elif rec.need_inc:
                cnt[(rec.eng, "c")] += 1
                rec.semval = cnt[(rec.eng, "c")]

        def emit_for(e, recs):
            hwm = {}
            for rec in recs:
                for key3, val, dep in sorted(rec.waitlist, key=lambda t: (str(t[0]), t[1])):
                    if hwm.get(key3, 0) >= val:
                        continue
                    hwm[key3] = val
                    e.wait_ge(sems[key3], val)
                ins = rec.fn(e)
                if rec.dma:
                    ins.then_inc(sems[(rec.eng, "d", rec.semkey)], 16)
                elif rec.need_inc:
                    ins.then_inc(sems[(rec.eng, "c")], 1)

        perengine = {e: [r2 for r2 in self.recs if r2.eng == e] for e in ENGS}
        with nc.Block() as block:
            for eng in ENGS:
                if perengine[eng]:
                    getattr(block, eng)(lambda e, recs=perengine[eng]: emit_for(e, recs))
        return nc

    def run(self, in_maps, core_ids=None, **kw):
        with self.ctx:
            nc = self.build()
        return run_bass_kernel_spmd(nc, in_maps,
                                    core_ids=core_ids or list(range(8)), **kw)


def fsl(t, c, ntok, lo=None, hi=None):
    if lo is None:
        return t[:, c * ntok:(c + 1) * ntok]
    return t[:, c * ntok + lo:c * ntok + hi]


def rearr_feat(dram_ap, p=128):
    return dram_ap.rearrange("(c p) t -> p c t", p=p)


class Consts:
    def __init__(self, k):
        self.ones_col = k.sb("ones_col", [128, 1], F32)
        self.ones_row = k.sb("ones_row", [1, 128], F32)
        self.reg = Reg("consts")
        k.op("gpsimd", lambda g, a=self.ones_col: g.memset(a[:], 1.0),
             w=[self.reg], small=True)
        k.op("gpsimd", lambda g, a=self.ones_row: g.memset(a[:], 1.0),
             w=[self.reg], small=True)


def emit_layernorm(k, sc, C, x_sb, x_regs, ln_sb, ln_regs, ntok, tag):
    nspl = (ntok + 511) // 512
    sq = [k.sb(f"lnsq{i}_{tag}", [128, ntok], F32, ctx=sc) for i in range(2)]
    sqr = [Reg(f"sq{i}") for i in range(2)]
    tmp = [k.sb(f"lntmp{i}_{tag}", [128, ntok], F32, ctx=sc) for i in range(1)]
    tmpr = [Reg(f"lntmp{i}") for i in range(1)]
    srow = k.sb(f"lnrow_{tag}", [1, 4 * ntok], F32, ctx=sc)
    ps_sum = k.ps(f"lnpsum_{tag}", [1, ntok], F32, ctx=sc)
    ps_sq = k.ps(f"lnpssq_{tag}", [1, ntok], F32, ctx=sc)
    ps_bc = k.ps(f"lnpsbc_{tag}", [128, 2 * ntok], F32, ctx=sc)
    mean_r, ex2_r = srow[:, 0:ntok], srow[:, ntok:2 * ntok]
    var_r, rstd_r = srow[:, 2 * ntok:3 * ntok], srow[:, 3 * ntok:4 * ntok]
    r_mean, r_ex2, r_var, r_rstd = Reg("mean"), Reg("ex2"), Reg("var"), Reg("rstd")
    r_pssum, r_pssq, r_psbc = Reg(), Reg(), Reg()

    for c in range(NFC):
        b, br = sq[c % 2], sqr[c % 2]
        k.op("scalar", lambda e, o=b, i=fsl(x_sb, c, ntok):
             e.activation(o[:], i, AF.Square), r=[x_regs[c]], w=[br])
        for j in range(nspl):
            sl = slice(j * 512, min((j + 1) * 512, ntok))
            k.op("tensor", lambda e, p=ps_sum[:, sl], o=C.ones_col[:],
                 i=fsl(x_sb, c, ntok)[:, sl], st=(c == 0), sp=(c == NFC - 1):
                 e.matmul(p, o, i, start=st, stop=sp),
                 r=[x_regs[c], C.reg],
                 w=([r_pssum] if (c == NFC - 1 and j == nspl - 1) or c == 0 else []),
                 small=True)
            k.op("tensor", lambda e, p=ps_sq[:, sl], o=C.ones_col[:], i=b[:, sl],
                 st=(c == 0), sp=(c == NFC - 1):
                 e.matmul(p, o, i, start=st, stop=sp),
                 r=[br, C.reg],
                 w=([r_pssq] if (c == NFC - 1 and j == nspl - 1) or c == 0 else []),
                 small=True)
    k.op("vector", lambda v, o=mean_r, i=ps_sum: v.tensor_scalar_mul(o, i[:], 1.0 / D),
         r=[r_pssum], w=[r_mean], small=True)
    k.op("vector", lambda v, o=ex2_r, i=ps_sq: v.tensor_scalar_mul(o, i[:], 1.0 / D),
         r=[r_pssq], w=[r_ex2], small=True)
    k.op("vector", lambda v, o=var_r, a=mean_r: v.tensor_mul(o, a, a),
         r=[r_mean], w=[r_var], small=True)
    k.op("vector", lambda v, o=var_r, a=ex2_r: v.tensor_sub(o, a, o),
         r=[r_ex2, r_var], w=[r_var], small=True)
    k.op("vector", lambda v, o=var_r: v.tensor_scalar_add(o, o, 1e-5),
         r=[r_var], w=[r_var], small=True)
    k.op("scalar", lambda e, o=rstd_r, i=var_r: e.activation(o, i, AF.Ln),
         r=[r_var], w=[r_rstd], small=True)
    k.op("scalar", lambda e, o=rstd_r, i=rstd_r: e.activation(o, i, AF.Exp, scale=-0.5),
         r=[r_rstd], w=[r_rstd], small=True)
    k.op("vector", lambda v, o=mean_r, a=rstd_r: v.tensor_mul(o, o, a),
         r=[r_rstd, r_mean], w=[r_mean], small=True)
    for j in range(nspl):
        sl = slice(j * 512, min((j + 1) * 512, ntok))
        k.op("tensor", lambda e, p=ps_bc[:, 0:ntok][:, sl], o=C.ones_row[:], i=rstd_r[:, sl]:
             e.matmul(p, o, i, start=True, stop=True),
             r=[r_rstd, C.reg], w=[r_psbc], small=True)
        k.op("tensor", lambda e, p=ps_bc[:, ntok:][:, sl], o=C.ones_row[:], i=mean_r[:, sl]:
             e.matmul(p, o, i, start=True, stop=True),
             r=[r_mean, C.reg], w=[r_psbc], small=True)
    for c in range(NFC):
        t, tr = tmp[0], tmpr[0]
        k.op("vector", lambda v, o=t, i=fsl(x_sb, c, ntok), b2=ps_bc[:, 0:ntok]:
             v.tensor_mul(o[:], i, b2[:]), r=[x_regs[c], r_psbc], w=[tr])
        k.op("vector", lambda v, o=fsl(ln_sb, c, ntok), i=t, b2=ps_bc[:, ntok:]:
             v.tensor_tensor(o, i[:], b2[:], ALU.subtract),
             r=[tr, r_psbc], w=[ln_regs[c]])



def emit_xload(k, xsc, x_d, x2_d, io, with_partials, legacy_partials, r_x2d):
    x_sb = k.sb("x", [128, NFC * TOK], F32, ctx=xsc)
    r_x = [Reg(f"x{c}") for c in range(NFC)]
    xsrc = rearr_feat(x_d)
    if with_partials:
        x2dst = rearr_feat(x2_d)
        if legacy_partials:
            pstg = [k.sb(f"pstg{i}", [128, TOK], F32, ctx=xsc) for i in range(2)]
            r_pstg = [Reg(f"pstg{i}") for i in range(2)]
            pasrc, pbsrc = rearr_feat(io["pa"]), rearr_feat(io["pb"])
            for c in range(NFC):
                k.dma("sync", x_sb[:, c * TOK:(c + 1) * TOK], xsrc[:, c, :], w=[r_x[c]])
                k.dma("sync", pstg[0][:], pasrc[:, c, :], w=[r_pstg[0]])
                k.dma("sync", pstg[1][:], pbsrc[:, c, :], w=[r_pstg[1]])
                k.op("vector", lambda v, o=fsl(x_sb, c, TOK), a=pstg[0]:
                     v.tensor_add(o, o, a[:]), r=[r_pstg[0], r_x[c]], w=[r_x[c]])
                k.op("vector", lambda v, o=fsl(x_sb, c, TOK), b2=pstg[1]:
                     v.tensor_add(o, o, b2[:]), r=[r_pstg[1], r_x[c]], w=[r_x[c]])
                k.dma("sync", x2dst[:, c, :], x_sb[:, c * TOK:(c + 1) * TOK],
                      r=[r_x[c]], w=[r_x2d])
        else:
            G_d = io["Gpo"]
            msel = k.sb("msel", [128, 2], F32, ctx=xsc)
            r_msel = Reg("msel")
            k.dma("sync", msel[:], io["msel"], w=[r_msel])
            g0 = [k.sb(f"g0_{i}", [128, 2 * TOK], BF16, ctx=xsc) for i in range(2)]
            g1 = [k.sb(f"g1_{i}", [128, 2 * TOK], BF16, ctx=xsc) for i in range(2)]
            gs = [k.sb("gs_0", [128, 2 * TOK], F32, ctx=xsc)] * 2
            r_g0 = [Reg() for _ in range(2)]
            r_g1 = [Reg() for _ in range(2)]
            r_gs = [Reg()] * 2
            gsrc = G_d.rearrange("r (c p) t -> r p c t", p=128)
            for c in range(NFC):
                b = c % 2
                k.dma("sync", x_sb[:, c * TOK:(c + 1) * TOK], xsrc[:, c, :], w=[r_x[c]])
                k.dma("sync", g0[b][:], gsrc[0][:, c, :], w=[r_g0[b]])
                k.dma("sync", g1[b][:], gsrc[1][:, c, :], w=[r_g1[b]])
                k.op("vector", lambda v, o=gs[b], a=g0[b], b2=g1[b]:
                     v.tensor_add(o[:], a[:], b2[:]), r=[r_g0[b], r_g1[b]], w=[r_gs[b]])
                k.op("vector", lambda v, o=fsl(x_sb, c, TOK), i=gs[b], s1=msel[:, 0:1]:
                     v.scalar_tensor_tensor(o, i[:, 0:TOK], s1, o, ALU.mult, ALU.add),
                     r=[r_gs[b], r_msel, r_x[c]], w=[r_x[c]])
                k.op("vector", lambda v, o=fsl(x_sb, c, TOK), i=gs[b], s1=msel[:, 1:2]:
                     v.scalar_tensor_tensor(o, i[:, TOK:2 * TOK], s1, o, ALU.mult, ALU.add),
                     r=[r_gs[b], r_msel, r_x[c]], w=[r_x[c]])
                k.dma("scalar", x2dst[:, c, :], x_sb[:, c * TOK:(c + 1) * TOK],
                      r=[r_x[c]], w=[r_x2d])
    else:
        for c in range(NFC):
            k.dma("sync", x_sb[:, c * TOK:(c + 1) * TOK], xsrc[:, c, :], w=[r_x[c]])
    return x_sb, r_x


def build_ffn0(with_partials):
    k = K()
    io = dict(
        x_in=k.din("x_t", [D, TOK], F32),
        wg=k.din("wg", [D, 8192], I8), wu=k.din("wu", [D, 8192], I8),
        wdn=k.din("wdn", [8192, D], I8),
        sg2=k.din("sg2", [128, 64], F32), su2=k.din("su2", [128, 64], F32),
        sd2=k.din("sd2", [128, 16], F32),
        x_out=k.dout("xo", [D, TOK], F32),
    )
    if with_partials:
        io["pa"] = k.din("pa", [D, TOK], F32)
        io["pb"] = k.din("pb", [D, TOK], F32)
        io["x2"] = k.dint("x2", [D, TOK], F32)
    emit_ffn0(k, Consts(k), io, "F0_", with_partials, legacy_partials=with_partials)
    return k


def emit_ffn0(k, C, io, tag, with_partials, legacy_partials=False):
    pfx0 = k.prefix
    k.prefix = tag
    x_d = io["x_in"]
    wg_d, wu_d, wd_d = io["wg"], io["wu"], io["wdn"]
    sg_d, su_d, sd_d = io["sg2"], io["su2"], io["sd2"]
    xo_d = io["x_out"]
    x2_d = io.get("x2", x_d)
    scales = k.sb("scales", [128, 144], F32)
    r_sc = Reg("scales")
    ln = k.sb("ln", [128, NFC * TOK], BF16)
    r_ln = [Reg(f"ln{c}") for c in range(NFC)]
    gate = [k.sb(f"gate{i}", [128, 512], BF16) for i in range(4)]
    r_gate = [Reg(f"g{i}") for i in range(4)]
    xres = [k.sb(f"xres{i}", [128, TOK], F32) for i in range(2)]
    r_xres = [Reg(f"xres{i}") for i in range(2)]
    xout = [k.sb(f"xout{i}", [128, 512], F32) for i in range(4)]
    r_xout = [Reg(f"xo{i}") for i in range(4)]
    r_xod = Reg("xo_dram")
    r_x2d = Reg("x2_dram")

    WBLK = 128
    NWB = 8192 // WBLK
    wgb = [k.sb(f"wgb{i}", [128, NFC * WBLK], BF16) for i in range(2)]
    wub = [k.sb(f"wub{i}", [128, NFC * WBLK], BF16) for i in range(2)]
    r_wgb = [Reg(f"wgb{i}") for i in range(2)]
    r_wub = [Reg(f"wub{i}") for i in range(2)]
    OBLK = 128
    NOB = D // OBLK
    wdb = [k.sb(f"wdb{i}", [128, 32 * OBLK], BF16) for i in range(2)]
    r_wdb = [Reg(f"wdb{i}") for i in range(2)]

    wg_src = wg_d.rearrange("(c p) i -> p c i", p=128)
    wu_src = wu_d.rearrange("(c p) i -> p c i", p=128)
    wd_src = wd_d.rearrange("(ic p) o -> p ic o", p=128)

    def wgu_dma(b):
        buf = b % 2
        isl = slice(b * WBLK, (b + 1) * WBLK)
        k.dma("gpsimd", wgb[buf][:].rearrange("p (c i) -> p c i", c=NFC),
              wg_src[:, :, isl], w=[r_wgb[buf]])
        k.dma("gpsimd", wub[buf][:].rearrange("p (c i) -> p c i", c=NFC),
              wu_src[:, :, isl], w=[r_wub[buf]])

    def wd_dma(wave, ob):
        buf = ob % 2
        osl = slice(ob * OBLK, (ob + 1) * OBLK)
        icsl = slice(wave * 32, wave * 32 + 32)
        k.dma("gpsimd", wdb[buf][:].rearrange("p (ic o) -> p ic o", ic=32),
              wd_src[:, icsl, osl], w=[r_wdb[buf]])

    xsc = ExitStack()
    x_sb, r_x = emit_xload(k, xsc, x_d, x2_d, io, with_partials, legacy_partials, r_x2d)
    k.dma("sync", scales[:, 0:64], sg_d, w=[r_sc])
    k.dma("sync", scales[:, 64:128], su_d, w=[r_sc])
    k.dma("sync", scales[:, 128:144], sd_d, w=[r_sc])
    k.op("vector", lambda v, o=scales[:, 128:144]: v.tensor_scalar_mul(o, o, 0.5),
         r=[r_sc], w=[r_sc], small=True)

    lnsc = ExitStack()
    emit_layernorm(k, lnsc, C, x_sb, r_x, ln, r_ln, TOK, "f")
    lnsc.close()
    xsc.close()
    k.barrier()

    ps_g = [k.ps(f"psg{i}", [128, 512], F32) for i in range(2)]
    ps_u = [k.ps(f"psu{i}", [128, 512], F32) for i in range(2)]
    ps_d = [k.ps(f"psd{i}", [128, 512], F32) for i in range(2)]
    r_psg = [Reg(f"psg{i}") for i in range(2)]
    r_psu = [Reg(f"psu{i}") for i in range(2)]
    r_psd = [Reg(f"psd{i}") for i in range(2)]


    h = k.sb("h", [128, 32 * TOK], BF16)
    r_h = [Reg(f"h{i}") for i in range(64)]
    acc = k.sb("acc", [128, NFC * TOK], BF16)
    r_acc = [Reg(f"acc{i}") for i in range(32)]

    wgu_dma(0)
    wgu_dma(1)
    GUPB = WBLK // 128
    for wave in range(2):
        for ic in range(32):
            icg = wave * 32 + ic
            b, wo = divmod(icg, GUPB)
            wo *= 128
            buf = b % 2
            if icg % GUPB == 0 and b >= 1 and b + 1 < NWB:
                wgu_dma(b + 1)
            for tt in range(2):
                pg, rpg = ps_g[tt], r_psg[tt]
                for c in range(NFC):
                    k.op("tensor", lambda e, p=pg,
                         wt=wgb[buf][:, c * WBLK + wo:c * WBLK + wo + 128],
                         i=fsl(ln, c, TOK, tt * 512, (tt + 1) * 512),
                         st=(c == 0), sp=(c == NFC - 1):
                         e.matmul(p[:], wt, i, start=st, stop=sp),
                         r=[r_wgb[buf], r_ln[c]],
                         w=([rpg] if c in (0, NFC - 1) else []))
                g = (2 * icg + tt) % 4
                k.op("scalar", lambda e, o=gate[g], i=pg, s1=scales[:, icg:icg + 1]:
                     e.activation(o[:], i[:], AF.Silu, scale=s1),
                     r=[rpg, r_sc], w=[r_gate[g]])
                pu, rpu = ps_u[tt], r_psu[tt]
                for c in range(NFC):
                    k.op("tensor", lambda e, p=pu,
                         wt=wub[buf][:, c * WBLK + wo:c * WBLK + wo + 128],
                         i=fsl(ln, c, TOK, tt * 512, (tt + 1) * 512),
                         st=(c == 0), sp=(c == NFC - 1):
                         e.matmul(p[:], wt, i, start=st, stop=sp),
                         r=[r_wub[buf], r_ln[c]],
                         w=([rpu] if c in (0, NFC - 1) else []))
                k.op("vector", lambda v, o=fsl(h, ic, TOK, tt * 512, (tt + 1) * 512),
                     i=pu, s1=scales[:, 64 + icg:65 + icg], gg=gate[g]:
                     v.scalar_tensor_tensor(o, i[:], s1, gg[:], ALU.mult, ALU.mult),
                     r=[rpu, r_sc, r_gate[g]], w=[r_h[2 * ic + tt]])
        if wave == 0:
            wd_dma(0, 0)
            wd_dma(0, 1)
        for ob in range(NOB):
            buf = ob % 2
            g = wave * NOB + ob
            if g >= 1 and g + 1 < 2 * NOB:
                wd_dma((g + 1) // NOB, (g + 1) % NOB)
            for oc2 in range(OBLK // 128):
                oc = ob * (OBLK // 128) + oc2
                for tt in range(2):
                    pd, rpd = ps_d[tt], r_psd[tt]
                    for ic in range(32):
                        k.op("tensor", lambda e, p=pd,
                             wt=wdb[buf][:, ic * OBLK + oc2 * 128:ic * OBLK + oc2 * 128 + 128],
                             i=fsl(h, ic, TOK, tt * 512, (tt + 1) * 512),
                             st=(ic == 0), sp=(ic == 31):
                             e.matmul(p[:], wt, i, start=st, stop=sp),
                             r=[r_wdb[buf], r_h[2 * ic + tt]],
                             w=([rpd] if ic in (0, 31) else []))
                    asl = fsl(acc, oc, TOK, tt * 512, (tt + 1) * 512)
                    if wave == 0:
                        k.op("vector", lambda v, o=asl, i=pd, s1=scales[:, 128 + oc:129 + oc]:
                             v.tensor_scalar_mul(o, i[:], s1),
                             r=[rpd, r_sc], w=[r_acc[2 * oc + tt]])
                    else:
                        if tt == 0:
                            xr = xres[oc % 2]
                            k.dma("sync", xr[:], rearr_feat(x2_d)[:, oc, :],
                                  r=[r_x2d], w=[r_xres[oc % 2]])
                        j = (2 * oc + tt) % 4
                        k.op("vector", lambda v, o=xout[j], i=pd,
                             s1=scales[:, 128 + oc:129 + oc], a=asl:
                             v.scalar_tensor_tensor(o[:], i[:], s1, a, ALU.mult, ALU.add),
                             r=[rpd, r_sc, r_acc[2 * oc + tt]], w=[r_xout[j]])
                        k.op("vector", lambda v, o=xout[j],
                             a=xres[oc % 2][:, tt * 512:(tt + 1) * 512]:
                             v.tensor_add(o[:], o[:], a),
                             r=[r_xres[oc % 2], r_xout[j]], w=[r_xout[j]])
                        k.dma("scalar", rearr_feat(xo_d)[:, oc, tt * 512:(tt + 1) * 512],
                              xout[j][:], r=[r_xout[j]], w=[r_xod])
    k.op("sync", lambda e: e.nop(), r=[r_xod])
    k.prefix = pfx0
    return k



def _shard_x(x_full):
    xt = x_full.reshape(8192, D)
    return [np.ascontiguousarray(xt[c * TOK:(c + 1) * TOK].T) for c in range(8)]


def _unshard_x(parts):
    out = np.empty((8192, D), np.float32)
    for c, p in enumerate(parts):
        out[c * TOK:(c + 1) * TOK] = np.asarray(p).T
    return out.reshape(4, 2048, D)


def _scale2d(s, nch):
    return np.ascontiguousarray(np.asarray(s, np.float32).reshape(nch, 128).T)


def run_ffn0(xs, inputs, partials=None):
    k = build_ffn0(partials is not None)
    shared = dict(
        wg=np.asarray(inputs["L0_ffn_w_gate"]),
        wu=np.asarray(inputs["L0_ffn_w_up"]),
        wdn=np.asarray(inputs["L0_ffn_w_down"]),
        sg2=_scale2d(inputs["L0_ffn_s_gate"], 64),
        su2=_scale2d(inputs["L0_ffn_s_up"], 64),
        sd2=_scale2d(inputs["L0_ffn_s_down"], 16),
    )
    in_maps = []
    for c in range(8):
        m = dict(shared)
        m["x_t"] = xs[c]
        if partials is not None:
            m["pa"], m["pb"] = partials[0][c], partials[1][c]
        in_maps.append(m)
    res = k.run(in_maps)
    return [res.results[c]["xo"] for c in range(8)]






PAIRS = [[0, 1], [2, 3], [4, 5], [6, 7]]


def build_attn0(stages=99, nheads=8, sub=99):
    k = K()
    io = dict(
        xseq=k.din("xseq", [D, 2048], F32),
        wqt=k.din("wqt", [D, 1024], F32), wkt=k.din("wkt", [D, 1024], F32),
        wvt=k.din("wvt", [D, 1024], F32), wot=k.din("wot", [1024, D], F32),
        gq=k.din("gq", [128, 1], F32), gk=k.din("gk", [128, 1], F32),
        po=k.dout("po", [D, 2048], F32),
    )
    if stages == 1:
        io["dbgln"] = k.dout("dbgln", [128, NFC * 2048], BF16)
    if stages == 2:
        io["dbgq"] = k.dout("dbgq", [128, 2048], BF16)
        io["dbgk"] = k.dout("dbgk", [128, 2048], BF16)
    emit_attn0(k, io, "A0_", fused=False, stages=stages, nheads=nheads, sub=sub)
    return k


def emit_attn0(k, io, tag, fused, stages=99, nheads=8, sub=99):
    pfx0 = k.prefix
    k.prefix = tag
    T = 2048
    NH = 8
    wqt_d, wkt_d, wvt_d, wot_d = io["wqt"], io["wkt"], io["wvt"], io["wot"]
    gq_d, gk_d = io["gq"], io["gk"]
    po_d = io["po"]
    if not fused:
        x_d = io["xseq"]

    C = Consts(k)
    ocbf = k.sb("ocbf", [128, 1], BF16)
    orbf = k.sb("orbf", [1, 128], BF16)
    i_bf = k.sb("i_bf", [128, 128], BF16)
    i_f32 = k.sb("i_f32", [128, 128], F32)
    dmask = k.sb("dmask", [128, 128], BF16)
    tmask = k.sb("tmask", [128, 4 * 512], BF16)
    gqk = k.sb("gqk", [128, 2], F32)
    gqk2f = k.sb("gqk2f", [128, 2], F32)
    gqk2 = k.sb("gqk2", [128, 2], BF16)
    rC = Reg("aconst")

    def gop(fn):
        k.op("gpsimd", fn, w=[rC], small=True)
    gop(lambda g: g.memset(ocbf[:], 1.0))
    gop(lambda g: g.memset(orbf[:], 1.0))
    gop(lambda g: g.memset(i_bf[:], 0.0))
    gop(lambda g: g.affine_select(out=i_bf[:], in_=i_bf[:], compare_op=ALU.not_equal,
                                  fill=1.0, base=0, pattern=[[-1, 128]], channel_multiplier=1))
    gop(lambda g: g.memset(i_f32[:], 0.0))
    gop(lambda g: g.affine_select(out=i_f32[:], in_=i_f32[:], compare_op=ALU.not_equal,
                                  fill=1.0, base=0, pattern=[[-1, 128]], channel_multiplier=1))
    gop(lambda g: g.memset(dmask[:], 0.0))
    gop(lambda g: g.affine_select(out=dmask[:], in_=dmask[:], compare_op=ALU.is_ge,
                                  fill=NEG, base=0, pattern=[[-1, 128]], channel_multiplier=1))
    for j in range(4):
        gop(lambda g, j=j: g.memset(tmask[:, j * 512:(j + 1) * 512], 0.0))
        if j > 0:
            gop(lambda g, j=j: g.affine_select(
                out=tmask[:, j * 512:(j + 1) * 512], in_=tmask[:, j * 512:(j + 1) * 512],
                compare_op=ALU.is_ge, fill=NEG, base=-j * 128,
                pattern=[[1, 512]], channel_multiplier=0))
        gop(lambda g, j=j: g.affine_select(
            out=tmask[:, j * 512 + j * 128:j * 512 + (j + 1) * 128],
            in_=tmask[:, j * 512 + j * 128:j * 512 + (j + 1) * 128],
            compare_op=ALU.is_ge, fill=NEG, base=0,
            pattern=[[1, 128]], channel_multiplier=-1))
    k.dma("sync", gqk[:, 0:1], gq_d, w=[rC])
    k.dma("sync", gqk[:, 1:2], gk_d, w=[rC])
    k.op("vector", lambda v: v.tensor_mul(gqk2f[:], gqk[:], gqk[:]), r=[rC], w=[rC], small=True)
    k.op("vector", lambda v: v.reciprocal(gqk2f[:], gqk2f[:]), r=[rC], w=[rC], small=True)
    k.op("vector", lambda v: v.tensor_copy(gqk2[:], gqk2f[:]), r=[rC], w=[rC], small=True)

    ob = k.sb("ob", [128, NH * T], BF16)
    r_ob = [Reg(f"ob{i}") for i in range(NH * 4)]

    main = ExitStack()
    ln = k.sb("lnA", [128, NFC * T], BF16, ctx=main)
    r_ln = [Reg(f"ln{i}") for i in range(NFC * 2)]
    wbsc = ExitStack()
    wbh = [k.sb(f"wbh{i}", [128, NFC * 128], BF16, ctx=wbsc) for i in range(4)]
    r_wbh = [Reg(f"wbh{i}") for i in range(4)]
    wv_sb = k.sb("wv_sb", [128, NFC * 512], BF16, ctx=wbsc)
    r_wv = Reg("wv")
    wq_src = wqt_d.rearrange("(c p) o -> p c o", p=128)
    wk_src = wkt_d.rearrange("(c p) o -> p c o", p=128)
    wv_src0 = wvt_d.rearrange("(c p) o -> p c o", p=128)

    def emitW(h, rdeps=()):
        hp = h % 2
        bq, bk = 2 * hp, 2 * hp + 1
        k.dma("gpsimd", wbh[bq][:].rearrange("p (c o) -> p c o", c=NFC),
              wq_src[:, :, h * 128:(h + 1) * 128], r=list(rdeps), w=[r_wbh[bq]])
        k.dma("gpsimd", wbh[bk][:].rearrange("p (c o) -> p c o", c=NFC),
              wk_src[:, :, h * 128:(h + 1) * 128], r=list(rdeps), w=[r_wbh[bk]])

    def emit_wv(ns, rdeps=()):
        k.dma("gpsimd", wv_sb[:].rearrange("p (c o) -> p c o", o=512),
              wv_src0[:, :, ns * 512:(ns + 1) * 512], r=list(rdeps), w=[r_wv])

    lnsc = ExitStack()
    stg = [k.sb(f"stgA{i}", [128, TOK], BF16, ctx=lnsc) for i in range(2)]
    r_stg = [Reg(f"stg{i}") for i in range(2)]
    xstg = k.sb("xstgA", [128, NFC * TOK], BF16, ctx=lnsc)
    r_xstg = [Reg(f"xstg{i}") for i in range(NFC)]
    sqb = [k.sb(f"sqA{i}", [128, TOK], F32, ctx=lnsc) for i in range(2)]
    r_sqb = [Reg(f"sq{i}") for i in range(2)]
    srow = k.sb("rowA", [1, 4 * TOK], F32, ctx=lnsc)
    ps_sum = k.ps("psLs", [1, TOK], F32, ctx=lnsc)
    ps_sq = k.ps("psLq", [1, TOK], F32, ctx=lnsc)
    ps_bc = k.ps("psLb", [128, 2 * TOK], F32, ctx=lnsc)
    mean_r, ex2_r = srow[:, 0:TOK], srow[:, TOK:2 * TOK]
    var_r, rstd_r = srow[:, 2 * TOK:3 * TOK], srow[:, 3 * TOK:4 * TOK]
    if fused:
        xsrc = rearr_feat(io["x_in"])
        lnst = [k.sb(f"lnst{i}", [128, TOK], BF16, ctx=lnsc) for i in range(2)]
        r_lnst = [Reg(f"lnst{i}") for i in range(2)]
        ccl_dst = io["ccl_in"].rearrange("(c p) t -> p c t", p=128)
        r_ccl = Reg("ccl")
        halves = (0,)
    else:
        xsrc = rearr_feat(x_d)
        halves = (0, 1)
    for half in halves:
        r_mean, r_var, r_ex2, r_rstd = Reg(), Reg(), Reg(), Reg()
        r_pssum, r_pssq, r_psbc = Reg(), Reg(), Reg()
        tsl = slice(half * TOK, (half + 1) * TOK) if not fused else slice(0, TOK)
        for c in range(NFC):
            s, rs = xstg[:, c * TOK:(c + 1) * TOK], r_xstg[c]
            k.dma("gpsimd", s, xsrc[:, c, tsl], w=[rs])
            b, rb = sqb[c % 2], r_sqb[c % 2]
            k.op("scalar", lambda e, o=b, i=s: e.activation(o[:], i, AF.Square),
                 r=[rs], w=[rb])
            for j in range(2):
                sl = slice(j * 512, (j + 1) * 512)
                k.op("tensor", lambda e, p=ps_sum[:, sl], o=ocbf[:],
                     i=xstg[:, c * TOK + sl.start:c * TOK + sl.stop],
                     st=(c == 0), sp=(c == NFC - 1):
                     e.matmul(p, o, i, start=st, stop=sp, skip_group_check=True),
                     r=[rs, rC], w=([r_pssum] if c in (0, NFC - 1) else []), small=True)
                k.op("tensor", lambda e, p=ps_sq[:, sl], o=C.ones_col[:], i=b[:, sl],
                     st=(c == 0), sp=(c == NFC - 1):
                     e.matmul(p, o, i, start=st, stop=sp, skip_group_check=True),
                     r=[rb, C.reg], w=([r_pssq] if c in (0, NFC - 1) else []), small=True)
        k.op("vector", lambda v, o=mean_r, i=ps_sum: v.tensor_scalar_mul(o, i[:], 1.0 / D),
             r=[r_pssum], w=[r_mean], small=True)
        k.op("vector", lambda v, o=ex2_r, i=ps_sq: v.tensor_scalar_mul(o, i[:], 1.0 / D),
             r=[r_pssq], w=[r_ex2], small=True)
        k.op("vector", lambda v, o=var_r, a=mean_r: v.tensor_mul(o, a, a),
             r=[r_mean], w=[r_var], small=True)
        k.op("vector", lambda v, o=var_r, a=ex2_r: v.tensor_sub(o, a, o),
             r=[r_ex2, r_var], w=[r_var], small=True)
        k.op("vector", lambda v, o=var_r: v.tensor_scalar_add(o, o, 1e-5),
             r=[r_var], w=[r_var], small=True)
        k.op("scalar", lambda e, o=rstd_r, i=var_r: e.activation(o, i, AF.Ln),
             r=[r_var], w=[r_rstd], small=True)
        k.op("scalar", lambda e, o=rstd_r, i=rstd_r: e.activation(o, i, AF.Exp, scale=-0.5),
             r=[r_rstd], w=[r_rstd], small=True)
        k.op("vector", lambda v, o=mean_r, a=rstd_r: v.tensor_mul(o, o, a),
             r=[r_rstd, r_mean], w=[r_mean], small=True)
        for j in range(2):
            sl = slice(j * 512, (j + 1) * 512)
            k.op("tensor", lambda e, p=ps_bc[:, 0:TOK][:, sl], o=C.ones_row[:], i=rstd_r[:, sl]:
                 e.matmul(p, o, i, start=True, stop=True),
                 r=[r_rstd, C.reg], w=[r_psbc], small=True)
            k.op("tensor", lambda e, p=ps_bc[:, TOK:][:, sl], o=C.ones_row[:], i=mean_r[:, sl]:
                 e.matmul(p, o, i, start=True, stop=True),
                 r=[r_mean, C.reg], w=[r_psbc], small=True)
        for c in range(NFC):
            s, rs = xstg[:, c * TOK:(c + 1) * TOK], r_xstg[c]
            b, rb = sqb[c % 2], r_sqb[c % 2]
            k.op("vector", lambda v, o=b, i=s, b2=ps_bc[:, 0:TOK]:
                 v.tensor_mul(o[:], i, b2[:]), r=[rs, r_psbc], w=[rb])
            if fused:
                st2, rst2 = lnst[c % 2], r_lnst[c % 2]
                k.op("vector", lambda v, o=st2, i=b, b2=ps_bc[:, TOK:]:
                     v.tensor_tensor(o[:], i[:], b2[:], ALU.subtract),
                     r=[rb, r_psbc], w=[rst2])
                k.dma("sync", ccl_dst[:, c, :], st2[:], r=[rst2], w=[r_ccl])
            else:
                k.op("vector", lambda v, o=ln[:, c * T + half * TOK:c * T + (half + 1) * TOK],
                     i=b, b2=ps_bc[:, TOK:]:
                     v.tensor_tensor(o, i[:], b2[:], ALU.subtract),
                     r=[rb, r_psbc], w=[r_ln[2 * c + half]])
    emit_wv(0, r_xstg)
    emitW(0, r_xstg)
    if nheads > 1:
        emitW(1, r_xstg)
    if fused:
        r_cclo = Reg("cclo")
        k.op("gpsimd", lambda g, i=io["ccl_in"], o=io["ccl_out"]:
             g.collective_compute("AllGather", ALU.bypass, replica_groups=PAIRS,
                                  ins=[i.opt()], outs=[o.opt()]),
             r=[r_ccl], w=[r_cclo], small=True)
        gsrc = io["ccl_out"].rearrange("r (c p) t -> r p c t", p=128)
        for h2 in range(2):
            for c in range(NFC):
                k.dma("sync", ln[:, c * T + h2 * TOK:c * T + (h2 + 1) * TOK],
                      gsrc[h2][:, c, :], r=[r_cclo], w=[r_ln[2 * c + h2]])
    lnsc.close()
    k.barrier()
    if stages <= 1:
        if "dbgln" in io:
            r_dl = Reg("dbgln")
            for c in range(NFC):
                k.dma("sync", io["dbgln"][:, c * T:(c + 1) * T],
                      ln[:, c * T:(c + 1) * T], r=[r_ln[2 * c], r_ln[2 * c + 1]], w=[r_dl])
            k.op("sync", lambda e: e.nop(), r=[r_dl])
        wbsc.close()
        main.close()
        return k

    p512 = [k.ps(f"p512_{i}", [128, 512], F32) for i in range(4)]
    rp512 = [Reg(f"p512_{i}") for i in range(4)]
    ps_l = k.ps("ps_l", [1, 512], F32)
    rps_l = Reg("ps_l")
    ps_o = k.ps("ps_o", [128, 512], F32)
    rps_o = Reg("ps_o")

    def lnr(c):
        return [r_ln[2 * c], r_ln[2 * c + 1]]

    hsc = ExitStack()
    vt_all = k.sb("vt_all", [128, 16 * 1024], BF16, ctx=hsc)
    r_vt = Reg("vt")
    qg2 = [k.sb(f"qg{i}", [128, T], BF16, ctx=hsc) for i in range(2)]
    knh2 = [k.sb(f"knh{i}", [128, T], BF16, ctx=hsc) for i in range(2)]
    r_qg2 = [Reg(f"qg{i}") for i in range(2)]
    r_kn2 = [Reg(f"kn{i}") for i in range(2)]
    sq2 = [k.sb(f"sq2_{i}", [128, 512], BF16, ctx=hsc) for i in range(2)]
    r_sq2 = [Reg(f"sq2{i}") for i in range(2)]
    rowq1 = k.sb("rowq", [1, T], F32, ctx=hsc)
    rowq2 = [rowq1, rowq1]
    mpart2 = [k.sb(f"mpart{i}", [128, 16], F32, ctx=hsc) for i in range(2)]
    negmb1 = k.sb("negmb", [1, T], BF16, ctx=hsc)
    negmb2 = [negmb1, negmb1]
    r_negmb1 = Reg("negmb")
    r_negmb2 = [r_negmb1, r_negmb1]
    expb = [k.sb(f"expb{i}", [128, 512], BF16, ctx=hsc) for i in range(2)]
    r_expb = [Reg(f"expb{i}") for i in range(2)]
    rlrow1 = k.sb("rlrow", [1, 512], F32, ctx=hsc)
    rlrow2 = [rlrow1, rlrow1]
    r_rlrow1 = Reg("rlrow")
    r_rlrow2 = [r_rlrow1, r_rlrow1]
    rlb1 = k.sb("rlb", [128, 512], F32, ctx=hsc)
    rlb2 = [rlb1, rlb1]
    r_rlb1 = Reg("rlb")
    r_rlb2 = [r_rlb1, r_rlb1]
    ps_l2 = [ps_l, k.ps("ps_l1", [1, 512], F32, ctx=hsc)]
    rps_l2 = [rps_l, Reg("ps_l1")]
    ps_o2 = [ps_o, k.ps("ps_o1", [128, 512], F32, ctx=hsc)]
    rps_o2 = [rps_o, Reg("ps_o1")]

    r_vt2 = [Reg("vt0"), Reg("vt1")]

    def genV(ns, banks):
        for tc in range(16):
            pp, rp = banks[tc % len(banks)]
            for c in range(NFC):
                k.op("tensor", lambda e, p=pp,
                     w=ln[:, c * T + tc * 128:c * T + (tc + 1) * 128],
                     i=wv_sb[:, c * 512:(c + 1) * 512],
                     st=(c == 0), sp=(c == NFC - 1):
                     e.matmul(p[:], w, i, start=st, stop=sp, skip_group_check=True),
                     r=[r_wv] + lnr(c), w=([rp] if c in (0, NFC - 1) else []))
            k.op("vector", lambda v,
                 o=vt_all[:, tc * 1024 + ns * 512:tc * 1024 + (ns + 1) * 512],
                 i=pp: v.tensor_copy(o, i[:]), r=[rp], w=[r_vt2[ns]])
            yield


    def genAB(h):
        hp = h % 2
        qg, knh = qg2[hp], knh2[hp]
        r_qg, r_kn = r_qg2[hp], r_kn2[hp]
        rowq = rowq2[hp]
        ps_lh, rps_lh = ps_l2[hp], rps_l2[hp]
        bq, bk = 2 * hp, 2 * hp + 1
        pend = [None]

        def flush_pend(bpp, brp):
            if pend[0] is not None:
                pend[0](bpp, brp)
                pend[0] = None

        for wi, (buf, dst, rdst, gcol) in enumerate(
                ((bq, qg, r_qg, gqk[:, 0:1]), (bk, knh, r_kn, gqk[:, 1:2]))):
            r_ssq = Reg("ssq")
            sc_f = float(128 ** -0.5) if wi == 0 else 1.0
            for ttp in range(2):
                for c in range(NFC):
                    for t2 in range(2):
                        tt = ttp * 2 + t2
                        pp, rp = p512[t2], rp512[t2]
                        k.op("tensor", lambda e, p=pp, w=wbh[buf][:, c * 128:(c + 1) * 128],
                             i=ln[:, c * T + tt * 512:c * T + (tt + 1) * 512],
                             st=(c == 0), sp=(c == NFC - 1):
                             e.matmul(p[:], w, i, start=st, stop=sp, skip_group_check=True),
                             r=[r_wbh[buf]] + lnr(c), w=([rp] if c in (0, NFC - 1) else []))
                        yield
                for t2 in range(2):
                    tt = ttp * 2 + t2
                    pp, rp = p512[t2], rp512[t2]
                    dsl = dst[:, tt * 512:(tt + 1) * 512]
                    sl = slice(tt * 512, (tt + 1) * 512)
                    b2, rb2 = sq2[t2], r_sq2[t2]
                    k.op("vector", lambda v, o=dsl, i=pp, s1=gcol:
                         v.tensor_scalar_mul(o, i[:], s1), r=[rp, rC], w=[rdst, rp])
                    k.op("vector", lambda v, o=b2, i=dsl:
                         v.tensor_mul(o[:], i, i), r=[rdst], w=[rb2], small=True)
                    k.op("tensor", lambda e, p=ps_lh, o=gqk2[:, wi:wi + 1], i=b2:
                         e.matmul(p[:], o, i[:], start=True, stop=True),
                         r=[rb2, rC], w=[rps_lh], small=True)
                    k.op("vector", lambda v, o=rowq[:, sl],
                         i=ps_lh: v.tensor_scalar_mul(o, i[:], 1.0 / 128),
                         r=[rps_lh], w=[r_ssq], small=True)
                    yield
                    flush_pend(pp, rp)
                    k.op("vector", lambda v, o=rowq[:, sl]: v.tensor_scalar_add(o, o, 1e-5),
                         r=[r_ssq], w=[r_ssq], small=True)
                    k.op("scalar", lambda e, o=rowq[:, sl], i=rowq[:, sl]:
                         e.activation(o, i, AF.Ln), r=[r_ssq], w=[r_ssq], small=True)
                    k.op("scalar", lambda e, o=rowq[:, sl], i=rowq[:, sl], s=-0.5:
                         e.activation(o, i, AF.Exp, scale=s), r=[r_ssq], w=[r_ssq], small=True)
                    if wi == 0:
                        k.op("vector", lambda v, o=rowq[:, sl], s1=sc_f:
                             v.tensor_scalar_mul(o, o, s1), r=[r_ssq], w=[r_ssq], small=True)

                    def mk_fold(sl=sl, dsl=dsl, rdst=rdst, rr=r_ssq):
                        def f(bpp, brp):
                            k.op("tensor", lambda e, p=bpp, o=C.ones_row[:], i=rowq[:, sl]:
                                 e.matmul(p[:], o, i, start=True, stop=True),
                                 r=[rr, C.reg], w=[brp], small=True)
                            k.op("vector", lambda v, o=dsl, i=bpp:
                                 v.tensor_mul(o, o, i[:]), r=[brp, rdst], w=[rdst])
                        return f
                    pend[0] = mk_fold()
                    yield

        flush_pend(p512[0], rp512[0])
        yield

    def genCD(h):
        hp = h % 2
        qg, knh = qg2[hp], knh2[hp]
        r_qg, r_kn = r_qg2[hp], r_kn2[hp]
        mpart, negmb = mpart2[hp], negmb2[hp]
        r_negmb = r_negmb2[hp]
        rlrow, rlb = rlrow2[hp], rlb2[hp]
        r_rlrow, r_rlb = r_rlrow2[hp], r_rlb2[hp]
        ps_lh, rps_lh = ps_l2[hp], rps_l2[hp]
        ps_oh, rps_oh = ps_o2[hp], rps_o2[hp]
        if stages <= 2:
            return
        r_mq = [Reg("mq0"), Reg("mq1")]

        def emit_mtr(qc):
            off = (qc % 2) * 8
            rm = r_mq[qc % 2]
            k.op("tensor", lambda e, p=ps_lh[:, 0:128], o=mpart[:, off + 4:off + 5],
                 i=i_f32[:]: e.matmul(p, o, i, start=True, stop=True),
                 r=[rm, rC], w=[rps_lh], small=True)
            k.op("vector", lambda v, o=negmb[:, qc * 128:(qc + 1) * 128],
                 i=ps_lh[:, 0:128]: v.tensor_copy(o, i),
                 r=[rps_lh], w=[r_negmb], small=True)

        for qc in range(16):
            ntile = qc // 4 + 1
            off = (qc % 2) * 8
            rm = r_mq[qc % 2]
            for kt in range(ntile):
                wdt = 512 if kt < qc // 4 else (qc % 4 + 1) * 128
                diag = (kt == qc // 4)
                pp, rp = p512[2 + kt % 2], rp512[2 + kt % 2]
                k.op("tensor", lambda e, p=pp[:, 0:wdt], w=qg[:, qc * 128:(qc + 1) * 128],
                     i=knh[:, kt * 512:kt * 512 + wdt], sp=(not diag):
                     e.matmul(p, w, i, start=True, stop=sp, skip_group_check=True),
                     r=[r_qg, r_kn], w=[rp])
                if diag:
                    k.op("tensor", lambda e, p=pp[:, wdt - 128:wdt], o=i_bf[:], i=dmask[:]:
                         e.matmul(p, o, i, start=False, stop=True, skip_group_check=True),
                         r=[rC], w=[rp])
                k.op("vector", lambda v, o=mpart[:, off + kt:off + kt + 1],
                     i=pp[:, 0:wdt], wd=wdt: v.reduce_max(o, i, axis=AX.X),
                     r=[rp], w=[rm], small=True)
                yield "c"
            k.op("vector", lambda v, o=mpart[:, off + 4:off + 5], i=mpart[:, off:off + ntile]:
                 v.reduce_max(o, i, axis=AX.X), r=[rm], w=[rm], small=True)
            k.op("vector", lambda v, o=mpart[:, off + 4:off + 5]:
                 v.tensor_scalar_mul(o, o, -1.0), r=[rm], w=[rm], small=True)
            if qc > 0:
                emit_mtr(qc - 1)
            yield "c"
        emit_mtr(15)
        if stages <= 3:
            return
        for qt in range(4):
            nkc = qt * 4 + 4
            for kc in range(nkc):
                pp, rp = p512[2 + kc % 2], rp512[2 + kc % 2]
                k.op("tensor", lambda e, p=pp, w=knh[:, kc * 128:(kc + 1) * 128],
                     i=qg[:, qt * 512:(qt + 1) * 512]:
                     e.matmul(p[:], w, i, start=True, stop=False, skip_group_check=True),
                     r=[r_qg, r_kn], w=[rp])
                k.op("tensor", lambda e, p=pp, o=orbf[:],
                     i=negmb[:, qt * 512:(qt + 1) * 512], last=(kc < qt * 4):
                     e.matmul(p[:], o, i, start=False, stop=last, skip_group_check=True),
                     r=[r_negmb], w=[rp])
                if kc >= qt * 4:
                    j = kc - qt * 4
                    k.op("tensor", lambda e, p=pp, o=i_bf[:], i=tmask[:, j * 512:(j + 1) * 512]:
                         e.matmul(p[:], o, i, start=False, stop=True, skip_group_check=True),
                         r=[rC], w=[rp])
                eb, re = expb[kc % 2], r_expb[kc % 2]
                k.op("scalar", lambda e, o=eb, i=pp: e.activation(o[:], i[:], AF.Exp),
                     r=[rp], w=[re, rp])
                k.op("tensor", lambda e, p=ps_lh, o=ocbf[:], i=eb,
                     st=(kc == 0), sp=(kc == nkc - 1):
                     e.matmul(p[:], o, i[:], start=st, stop=sp, skip_group_check=True),
                     r=[re, rC], w=([rps_lh] if kc in (0, nkc - 1) else []), small=True)
                k.op("tensor", lambda e, p=ps_oh,
                     w=vt_all[:, kc * 1024 + h * 128:kc * 1024 + (h + 1) * 128], i=eb,
                     st=(kc == 0), sp=(kc == nkc - 1):
                     e.matmul(p[:], w, i[:], start=st, stop=sp, skip_group_check=True),
                     r=[re, r_vt2[h // 4]], w=([rps_oh] if kc in (0, nkc - 1) else []))
                yield "d"
            k.op("vector", lambda v, o=rlrow[:], i=ps_lh: v.reciprocal(o, i[:]),
                 r=[rps_lh], w=[r_rlrow, rps_lh], small=True)
            pb3, rb3 = p512[2 + hp], rp512[2 + hp]
            k.op("tensor", lambda e, p=pb3, o=C.ones_row[:], i=rlrow[:]:
                 e.matmul(p[:], o, i, start=True, stop=True),
                 r=[r_rlrow, C.reg], w=[rb3], small=True)
            k.op("vector", lambda v, o=rlb, i=pb3: v.tensor_copy(o[:], i[:]),
                 r=[rb3], w=[r_rlb])
            yield "d"
            k.op("vector", lambda v, o=ob[:, h * T + qt * 512:h * T + (qt + 1) * 512],
                 i=ps_oh, b2=rlb: v.tensor_mul(o, i[:], b2[:]),
                 r=[rps_oh, r_rlb], w=[r_ob[h * 4 + qt]])
            yield "d"

    NHH = min(NH, nheads)
    for h in range(NHH + 1):
        ab = genAB(h) if h < NHH else None
        cd = genCD(h - 1) if h >= 1 else None
        if h == 1 and ab is not None:
            gv1 = genV(1, [(ps_o2[1], rps_o2[1])])
            emit_wv(1)

            def mixed(ab0=ab, gv=gv1):
                i = 0
                for chunk in ab0:
                    yield chunk
                    i += 1
                    if i > 60 and i % 5 == 0:
                        next(gv, None)
                for _ in gv:
                    yield
            ab = mixed()
        if cd is None:
            gv = genV(0, [(p512[2], rp512[2]), (p512[3], rp512[3])])
            acc, alive = 0.0, True
            for _ in gv:
                acc += 8.5
                while acc >= 1.0 and alive:
                    try:
                        next(ab)
                    except StopIteration:
                        alive = False
                    acc -= 1.0
            while alive:
                try:
                    next(ab)
                except StopIteration:
                    alive = False
        elif ab is None:
            wo_sb = ln[:, 0:NH * D]
            r_wo8 = [Reg(f"wo{i}") for i in range(NH)]
            wo_src = wot_d.rearrange("(hh p) o -> p hh o", p=128)
            r_abl = [r_qg2[(NHH - 1) % 2], r_kn2[(NHH - 1) % 2]]
            for hh in range(NH):
                k.dma("gpsimd", wo_sb[:, hh * D:(hh + 1) * D]
                      .rearrange("p (a o) -> p a o", a=1),
                      wo_src[:, hh:hh + 1, :], r=r_abl, w=[r_wo8[hh]])
            for _ in cd:
                pass
        else:
            acc, alive = 0.0, True
            for tag in cd:
                acc += 1.0 if tag == "c" else 2.0
                while acc >= 1.0 and alive:
                    try:
                        next(ab)
                    except StopIteration:
                        alive = False
                    acc -= 1.0
            while alive:
                try:
                    next(ab)
                except StopIteration:
                    alive = False
        if h + 2 < NHH:
            emitW(h + 2)
    if "dbgq" in io:
        r_dq = Reg("dbgq")
        k.dma("sync", io["dbgq"], qg2[0][:], r=[r_qg2[0]], w=[r_dq])
        k.dma("sync", io["dbgk"], knh2[0][:], r=[r_kn2[0]], w=[r_dq])
        k.op("sync", lambda e: e.nop(), r=[r_dq])
    hsc.close()
    k.barrier()
    if stages <= 4:
        wbsc.close()
        main.close()
        return k

    osc = ExitStack()
    postg = [k.sb(f"postg{i}", [128, 512], BF16 if fused else F32, ctx=osc) for i in range(2)]
    r_postg = [Reg(f"postg{i}") for i in range(2)]
    r_pod = Reg("po_dram")
    podst = rearr_feat(po_d)
    NHH = min(NH, nheads)
    for oc in range(NFC):
        for tt in range(4):
            pp, rp = p512[(4 * oc + tt) % 4], rp512[(4 * oc + tt) % 4]
            for hh in range(NHH):
                k.op("tensor", lambda e, p=pp, w=wo_sb[:, hh * D + oc * 128:hh * D + (oc + 1) * 128],
                     i=ob[:, hh * T + tt * 512:hh * T + (tt + 1) * 512],
                     st=(hh == 0), sp=(hh == NHH - 1):
                     e.matmul(p[:], w, i, start=st, stop=sp, skip_group_check=True),
                     r=[r_wo8[hh], r_ob[hh * 4 + tt]], w=([rp] if hh in (0, NHH - 1) else []))
            g2 = (2 * oc + tt) % 2
            k.op("vector", lambda v, o=postg[g2], i=pp: v.tensor_copy(o[:], i[:]),
                 r=[rp], w=[r_postg[g2]])
            k.dma("sync", podst[:, oc, tt * 512:(tt + 1) * 512], postg[g2][:],
                  r=[r_postg[g2]], w=[r_pod])
    if fused:
        r_gpo = Reg("gpo")
        k.op("gpsimd", lambda g, i=po_d, o=io["Gpo"]:
             g.collective_compute("AllGather", ALU.bypass, replica_groups=PAIRS,
                                  ins=[i.opt()], outs=[o.opt()]),
             r=[r_pod], w=[r_gpo], small=True)
    k.op("sync", lambda e: e.nop(), r=[r_pod])
    osc.close()
    wbsc.close()
    main.close()
    k.prefix = pfx0
    return k


def run_attn0(xs, inputs):
    wq = np.asarray(inputs["L0_attn_Wq"], np.float32)
    wk = np.asarray(inputs["L0_attn_Wk"], np.float32)
    wv = np.asarray(inputs["L0_attn_Wv"], np.float32)
    wo = np.asarray(inputs["L0_attn_Wo"], np.float32)
    wqt = np.ascontiguousarray(wq.T)
    wkt = np.ascontiguousarray(wk.T)
    wvt = np.ascontiguousarray(wv.T)
    wot = np.ascontiguousarray(wo.T)
    gq = np.ascontiguousarray(np.asarray(inputs["L0_attn_gq"], np.float32).reshape(128, 1))
    gk = np.ascontiguousarray(np.asarray(inputs["L0_attn_gk"], np.float32).reshape(128, 1))
    k = build_attn0()
    in_maps = []
    for c in range(8):
        s, hg = c // 2, c % 2
        xseq = np.ascontiguousarray(np.concatenate([xs[2 * s], xs[2 * s + 1]], axis=1))
        cols = slice(hg * 1024, (hg + 1) * 1024)
        in_maps.append(dict(
            xseq=xseq,
            wqt=np.ascontiguousarray(wqt[:, cols]),
            wkt=np.ascontiguousarray(wkt[:, cols]),
            wvt=np.ascontiguousarray(wvt[:, cols]),
            wot=np.ascontiguousarray(wot[cols, :]),
            gq=gq, gk=gk,
        ))
    res = k.run(in_maps)
    pos = [np.asarray(res.results[c]["po"]) for c in range(8)]
    pas, pbs = [], []
    for c in range(8):
        s, h = c // 2, c % 2
        tsl = slice(h * 1024, (h + 1) * 1024)
        pas.append(np.ascontiguousarray(pos[2 * s][:, tsl]))
        pbs.append(np.ascontiguousarray(pos[2 * s + 1][:, tsl]))
    return pas, pbs



def build_moe(with_partials):
    k = K()
    io = dict(
        x_in=k.din("x_t", [D, TOK], F32),
        rw=k.din("rw", [D, 8], F32), guw=k.din("guw", [8, D, 2048], F32),
        dww=k.din("dww", [8, 1024, D], F32), sgw=k.din("sgw", [D, 1024], F32),
        suw=k.din("suw", [D, 1024], F32), sdw=k.din("sdw", [1024, D], F32),
        x_out=k.dout("xo", [D, TOK], F32),
    )
    if with_partials:
        io["pa"] = k.din("pa", [D, TOK], F32)
        io["pb"] = k.din("pb", [D, TOK], F32)
        io["x2"] = k.dint("x2", [D, TOK], F32)
    emit_moe(k, Consts(k), io, "M0_", with_partials, legacy_partials=with_partials)
    return k


def emit_moe(k, C, io, tag, with_partials, legacy_partials=False):
    pfx0 = k.prefix
    k.prefix = tag
    x_d = io["x_in"]
    rw_d, gu_d, dw_d = io["rw"], io["guw"], io["dww"]
    sg_d, su_d, sd_d = io["sgw"], io["suw"], io["sdw"]
    xo_d = io["x_out"]
    x2_d = io.get("x2", x_d)
    i_bf = k.sb("i_bfM", [128, 128], BF16)
    rC = Reg("mconst")
    k.op("gpsimd", lambda g: g.memset(i_bf[:], 0.0), w=[rC], small=True)
    k.op("gpsimd", lambda g: g.affine_select(out=i_bf[:], in_=i_bf[:],
         compare_op=ALU.not_equal, fill=1.0, base=0, pattern=[[-1, 128]],
         channel_multiplier=1), w=[rC], small=True)

    ln = k.sb("lnM", [128, NFC * TOK], BF16)
    r_ln = [Reg(f"ln{c}") for c in range(NFC)]
    probs = k.sb("probs", [128, 8 * 8], F32)
    r_probs = Reg("probs")
    gtbuf = [k.sb(f"gtM{i}", [128, 512], BF16) for i in range(2)]
    r_gt = [Reg(f"gt{i}") for i in range(2)]
    xres = [k.sb(f"xresM{i}", [128, TOK], F32) for i in range(2)]
    r_xres = [Reg(f"xres{i}") for i in range(2)]
    xout = [k.sb(f"xoutM{i}", [128, 512], F32) for i in range(4)]
    r_xout = [Reg(f"xo{i}") for i in range(4)]
    r_xod = Reg("xo_dram")
    r_x2d = Reg("x2_dram")

    xsc = ExitStack()
    x_sb, r_x = emit_xload(k, xsc, x_d, x2_d, io, with_partials, legacy_partials, r_x2d)

    lnsc = ExitStack()
    emit_layernorm(k, lnsc, C, x_sb, r_x, ln, r_ln, TOK, "m")
    lnsc.close()
    xsc.close()
    k.barrier()

    acc = k.sb("accM", [128, NFC * TOK], BF16)
    r_acc = [Reg(f"acc{i}") for i in range(32)]
    htok = k.sb("htok", [128, 8 * 1024], BF16)
    r_htok = [Reg(f"ht{i}") for i in range(8)]
    hT = k.sb("hT", [128, 8 * 1024], BF16)
    r_hT = [Reg(f"hT{i}") for i in range(16)]
    wgu = [k.sb(f"wguM{i}", [128, NFC * 512], BF16) for i in range(4)]
    r_wgu = [Reg(f"wgu{i}") for i in range(4)]
    wdn = [k.sb(f"wdnM{i}", [128, 8 * 256], BF16) for i in range(2)]
    r_wdn = [Reg(f"wdn{i}") for i in range(2)]

    ps_z = [k.ps(f"pszM{i}", [128, 512], F32) for i in range(2)]
    r_psz = [Reg(f"psz{i}") for i in range(2)]
    ps_t = [k.ps(f"pstM{i}", [128, 512], BF16) for i in range(2)]
    r_pst = [Reg(f"pst{i}") for i in range(2)]
    ps_d = [k.ps(f"psdM{i}", [128, 512], F32) for i in range(2)]
    r_psd = [Reg(f"psd{i}") for i in range(2)]
    ps_r = k.ps("psrM", [128, 8], F32)
    r_psr = Reg("psr")

    rw_sb = k.sb("rwM", [128, NFC * 8], BF16)
    r_rw = Reg("rw")
    k.dma("gpsimd", rw_sb[:].rearrange("p (c e) -> p c e", c=NFC),
          rw_d.rearrange("(c p) e -> p c e", p=128), w=[r_rw])
    lgt = k.sb("lgtM", [128, 4 * 8], F32)
    r_lgt = Reg("lgt")
    for tc in range(8):
        for c in range(NFC):
            k.op("tensor", lambda e, p=ps_r, w=fsl(ln, c, TOK, tc * 128, (tc + 1) * 128),
                 i=rw_sb[:, c * 8:(c + 1) * 8], st=(c == 0), sp=(c == NFC - 1):
                 e.matmul(p[:], w, i, start=st, stop=sp, skip_group_check=True),
                 r=[r_rw] + [r_ln[c]], w=([r_psr] if c in (0, NFC - 1) else []))
        lg, m1, m2, msk = lgt[:, 0:8], lgt[:, 8:9], lgt[:, 9:10], lgt[:, 16:24]
        k.op("vector", lambda v, o=lg, i=ps_r: v.tensor_copy(o, i[:]),
             r=[r_psr], w=[r_lgt], small=True)
        k.op("vector", lambda v, o=m1, i=lg: v.reduce_max(o, i, axis=AX.X),
             r=[r_lgt], w=[r_lgt], small=True)
        k.op("vector", lambda v, o=msk, i=lg, s=m1: v.tensor_scalar(o, i, s, None, ALU.is_ge),
             r=[r_lgt], w=[r_lgt], small=True)
        k.op("vector", lambda v, o=msk, i=msk, a=lg:
             v.scalar_tensor_tensor(o, i, NEG, a, ALU.mult, ALU.add),
             r=[r_lgt], w=[r_lgt], small=True)
        k.op("vector", lambda v, o=m2, i=msk: v.reduce_max(o, i, axis=AX.X),
             r=[r_lgt], w=[r_lgt], small=True)
        k.op("scalar", lambda e, o=msk, i=lg: e.activation(o, i, AF.Sigmoid),
             r=[r_lgt], w=[r_lgt], small=True)
        k.op("vector", lambda v, o=lg, i=lg, s=m2: v.tensor_scalar(o, i, s, None, ALU.is_ge),
             r=[r_lgt], w=[r_lgt], small=True)
        k.op("vector", lambda v, o=probs[:, tc * 8:(tc + 1) * 8], i=lg, a=msk:
             v.tensor_mul(o, i, a), r=[r_lgt], w=[r_probs], small=True)

    gu_src = gu_d.rearrange("e (c p) i -> e p c i", p=128)
    dw_src = dw_d.rearrange("e (ic p) o -> e p ic o", p=128)
    sg_src = sg_d.rearrange("(c p) i -> p c i", p=128)
    su_src = su_d.rearrange("(c p) i -> p c i", p=128)
    sd_src = sd_d.rearrange("(ic p) o -> p ic o", p=128)

    def wgu_load(e, pair):
        buf = ((e * 2 + pair) % 2) * 2
        gsl = pair * 512
        if e == 0:
            k.dma("gpsimd", wgu[buf][:].rearrange("p (c i) -> p c i", c=NFC),
                  sg_src[:, :, gsl:gsl + 512], w=[r_wgu[buf]])
            k.dma("gpsimd", wgu[buf + 1][:].rearrange("p (c i) -> p c i", c=NFC),
                  su_src[:, :, gsl:gsl + 512], w=[r_wgu[buf + 1]])
        else:
            k.dma("gpsimd", wgu[buf][:].rearrange("p (c i) -> p c i", c=NFC),
                  gu_src[e - 1][:, :, gsl:gsl + 512], w=[r_wgu[buf]])
            k.dma("gpsimd", wgu[buf + 1][:].rearrange("p (c i) -> p c i", c=NFC),
                  gu_src[e - 1][:, :, 1024 + gsl:1024 + gsl + 512], w=[r_wgu[buf + 1]])

    wgu_load(0, 0)
    for e in range(9):
        for pair in range(2):
            ne, np_ = (e, 1) if pair == 0 else (e + 1, 0)
            if ne < 9:
                wgu_load(ne, np_)
            buf = ((e * 2 + pair) % 2) * 2
            gsl = pair * 512
            for tc in range(8):
                pz, rz = ps_z[0], r_psz[0]
                for c in range(NFC):
                    k.op("tensor", lambda e2, p=pz, w=fsl(ln, c, TOK, tc * 128, (tc + 1) * 128),
                         i=wgu[buf][:, c * 512:(c + 1) * 512],
                         st=(c == 0), sp=(c == NFC - 1):
                         e2.matmul(p[:], w, i, start=st, stop=sp, skip_group_check=True),
                         r=[r_wgu[buf], r_ln[c]], w=([rz] if c in (0, NFC - 1) else []))
                g2 = tc % 2
                if e == 0:
                    k.op("scalar", lambda e2, o=gtbuf[g2], i=pz:
                         e2.activation(o[:], i[:], AF.Silu), r=[rz], w=[r_gt[g2], rz])
                else:
                    k.op("scalar", lambda e2, o=gtbuf[g2], i=pz,
                         s1=probs[:, tc * 8 + (e - 1):tc * 8 + e]:
                         e2.activation(o[:], i[:], AF.Silu, scale=s1),
                         r=[rz, r_probs], w=[r_gt[g2], rz])
                pu, ru = ps_z[1], r_psz[1]
                for c in range(NFC):
                    k.op("tensor", lambda e2, p=pu, w=fsl(ln, c, TOK, tc * 128, (tc + 1) * 128),
                         i=wgu[buf + 1][:, c * 512:(c + 1) * 512],
                         st=(c == 0), sp=(c == NFC - 1):
                         e2.matmul(p[:], w, i, start=st, stop=sp, skip_group_check=True),
                         r=[r_wgu[buf + 1], r_ln[c]], w=([ru] if c in (0, NFC - 1) else []))
                if e == 0:
                    k.op("vector", lambda v, o=htok[:, tc * 1024 + gsl:tc * 1024 + gsl + 512],
                         i=pu, g3=gtbuf[g2]: v.tensor_tensor(o, i[:], g3[:], ALU.mult),
                         r=[ru, r_gt[g2]], w=[r_htok[tc]])
                else:
                    k.op("vector", lambda v, o=htok[:, tc * 1024 + gsl:tc * 1024 + gsl + 512],
                         i=pu, s1=probs[:, tc * 8 + (e - 1):tc * 8 + e], g3=gtbuf[g2]:
                         v.scalar_tensor_tensor(o, i[:], s1, g3[:], ALU.mult, ALU.mult),
                         r=[ru, r_probs, r_gt[g2]], w=[r_htok[tc]])
        for tc in range(8):
            for grp in range(2):
                pt, rt = ps_t[grp], r_pst[grp]
                for s4 in range(4):
                    ic = grp * 4 + s4
                    k.op("tensor", lambda e2, p=pt[:, s4 * 128:(s4 + 1) * 128],
                         i=htok[:, tc * 1024 + ic * 128:tc * 1024 + (ic + 1) * 128],
                         ii=i_bf[:]: e2.transpose(p, i, ii),
                         r=[r_htok[tc], rC], w=[rt])
                dst = hT[:].rearrange("p (ic t) -> p ic t", ic=8)
                k.op("vector", lambda v, o=dst[:, grp * 4:(grp + 1) * 4, tc * 128:(tc + 1) * 128],
                     i=pt[:].rearrange("p (s t) -> p s t", s=4): v.tensor_copy(o, i),
                     r=[rt], w=[r_hT[ic2 * 2 + tc // 4] for ic2 in range(grp * 4, grp * 4 + 4)])
        for ob in range(8):
            buf = ob % 2
            osl = slice(ob * 256, (ob + 1) * 256)
            if e == 0:
                k.dma("gpsimd", wdn[buf][:].rearrange("p (ic o) -> p ic o", ic=8),
                      sd_src[:, :, osl], w=[r_wdn[buf]])
            else:
                k.dma("gpsimd", wdn[buf][:].rearrange("p (ic o) -> p ic o", ic=8),
                      dw_src[e - 1][:, :, osl], w=[r_wdn[buf]])
            for oc2 in range(2):
                oc = ob * 2 + oc2
                for tt in range(2):
                    pd, rd = ps_d[tt], r_psd[tt]
                    for ic in range(8):
                        k.op("tensor", lambda e2, p=pd,
                             w=wdn[buf][:, ic * 256 + oc2 * 128:ic * 256 + (oc2 + 1) * 128],
                             i=hT[:].rearrange("p (ic2 t) -> p ic2 t", ic2=8)[:, ic, tt * 512:(tt + 1) * 512],
                             st=(ic == 0), sp=(ic == 7):
                             e2.matmul(p[:], w, i, start=st, stop=sp, skip_group_check=True),
                             r=[r_wdn[buf], r_hT[ic * 2 + tt]],
                             w=([rd] if ic in (0, 7) else []))
                    asl = fsl(acc, oc, TOK, tt * 512, (tt + 1) * 512)
                    if e == 0:
                        k.op("vector", lambda v, o=asl, i=pd: v.tensor_copy(o, i[:]),
                             r=[rd], w=[r_acc[2 * oc + tt]])
                    else:
                        k.op("vector", lambda v, o=asl, i=pd: v.tensor_add(o, o, i[:]),
                             r=[rd, r_acc[2 * oc + tt]], w=[r_acc[2 * oc + tt]])
    for oc in range(NFC):
        xr = xres[oc % 2]
        k.dma("sync", xr[:], rearr_feat(x2_d)[:, oc, :], r=[r_x2d], w=[r_xres[oc % 2]])
        for tt in range(2):
            j = (2 * oc + tt) % 4
            k.op("vector", lambda v, o=xout[j], i=fsl(acc, oc, TOK, tt * 512, (tt + 1) * 512),
                 a=xr[:, tt * 512:(tt + 1) * 512]:
                 v.scalar_tensor_tensor(o[:], i, 0.5, a, ALU.mult, ALU.add),
                 r=[r_acc[2 * oc + tt], r_xres[oc % 2]], w=[r_xout[j]])
            k.dma("scalar", rearr_feat(xo_d)[:, oc, tt * 512:(tt + 1) * 512],
                  xout[j][:], r=[r_xout[j]], w=[r_xod])
    k.op("sync", lambda e: e.nop(), r=[r_xod])
    k.prefix = pfx0
    return k


def run_moe(xs, inputs, partials=None):
    k = build_moe(partials is not None)
    shared = dict(
        rw=np.ascontiguousarray(np.asarray(inputs["L1_ffn_router_w"], np.float32)),
        guw=np.ascontiguousarray(np.asarray(inputs["L1_ffn_gate_up_w"], np.float32)),
        dww=np.ascontiguousarray(np.asarray(inputs["L1_ffn_down_w"], np.float32)),
        sgw=np.ascontiguousarray(np.asarray(inputs["L1_ffn_shared_gate_w"], np.float32)),
        suw=np.ascontiguousarray(np.asarray(inputs["L1_ffn_shared_up_w"], np.float32)),
        sdw=np.ascontiguousarray(np.asarray(inputs["L1_ffn_shared_down_w"], np.float32)),
    )
    in_maps = []
    for c in range(8):
        m = dict(shared)
        m["x_t"] = xs[c]
        if partials is not None:
            m["pa"], m["pb"] = partials[0][c], partials[1][c]
        in_maps.append(m)
    res = k.run(in_maps)
    return [res.results[c]["xo"] for c in range(8)]



def build_attn1(nheads=6, nchunks=16):
    k = K()
    NH = nheads
    io = dict(
        xseq=k.din("xseq", [D, 2048], F32),
        wq=k.din("wq", [D, NH * 128], F32), wk=k.din("wk", [D, NH * 128], F32),
        wv=k.din("wv", [D, NH * 256], F32), wgg=k.din("wgg", [D, NH * 256], F32),
        wbb=k.din("wbb", [D, NH], F32), waa=k.din("waa", [D, NH], F32),
        wo=k.din("wo", [NH * 256, D], F32),
        cwq=k.din("cwq", [128, NH * 4], F32), cwk=k.din("cwk", [128, NH * 4], F32),
        cwv=k.din("cwv", [128, NH * 2 * 4], F32),
        aldt=k.din("aldt", [NH, 2], F32), normw=k.din("normw", [1, 256], F32),
        po=k.dout("po", [D, 2048], F32),
    )
    emit_attn1(k, io, "A1_", fused=False, nheads=nheads, nchunks=nchunks)
    return k


def emit_attn1(k, io, tag, fused, nheads=6, nchunks=16):
    pfx0 = k.prefix
    k.prefix = tag
    T = 2048
    NH = nheads
    DK, DV = 128, 256
    NQC = NH
    NVC = NH * 2
    CP = 2056
    PAD = 8
    x_d = io["xseq"] if not fused else io["x_in"]
    wq_d, wk_d, wv_d, wg_d = io["wq"], io["wk"], io["wv"], io["wgg"]
    wb_d, wa_d, wo_d = io["wbb"], io["waa"], io["wo"]
    cwq_d, cwk_d, cwv_d = io["cwq"], io["cwk"], io["cwv"]
    aldt_d, nw_d = io["aldt"], io["normw"]
    po_d = io["po"]
    gg_d = k.dint(tag + "ggscr", [T, NH * DV], BF16)
    onT_d = k.dint(tag + "onTscr", [NH * DV, T], BF16)

    C = Consts(k)
    ocbf = k.sb("ocbf1", [128, 1], BF16)
    orbf = k.sb("orbf1", [1, 128], BF16)
    i_bf = k.sb("i_bf1", [128, 128], BF16)
    i_f32 = k.sb("i_f321", [128, 128], F32)
    ltones = k.sb("ltones", [128, 128], F32)
    mstrict = k.sb("mstrict", [128, 128], F32)
    mincl = k.sb("mincl", [128, 128], F32)
    rC = Reg("a1const")

    def gop(fn):
        k.op("gpsimd", fn, w=[rC], small=True)
    gop(lambda g: g.memset(ocbf[:], 1.0))
    gop(lambda g: g.memset(orbf[:], 1.0))
    gop(lambda g: g.memset(i_bf[:], 0.0))
    gop(lambda g: g.affine_select(out=i_bf[:], in_=i_bf[:], compare_op=ALU.not_equal,
                                  fill=1.0, base=0, pattern=[[-1, 128]], channel_multiplier=1))
    gop(lambda g: g.memset(i_f32[:], 0.0))
    gop(lambda g: g.affine_select(out=i_f32[:], in_=i_f32[:], compare_op=ALU.not_equal,
                                  fill=1.0, base=0, pattern=[[-1, 128]], channel_multiplier=1))
    gop(lambda g: g.memset(ltones[:], 1.0))
    gop(lambda g: g.affine_select(out=ltones[:], in_=ltones[:], compare_op=ALU.is_ge,
                                  fill=0.0, base=0, pattern=[[1, 128]], channel_multiplier=-1))
    gop(lambda g: g.memset(mstrict[:], 0.0))
    gop(lambda g: g.affine_select(out=mstrict[:], in_=mstrict[:], compare_op=ALU.is_gt,
                                  fill=NEG, base=0, pattern=[[-1, 128]], channel_multiplier=1))
    gop(lambda g: g.memset(mincl[:], 0.0))
    gop(lambda g: g.affine_select(out=mincl[:], in_=mincl[:], compare_op=ALU.is_ge,
                                  fill=NEG, base=0, pattern=[[1, 128]], channel_multiplier=-1))

    qpre = k.sb("qpre", [128, NQC * CP], BF16)
    kpre = k.sb("kpre", [128, NQC * CP], BF16)
    vpre_d = k.dint(tag + "vprescr", [NVC, 128, CP], BF16)
    r_vpreD = Reg("vpreD")
    r_qpre = [Reg(f"qpre{i}") for i in range(NQC)]
    r_kpre = [Reg(f"kpre{i}") for i in range(NQC)]
    b_raw = k.sb("b_raw", [NH, T], BF16)
    a_raw = k.sb("a_raw", [NH, T], F32)
    g_rows = k.sb("g_rows", [NH, T], F32)
    beta_rows = b_raw
    r_braw, r_araw, r_grows = Reg(), Reg(), Reg()
    r_beta = r_braw
    aldt = k.sb("aldt_sb", [NH, 2], F32)
    negexpA = k.sb("negexpA", [NH, 1], F32)
    r_aldt = Reg("aldt")
    nwb = k.sb("nwb", [128, DV], F32)
    r_nwb = Reg("nwb")

    for c in range(NQC):
        k.op("gpsimd", lambda g, o=qpre[:, c * CP:c * CP + PAD]: g.memset(o, 0.0),
             w=[r_qpre[c]], small=True)
        k.op("gpsimd", lambda g, o=kpre[:, c * CP:c * CP + PAD]: g.memset(o, 0.0),
             w=[r_kpre[c]], small=True)
    zpad = k.sb("zpad1", [128, PAD], BF16)
    k.op("gpsimd", lambda g: g.memset(zpad[:], 0.0), w=[rC], small=True)
    for c in range(NVC):
        k.dma("sync", vpre_d[c][:, 0:PAD], zpad[:], r=[rC], w=[r_vpreD])
    k.dma("sync", aldt[:], aldt_d, w=[r_aldt])

    main = ExitStack()
    ln = k.sb("lnA1", [128, NFC * T], BF16, ctx=main)
    r_lnH = [Reg(f"ln{i}") for i in range(NFC * 2)]
    lnsc = ExitStack()
    xstg1 = k.sb("xstgA1", [128, NFC * TOK], BF16, ctx=lnsc)
    r_xstg1 = [Reg(f"xstg1_{i}") for i in range(NFC)]
    sqb = [k.sb(f"sqA1{i}", [128, TOK], BF16, ctx=lnsc) for i in range(1)]
    sqb = [sqb[0], sqb[0]]
    r_sqb0 = Reg("sq0")
    r_sqb = [r_sqb0, r_sqb0]
    srow = k.sb("rowA1", [1, 3 * TOK], F32, ctx=lnsc)
    P8 = [k.ps(f"P8_{i}", [128, 512], F32, ctx=lnsc) for i in range(8)]
    rP8 = [Reg(f"P8_{i}") for i in range(8)]
    wbh = [k.sb(f"wbh1{i}", [128, NFC * 128], BF16, ctx=lnsc) for i in range(2)]
    r_wbh = [Reg(f"wbh{i}") for i in range(2)]
    ps_p = [P8[0], P8[1]]
    r_psp = [rP8[0], rP8[1]]
    ps_ba = P8[2]
    r_psba = rP8[2]
    wgb = [k.sb(f"wgb1{i}", [128, NFC * 256], BF16, ctx=lnsc) for i in range(2)]
    r_wgb = [Reg(f"wgb{i}") for i in range(2)]
    ggst1 = k.sb("ggst0", [128, 256], BF16, ctx=lnsc)
    ggst = [ggst1, ggst1]
    r_ggst1 = Reg("ggst0")
    r_ggst = [r_ggst1, r_ggst1]
    r_ggd = Reg("ggd")
    mean_r, ex2_r = srow[:, 0:TOK], srow[:, TOK:2 * TOK]
    work_r = srow[:, 2 * TOK:3 * TOK]
    xsrc = rearr_feat(x_d)
    srcs = dict(q=wq_d.rearrange("(c p) o -> p c o", p=128),
                k2=wk_d.rearrange("(c p) o -> p c o", p=128),
                v=wv_d.rearrange("(c p) o -> p c o", p=128))
    wg_src = wg_d.rearrange("(c p) o -> p c o", p=128)
    gg_dst = gg_d.rearrange("(tc p) o -> p tc o", p=128)
    targets = [("q", qpre, r_qpre, NQC), ("k2", kpre, r_kpre, NQC), ("v", None, None, NVC)]
    NGS = (NH * DV) // 256
    wi = 0

    def emit_half_stats(xsrc2, tsl, apply_fn):
        r_mean, r_work, r_ex2 = Reg(), Reg(), Reg()
        for c in range(NFC):
            s, rs = xstg1[:, c * TOK:(c + 1) * TOK], r_xstg1[c]
            k.dma("gpsimd", s, xsrc2[:, c, tsl], w=[rs])
            b, rb = sqb[c % 2], r_sqb[c % 2]
            k.op("scalar", lambda e, o=b, i=s: e.activation(o[:], i, AF.Square),
                 r=[rs], w=[rb])
            for j in range(2):
                sl = slice(j * 512, (j + 1) * 512)
                k.op("tensor", lambda e, p=P8[0 + j][0:1, :], o=ocbf[:],
                     i=xstg1[:, c * TOK + sl.start:c * TOK + sl.stop],
                     st=(c == 0), sp=(c == NFC - 1):
                     e.matmul(p, o, i, start=st, stop=sp, skip_group_check=True),
                     r=[rs, rC], w=([rP8[0 + j]] if c in (0, NFC - 1) else []), small=True)
                k.op("tensor", lambda e, p=P8[2 + j][0:1, :], o=ocbf[:], i=b[:, sl],
                     st=(c == 0), sp=(c == NFC - 1):
                     e.matmul(p, o, i, start=st, stop=sp, skip_group_check=True),
                     r=[rb, rC], w=([rP8[2 + j]] if c in (0, NFC - 1) else []), small=True)
        for j in range(2):
            sl = slice(j * 512, (j + 1) * 512)
            k.op("vector", lambda v, o=mean_r[:, sl], i=P8[0 + j][0:1, :]:
                 v.tensor_scalar_mul(o, i, 1.0 / D), r=[rP8[0 + j]], w=[r_mean], small=True)
            k.op("vector", lambda v, o=ex2_r[:, sl], i=P8[2 + j][0:1, :]:
                 v.tensor_scalar_mul(o, i, 1.0 / D), r=[rP8[2 + j]], w=[r_ex2], small=True)
        k.op("vector", lambda v, o=work_r, a=mean_r: v.tensor_mul(o, a, a),
             r=[r_mean], w=[r_work], small=True)
        k.op("vector", lambda v, o=work_r, a=ex2_r: v.tensor_sub(o, a, o),
             r=[r_ex2, r_work], w=[r_work], small=True)
        k.op("vector", lambda v, o=work_r: v.tensor_scalar_add(o, o, 1e-5),
             r=[r_work], w=[r_work], small=True)
        k.op("scalar", lambda e, o=work_r, i=work_r: e.activation(o, i, AF.Ln),
             r=[r_work], w=[r_work], small=True)
        k.op("scalar", lambda e, o=work_r, i=work_r: e.activation(o, i, AF.Exp, scale=-0.5),
             r=[r_work], w=[r_work], small=True)
        k.op("vector", lambda v, o=mean_r, a=work_r: v.tensor_mul(o, o, a),
             r=[r_work, r_mean], w=[r_mean], small=True)
        for j in range(2):
            sl = slice(j * 512, (j + 1) * 512)
            k.op("tensor", lambda e, p=P8[4 + j][:], o=C.ones_row[:], i=work_r[:, sl]:
                 e.matmul(p, o, i, start=True, stop=True),
                 r=[r_work, C.reg], w=[rP8[4 + j]], small=True)
            k.op("tensor", lambda e, p=P8[6 + j][:], o=C.ones_row[:], i=mean_r[:, sl]:
                 e.matmul(p, o, i, start=True, stop=True),
                 r=[r_mean, C.reg], w=[rP8[6 + j]], small=True)
        for c in range(NFC):
            s, rs = xstg1[:, c * TOK:(c + 1) * TOK], r_xstg1[c]
            b, rb = sqb[c % 2], r_sqb[c % 2]
            for j in range(2):
                sl = slice(j * 512, (j + 1) * 512)
                k.op("vector", lambda v, o=b[:, sl],
                     i=xstg1[:, c * TOK + sl.start:c * TOK + sl.stop], b2=P8[4 + j]:
                     v.tensor_mul(o, i, b2[:]), r=[rs, rP8[4 + j]], w=[rb])
                apply_fn(c, sl, b, rb, j)

    def mk_apply_to_ln(half):
        def apply_to_ln(c, sl, b, rb, j):
            k.op("vector", lambda v,
                 o=ln[:, c * T + half * TOK + sl.start:c * T + half * TOK + sl.stop],
                 i=b[:, sl], b2=P8[6 + j]:
                 v.tensor_tensor(o, i, b2[:], ALU.subtract),
                 r=[rb, rP8[6 + j]], w=[r_lnH[2 * c + half]])
        return apply_to_ln

    if fused:
        lnst0 = kpre[:, 0:TOK]
        r_lnst0 = Reg("lnst0")
        ccl_dst = io["ccl_in"].rearrange("(c p) t -> p c t", p=128)
        r_ccl = Reg("ccl")

        def apply_to_cc(c, sl, b, rb, j):
            k.op("vector", lambda v, o=lnst0[:, sl.start:sl.stop], i=b[:, sl], b2=P8[6 + j]:
                 v.tensor_tensor(o, i, b2[:], ALU.subtract),
                 r=[rb, rP8[6 + j]], w=[r_lnst0])
            if j == 1:
                k.dma("sync", ccl_dst[:, c, :], lnst0, r=[r_lnst0], w=[r_ccl])
        emit_half_stats(rearr_feat(x_d), slice(0, TOK), apply_to_cc)
        r_cclo = Reg("cclo")
        k.op("gpsimd", lambda g, i=io["ccl_in"], o=io["ccl_out"]:
             g.collective_compute("AllGather", ALU.bypass, replica_groups=PAIRS,
                                  ins=[i.opt()], outs=[o.opt()]),
             r=[r_ccl], w=[r_cclo], small=True)
        glsrc = io["ccl_out"].rearrange("r (c p) t -> r p c t", p=128)

    vstg1 = sqb[0][:, 0:512]
    vstg = [vstg1, vstg1]
    r_vstg1 = Reg("vstg0")
    r_vstg = [r_vstg1, r_vstg1]
    for half in range(2):
        if fused:
            for c in range(NFC):
                k.dma("sync", ln[:, c * T + half * TOK:c * T + (half + 1) * TOK],
                      glsrc[half][:, c, :], r=[r_cclo], w=[r_lnH[2 * c + half]])
        else:
            emit_half_stats(rearr_feat(x_d), slice(half * TOK, (half + 1) * TOK),
                            mk_apply_to_ln(half))

    def lnr1(c):
        return [r_lnH[2 * c], r_lnH[2 * c + 1]]

    groups = [(name, dst, rdst, ch)
              for name, dst, rdst, nch in targets for ch in range(nch)]

    def emit_wdma(gi):
        name, _, _, ch = groups[gi]
        buf = gi % 2
        k.dma("gpsimd", wbh[buf][:].rearrange("p (c o) -> p c o", c=NFC),
              srcs[name][:, :, ch * 128:(ch + 1) * 128], w=[r_wbh[buf]])
    emit_wdma(0)
    for gi, (name, dst, rdst, ch) in enumerate(groups):
        buf = gi % 2
        if gi + 1 < len(groups):
            emit_wdma(gi + 1)
        for tt in range(4):
            pp, rp = ps_p[tt % 2], r_psp[tt % 2]
            for c in range(NFC):
                k.op("tensor", lambda e, p=pp, w=wbh[buf][:, c * 128:(c + 1) * 128],
                     i=ln[:, c * T + tt * 512:c * T + (tt + 1) * 512],
                     st=(c == 0), sp=(c == NFC - 1):
                     e.matmul(p[:], w, i, start=st, stop=sp, skip_group_check=True),
                     r=[r_wbh[buf]] + lnr1(c), w=([rp] if c in (0, NFC - 1) else []))
            if name == "v":
                vs, rvs = vstg[tt % 2], r_vstg[tt % 2]
                k.op("vector", lambda v, o=vs, i=pp: v.tensor_copy(o, i[:]),
                     r=[rp], w=[rvs, r_sqb0])
                k.dma("sync", vpre_d[ch][:, PAD + tt * 512:PAD + (tt + 1) * 512],
                      vs, r=[rvs], w=[r_vpreD])
            else:
                exw = [r_lnst0] if (fused and name == "k2" and ch == 0) else []
                k.op("vector", lambda v,
                     o=dst[:, ch * CP + PAD + tt * 512:ch * CP + PAD + (tt + 1) * 512],
                     i=pp: v.tensor_copy(o, i[:]), r=[rp], w=[rdst[ch]] + exw)
    for name, wsrc_d, dst, rdst in (("b", wb_d, b_raw, r_braw), ("a", wa_d, a_raw, r_araw)):
        buf = wi % 2
        wi += 1
        k.dma("gpsimd", wbh[buf][:, 0:NFC * NH].rearrange("p (c o) -> p c o", c=NFC),
              wsrc_d.rearrange("(c p) o -> p c o", p=128), w=[r_wbh[buf]])
        for tt in range(4):
            for c in range(NFC):
                k.op("tensor", lambda e, p=ps_ba[0:NH, :], w=wbh[buf][:, c * NH:(c + 1) * NH],
                     i=ln[:, c * T + tt * 512:c * T + (tt + 1) * 512],
                     st=(c == 0), sp=(c == NFC - 1):
                     e.matmul(p, w, i, start=st, stop=sp, skip_group_check=True),
                     r=[r_wbh[buf]] + lnr1(c), w=([r_psba] if c in (0, NFC - 1) else []))
            k.op("vector", lambda v, o=dst[:, tt * 512:(tt + 1) * 512], i=ps_ba[0:NH, :]:
                 v.tensor_copy(o, i), r=[r_psba], w=[rdst], small=True)
    for ns in range(NGS):
        buf = ns % 2
        k.dma("gpsimd", wgb[buf][:].rearrange("p (c o) -> p c o", c=NFC),
              wg_src[:, :, ns * 256:(ns + 1) * 256], w=[r_wgb[buf]])
        for tc in range(16):
            pp, rp = ps_p[tc % 2], r_psp[tc % 2]
            for c in range(NFC):
                k.op("tensor", lambda e, p=pp[:, 0:256],
                     w=ln[:, c * T + tc * 128:c * T + (tc + 1) * 128],
                     i=wgb[buf][:, c * 256:(c + 1) * 256],
                     st=(c == 0), sp=(c == NFC - 1):
                     e.matmul(p, w, i, start=st, stop=sp, skip_group_check=True),
                     r=[r_wgb[buf]] + lnr1(c), w=([rp] if c in (0, NFC - 1) else []))
            sg2 = tc % 2
            k.op("vector", lambda v, o=ggst[sg2], i=pp[:, 0:256]: v.tensor_copy(o[:], i),
                 r=[rp], w=[r_ggst[sg2]])
            k.dma("sync", gg_dst[:, tc, ns * 256:(ns + 1) * 256], ggst[sg2][:],
                  r=[r_ggst[sg2]], w=[r_ggd])
    lnsc.close()

    main.close()
    k.barrier()
    return _attn1_tail(k, locals())


def _attn1_tail(k, L):
    g = lambda n: L[n]
    NH, T, DK, DV, CP = g("NH"), g("T"), g("DK"), g("DV"), g("CP")
    PAD = g("PAD")
    NQC, NVC = g("NQC"), g("NVC")
    nchunks = g("nchunks")
    qpre, kpre = g("qpre"), g("kpre")
    vpre_d, r_vpreD = g("vpre_d"), g("r_vpreD")
    r_qpre, r_kpre = g("r_qpre"), g("r_kpre")
    b_raw, a_raw, g_rows, beta_rows = g("b_raw"), g("a_raw"), g("g_rows"), g("beta_rows")
    r_braw, r_araw, r_grows, r_beta = g("r_braw"), g("r_araw"), g("r_grows"), g("r_beta")
    aldt, negexpA, r_aldt = g("aldt"), g("negexpA"), g("r_aldt")
    nwb, r_nwb = g("nwb"), g("r_nwb")
    C, rC = g("C"), g("rC")
    ocbf, orbf, i_bf, i_f32 = g("ocbf"), g("orbf"), g("i_bf"), g("i_f32")
    ltones, mstrict, mincl = g("ltones"), g("mstrict"), g("mincl")
    cwq_d, cwk_d, cwv_d, nw_d = g("cwq_d"), g("cwk_d"), g("cwv_d"), g("nw_d")
    wo_d, po_d, gg_d, onT_d = g("wo_d"), g("po_d"), g("gg_d"), g("onT_d")

    k.op("scalar", lambda e, o=beta_rows[:], i=b_raw[:]: e.activation(o, i, AF.Sigmoid),
         r=[r_braw], w=[r_beta], small=True)
    k.op("scalar", lambda e, o=negexpA[:], i=aldt[:, 0:1]: e.activation(o[:], i, AF.Exp),
         r=[r_aldt], w=[r_aldt], small=True)
    k.op("vector", lambda v, o=negexpA[:]: v.tensor_scalar_mul(o[:], o[:], -1.0),
         r=[r_aldt], w=[r_aldt], small=True)
    k.op("vector", lambda v, o=a_raw[:], s=aldt[:, 1:2]: v.tensor_scalar_add(o, o, s),
         r=[r_araw, r_aldt], w=[r_araw], small=True)
    k.op("scalar", lambda e, o=g_rows[:], i=a_raw[:]: e.activation(o, i, AF.Abs),
         r=[r_araw], w=[r_grows], small=True)
    k.op("scalar", lambda e, o=g_rows[:], i=g_rows[:]: e.activation(o, i, AF.Exp, scale=-1.0),
         r=[r_grows], w=[r_grows], small=True)
    k.op("vector", lambda v, o=g_rows[:]: v.tensor_scalar_add(o, o, 1.0),
         r=[r_grows], w=[r_grows], small=True)
    k.op("scalar", lambda e, o=g_rows[:], i=g_rows[:]: e.activation(o, i, AF.Ln),
         r=[r_grows], w=[r_grows], small=True)
    k.op("vector", lambda v, o=a_raw[:], i=a_raw[:]: v.tensor_scalar_max(o, i, 0.0),
         r=[r_araw], w=[r_araw], small=True)
    k.op("vector", lambda v, o=g_rows[:], a=a_raw[:]: v.tensor_add(o, g_rows[:], a),
         r=[r_araw, r_grows], w=[r_grows], small=True)
    k.op("vector", lambda v, o=g_rows[:], s=negexpA[:, 0:1]: v.tensor_scalar_mul(o, g_rows[:], s),
         r=[r_grows, r_aldt], w=[r_grows], small=True)

    csc = ExitStack()
    cw = k.sb("cw1", [128, (2 * NQC + NVC) * 4], F32, ctx=csc)
    r_cw = Reg("cw")
    k.dma("sync", cw[:, 0:NQC * 4], cwq_d, w=[r_cw])
    k.dma("sync", cw[:, NQC * 4:2 * NQC * 4], cwk_d, w=[r_cw])
    k.dma("sync", cw[:, 2 * NQC * 4:], cwv_d, w=[r_cw])
    NDG = (2 * NQC + NVC) * 4
    dg = k.sb("dg1", [128, NDG * 128], BF16, ctx=csc)
    r_dg = Reg("dg")
    for i in range(NDG):
        k.op("vector", lambda v, o=dg[:, i * 128:(i + 1) * 128], a=i_bf[:],
             s=cw[:, i:i + 1]: v.tensor_scalar_mul(o, a, s),
             r=[r_cw, rC], w=[r_dg], small=True)

    qn = k.sb("qn1", [128, NH * 128], BF16, ctx=csc)
    kn = k.sb("kn1", [128, NH * 128], BF16, ctx=csc)
    vcv = k.sb("vcv1", [128, NH * 256], BF16, ctx=csc)
    r_qn, r_kn, r_vcv = Reg("qn"), Reg("kn"), Reg("vcv")
    ktok = k.sb("ktok1", [128, NH * 128], BF16, ctx=csc)
    vtok = k.sb("vtok1", [128, NH * 256], BF16, ctx=csc)
    r_ktok, r_vtok = Reg("ktok"), Reg("vtok")
    gcols = k.sb("gcols1", [128, NH], F32, ctx=csc)
    bcols = k.sb("bcols1", [128, NH], F32, ctx=csc)
    egc = k.sb("egc1", [128, 4 * NH], F32, ctx=csc)
    r_gcols, r_bcols, r_egc = Reg("gcols"), Reg("bcols"), Reg("egc")
    grows0 = k.sb("grows0", [1, 2 * NH * 128], F32, ctx=csc)
    r_grows0 = Reg("grows0")
    bcrow = k.sb("bcrow", [1, NH], F32, ctx=csc)
    r_bcrow = Reg("bcrow")
    dts = k.sb("dts1", [128, NH * 128], BF16, ctx=csc)
    das = k.sb("das1", [128, NH * 128], BF16, ctx=csc)
    r_dts, r_das = Reg("dts"), Reg("das")
    Asb = k.sb("Asb1", [128, NH * 128], BF16, ctx=csc)
    ATsb = k.sb("ATsb1", [128, NH * 128], BF16, ctx=csc)
    Pow = k.sb("Pow1", [128, NH * 128], BF16, ctx=csc)
    PowT = k.sb("PowT1", [128, NH * 128], BF16, ctx=csc)
    PT = k.sb("PT1", [128, NH * 128], BF16, ctx=csc)
    r_A, r_AT, r_Pow, r_PowT, r_PT = Reg("A"), Reg("AT"), Reg("Pow"), Reg("PowT"), Reg("PT")
    Wneg = k.sb("Wneg1", [128, NH * 128], BF16, ctx=csc)
    r_Wneg = Reg("Wneg")
    Kbp = k.sb("Kbp1", [128, NH * 128], BF16, ctx=csc)
    Kde = k.sb("Kde1", [128, NH * 128], BF16, ctx=csc)
    r_Kbp, r_Kde = Reg("Kbp"), Reg("Kde")
    Vb = k.sb("Vb1", [128, NH * 256], BF16, ctx=csc)
    r_Vb = Reg("Vb")
    Usb = k.sb("Usb1", [128, NH * 256], BF16, ctx=csc)
    r_U = Reg("U")
    Attn = k.sb("Attn1", [128, NH * 128], BF16, ctx=csc)
    r_Attn = Reg("Attn")
    Osb = k.sb("Osb1", [128, NH * 256], BF16, ctx=csc)
    r_O = Reg("O")
    S_sb = k.sb("S_sb1", [128, NH * 256], F32, ctx=csc)
    S_bf = k.sb("S_bf1", [128, NH * 256], BF16, ctx=csc)
    r_S, r_Sbf = Reg("S"), Reg("Sbf")
    onst = k.sb("onst1", [128, NH * 256], BF16, ctx=csc)
    r_onst = Reg("onst")
    rowsc = k.sb("rowsc1", [1, NH * 128], F32, ctx=csc)
    r_rowsc = Reg("rowsc")
    rstdc = k.sb("rstdc1", [128, 2 * NH], F32, ctx=csc)
    r_rstdc = Reg("rstdc")
    r_onTd = Reg("onTd")
    r_ggdr = L["r_ggd"]

    PA = [k.ps(f"PA1{i}", [128, 1024], F32, ctx=csc) for i in range(2)]
    r_PA = [Reg(f"PA{i}") for i in range(2)]
    PB = [k.ps(f"PB1{i}", [128, 512], F32, ctx=csc) for i in range(2)]
    r_PB = [Reg(f"PB{i}") for i in range(2)]
    PTr = k.ps("PTr1", [128, 512], BF16, ctx=csc)
    r_PTr = Reg("PTr")
    PRow = k.ps("PRow1", [1, 512], F32, ctx=csc)
    r_PRow = Reg("PRow")

    nwrow = k.sb("nwrow", [1, DV], F32, ctx=csc)
    r_nwrow = Reg("nwrow")
    k.dma("sync", nwrow[:], nw_d, w=[r_nwrow])
    k.op("tensor", lambda e, p=PB[0][:, 0:DV], o=C.ones_row[:], i=nwrow[:]:
         e.matmul(p, o, i, start=True, stop=True), r=[r_nwrow, C.reg], w=[r_PB[0]],
         small=True)
    k.op("vector", lambda v, o=nwb, i=PB[0][:, 0:DV]: v.tensor_copy(o[:], i),
         r=[r_PB[0]], w=[r_nwb], small=True)

    gg_src = gg_d.rearrange("(tc p) o -> p tc o", p=128)
    onT_dst = onT_d.rearrange("(vc p) t -> p vc t", p=128)
    VW = 132
    vwin = [k.sb(f"vwin{i}", [128, NVC * VW], BF16, ctx=csc) for i in range(2)]
    r_vwin = [Reg(f"vwin{i}") for i in range(2)]
    vp_src = vpre_d.rearrange("ch p t -> p ch t")

    def emit_vwin(j):
        base = PAD + j * 128 - 3
        k.op("scalar", lambda e,
             o=vwin[j % 2][:].rearrange("p (ch w) -> p ch w", w=VW)[:, :, 0:131],
             i=vp_src[:, :, base:base + 131]:
             e.dma_start(out=o, in_=i), r=[r_vpreD], w=[r_vwin[j % 2]], dma=True)

    wo_d2 = L["wo_d"]
    r_wo1 = Reg("wo1pf")
    wo_src1 = wo_d2.rearrange("(vc p) o -> p vc o", p=128)

    def emit_wo_prefetch():
        for vc in range(NH * 2):
            base, rb = (qpre, r_qpre[vc]) if vc < NH else (kpre, r_kpre[vc - NH])
            lvc = vc if vc < NH else vc - NH
            k.dma("gpsimd", base[:, lvc * 2048:(lvc + 1) * 2048]
                  .rearrange("p (a o) -> p a o", a=1),
                  wo_src1[:, vc:vc + 1, :], w=[r_wo1, rb])

    for tc in range(nchunks):
        tsl = slice(tc * 128, (tc + 1) * 128)
        if tc == 0:
            emit_vwin(0)
            emit_vwin(1)
        elif tc + 1 < nchunks:
            emit_vwin(tc + 1)
        k.dma("sync", onst[:], gg_src[:, tc, :], w=[r_onst])
        gidx = 0
        cbanks = [(PA[0], r_PA[0]), (PA[1], r_PA[1]), (PB[0], r_PB[0]), (PB[1], r_PB[1])]
        for (pre, rpre, dst, rdst, nch, cwoff) in (
                (qpre, r_qpre, qn, r_qn, NQC, 0),
                (kpre, r_kpre, kn, r_kn, NQC, NQC * 4),
                (None, None, vcv, r_vcv, NVC, 2 * NQC * 4)):
            for ch4 in range((nch + 3) // 4):
                pa, rpa = cbanks[gidx % 4]
                gidx += 1
                nsub = min(4, nch - ch4 * 4)
                for s4 in range(nsub):
                    ch = ch4 * 4 + s4
                    if pre is None:
                        iap = lambda tap, ch=ch: vwin[tc % 2][:, ch * VW + tap:ch * VW + tap + 128]
                        rdep = r_vwin[tc % 2]
                    else:
                        iap = lambda tap, pre=pre, ch=ch: pre[
                            :, ch * CP + PAD + tc * 128 - (3 - tap):
                            ch * CP + PAD + tc * 128 - (3 - tap) + 128]
                        rdep = rpre[ch]
                    for tap in range(4):
                        k.op("tensor", lambda e, p=pa[:, s4 * 128:(s4 + 1) * 128],
                             w=dg[:, (cwoff + ch * 4 + tap) * 128:(cwoff + ch * 4 + tap + 1) * 128],
                             i=iap(tap),
                             st=(tap == 0), sp=(tap == 3):
                             e.matmul(p, w, i, start=st, stop=sp, skip_group_check=True),
                             r=[r_dg, rdep], w=([rpa] if tap in (0, 3) else []))
                k.op("scalar", lambda e, o=dst[:, ch4 * 512:ch4 * 512 + nsub * 128],
                     i=pa[:, 0:nsub * 128]: e.activation(o, i[:], AF.Silu),
                     r=[rpa], w=[rdst, rpa])
        if tc == nchunks - 1:
            emit_wo_prefetch()
        for h in range(NH):
            sl4 = (h % 2) * 256
            for d2 in range(2):
                k.op("tensor", lambda e, p=PTr[:, sl4 + d2 * 128:sl4 + (d2 + 1) * 128],
                     i=vcv[:, h * 256 + d2 * 128:h * 256 + (d2 + 1) * 128], ii=i_bf[:]:
                     e.transpose(p, i, ii), r=[r_vcv, rC], w=[r_PTr])
            k.op("vector", lambda v, o=vtok[:, h * 256:(h + 1) * 256], i=PTr[:, sl4:sl4 + 256]:
                 v.tensor_copy(o, i), r=[r_PTr], w=[r_vtok])
        for (dst, rdst) in ((qn, r_qn), (kn, r_kn)):
            k.op("scalar", lambda e, o=das[:, 0:NH * 128], i=dst[:, 0:NH * 128]:
                 e.activation(o, i, AF.Square), r=[rdst], w=[r_das])
            for h in range(NH):
                sl4 = (h % 4) * 128
                k.op("tensor", lambda e, p=PRow[:, sl4:sl4 + 128], o=ocbf[:],
                     i=das[:, h * 128:(h + 1) * 128]:
                     e.matmul(p, o, i, start=True, stop=True, skip_group_check=True),
                     r=[r_das, rC], w=[r_PRow], small=True)
                k.op("vector", lambda v, o=rowsc[:, h * 128:(h + 1) * 128],
                     i=PRow[:, sl4:sl4 + 128]: v.tensor_copy(o, i), r=[r_PRow], w=[r_rowsc],
                     small=True)
            sl0 = slice(0, NH * 128)
            k.op("vector", lambda v, o=rowsc[:, sl0]: v.tensor_scalar_add(o, o, 1e-6),
                 r=[r_rowsc], w=[r_rowsc], small=True)
            k.op("scalar", lambda e, o=rowsc[:, sl0], i=rowsc[:, sl0]: e.activation(o, i, AF.Ln),
                 r=[r_rowsc], w=[r_rowsc], small=True)
            k.op("scalar", lambda e, o=rowsc[:, sl0], i=rowsc[:, sl0]:
                 e.activation(o, i, AF.Exp, scale=-0.5), r=[r_rowsc], w=[r_rowsc], small=True)
            for h in range(NH):
                pb, rpb = PB[h % 2], r_PB[h % 2]
                sl4 = (h // 2) * 128
                k.op("tensor", lambda e, p=pb[:, sl4:sl4 + 128], o=C.ones_row[:],
                     i=rowsc[:, h * 128:(h + 1) * 128]:
                     e.matmul(p, o, i, start=True, stop=True, skip_group_check=True),
                     r=[r_rowsc, C.reg], w=[rpb], small=True)
                k.op("vector", lambda v, o=dst[:, h * 128:(h + 1) * 128], i=pb[:, sl4:sl4 + 128]:
                     v.tensor_mul(o, o, i), r=[rpb, rdst], w=[rdst])
        for h in range(NH):
            sl4 = (h % 4) * 128
            k.op("tensor", lambda e, p=PTr[:, sl4:sl4 + 128],
                 i=kn[:, h * 128:(h + 1) * 128], ii=i_bf[:]:
                 e.transpose(p, i, ii), r=[r_kn, rC], w=[r_PTr])
            k.op("vector", lambda v, o=ktok[:, h * 128:(h + 1) * 128], i=PTr[:, sl4:sl4 + 128]:
                 v.tensor_copy(o, i), r=[r_PTr], w=[r_ktok])
        k.op("tensor", lambda e, p=PB[0][:, 0:NH], i=g_rows[:, tsl], ii=i_f32[0:NH, 0:NH]:
             e.transpose(p, i, ii), r=[r_grows, rC], w=[r_PB[0]], small=True)
        k.op("vector", lambda v, o=rstdc[:, NH:2 * NH], i=PB[0][:, 0:NH]:
             v.tensor_copy(o, i), r=[r_PB[0]], w=[r_rstdc], small=True)
        k.op("tensor", lambda e, p=PTr[:, 0:NH], i=beta_rows[:, tsl], ii=i_bf[0:NH, 0:NH]:
             e.transpose(p, i, ii), r=[r_beta, rC], w=[r_PTr], small=True)
        k.op("vector", lambda v, o=bcols, i=PTr[:, 0:NH]:
             v.tensor_copy(o[:], i), r=[r_PTr], w=[r_bcols], small=True)
        k.op("tensor", lambda e, p=PB[0][:, 0:NH], o=ltones[:], i=rstdc[:, NH:2 * NH]:
             e.matmul(p, o, i, start=True, stop=True, skip_group_check=True),
             r=[r_rstdc, rC], w=[r_PB[0]], small=True)
        k.op("vector", lambda v, o=gcols, i=PB[0][:, 0:NH]: v.tensor_copy(o[:], i),
             r=[r_PB[0]], w=[r_gcols], small=True)
        k.op("scalar", lambda e, o=egc[:, 0:NH], i=gcols[:]: e.activation(o, i[:], AF.Exp),
             r=[r_gcols], w=[r_egc], small=True)
        k.op("vector", lambda v, o=egc[:, 3 * NH:4 * NH], i=gcols[:]:
             v.tensor_scalar_mul(o, i[:], -1.0), r=[r_gcols], w=[r_egc], small=True)
        for h in range(NH):
            k.op("tensor", lambda e, p=PRow[:, 0:128], o=gcols[:, h:h + 1], i=i_f32[:]:
                 e.matmul(p, o, i, start=True, stop=True, skip_group_check=True),
                 r=[r_gcols, rC], w=[r_PRow], small=True)
            k.op("vector", lambda v, o=grows0[:, h * 128:(h + 1) * 128], i=PRow[:, 0:128]:
                 v.tensor_copy(o, i), r=[r_PRow], w=[r_grows0], small=True)
        k.op("vector", lambda v, o=grows0[:, NH * 128:2 * NH * 128], i=grows0[:, 0:NH * 128]:
             v.tensor_scalar_mul(o, i, -1.0), r=[r_grows0], w=[r_grows0], small=True)
        k.op("scalar", lambda e, o=bcrow[:],
             i=grows0[:, 0:NH * 128].rearrange("a (h t) -> a h t", h=NH)[:, :, 127]:
             e.activation(o[:], i, AF.Exp), r=[r_grows0], w=[r_bcrow], small=True)
        k.op("tensor", lambda e, p=PB[0][:, 0:NH], o=C.ones_row[:], i=bcrow[:]:
             e.matmul(p, o, i, start=True, stop=True, skip_group_check=True),
             r=[r_bcrow, C.reg], w=[r_PB[0]], small=True)
        k.op("vector", lambda v, o=egc[:, 2 * NH:3 * NH], i=PB[0][:, 0:NH]:
             v.tensor_copy(o, i), r=[r_PB[0]], w=[r_egc], small=True)
        k.op("tensor", lambda e, p=PB[1][:, 0:NH], o=C.ones_row[:],
             i=grows0[:, 0:NH * 128].rearrange("a (h t) -> a h t", h=NH)[:, :, 127]:
             e.matmul(p, o, i, start=True, stop=True, skip_group_check=True),
             r=[r_grows0, C.reg], w=[r_PB[1]], small=True)
        k.op("vector", lambda v, o=egc[:, NH:2 * NH], i=gcols[:], b2=PB[1][:, 0:NH]:
             v.tensor_sub(o, b2, i[:]), r=[r_PB[1], r_gcols], w=[r_egc], small=True)
        k.op("scalar", lambda e, o=egc[:, NH:2 * NH], i=egc[:, NH:2 * NH]:
             e.activation(o, i, AF.Exp), r=[r_egc], w=[r_egc], small=True)
        for (dst, rdst, mask, t_on_free) in ((dts, r_dts, mincl, True), (das, r_das, mstrict, False)):
            for h in range(NH):
                pa, rpa = PA[h % 2], r_PA[h % 2]
                sl4 = slice((h % 4) * 128, (h % 4 + 1) * 128) if False else slice(0, 128)
                if t_on_free:
                    k.op("tensor", lambda e, p=pa[:, 0:128],
                         o=grows0[:, NH * 128 + h * 128:NH * 128 + (h + 1) * 128],
                         i=C.ones_row[:].bitcast(F32) if False else None,
                         oo=C.ones_row[:]:
                         e.matmul(p, o, oo, start=True, stop=False, skip_group_check=True),
                         r=[r_grows0, C.reg], w=[rpa], small=True)
                    k.op("tensor", lambda e, p=pa[:, 0:128], o=C.ones_row[:],
                         i=grows0[:, h * 128:(h + 1) * 128]:
                         e.matmul(p, o, i, start=False, stop=False, skip_group_check=True),
                         r=[r_grows0, C.reg], w=[rpa], small=True)
                else:
                    k.op("tensor", lambda e, p=pa[:, 0:128],
                         o=grows0[:, h * 128:(h + 1) * 128], oo=C.ones_row[:]:
                         e.matmul(p, o, oo, start=True, stop=False, skip_group_check=True),
                         r=[r_grows0, C.reg], w=[rpa], small=True)
                    k.op("tensor", lambda e, p=pa[:, 0:128], o=C.ones_row[:],
                         i=grows0[:, NH * 128 + h * 128:NH * 128 + (h + 1) * 128]:
                         e.matmul(p, o, i, start=False, stop=False, skip_group_check=True),
                         r=[r_grows0, C.reg], w=[rpa], small=True)
                k.op("tensor", lambda e, p=pa[:, 0:128], o=i_f32[:], i=mask:
                     e.matmul(p, o, i[:], start=False, stop=True, skip_group_check=True),
                     r=[rC], w=[rpa], small=True)
                k.op("scalar", lambda e, o=dst[:, h * 128:(h + 1) * 128], i=pa[:, 0:128]:
                     e.activation(o, i, AF.Exp), r=[rpa], w=[rdst, rpa])
        for h in range(NH):
            pa, rpa = PA[0], r_PA[0]
            k.op("tensor", lambda e, p=pa[:, (h % 4) * 128:(h % 4 + 1) * 128] if False else pa[:, 0:128],
                 w=kn[:, h * 128:(h + 1) * 128], i=kn[:, h * 128:(h + 1) * 128]:
                 e.matmul(p, w, i, start=True, stop=True, skip_group_check=True),
                 r=[r_kn], w=[rpa])
            k.op("vector", lambda v, o=Asb[:, h * 128:(h + 1) * 128], i=pa[:, 0:128],
                 s1=bcols[:, h:h + 1], a=das[:, h * 128:(h + 1) * 128]:
                 v.scalar_tensor_tensor(o, i, s1, a, ALU.mult, ALU.mult),
                 r=[rpa, r_bcols, r_das], w=[r_A, rpa])
        k.op("vector", lambda v, o=Asb[:]: v.tensor_scalar_mul(o[:], o[:], -1.0),
             r=[r_A], w=[r_A])
        for h in range(NH):
            k.op("tensor", lambda e, p=PTr[:, 0:128], i=Asb[:, h * 128:(h + 1) * 128], ii=i_bf[:]:
                 e.transpose(p, i, ii), r=[r_A, rC], w=[r_PTr])
            k.op("vector", lambda v, o=ATsb[:, h * 128:(h + 1) * 128], i=PTr[:, 0:128]:
                 v.tensor_copy(o, i), r=[r_PTr], w=[r_AT])
        k.op("vector", lambda v, o=PT[:], i=ATsb[:],
             b2=i_bf[:].unsqueeze(1).broadcast_to([128, NH, 128]):
             v.tensor_tensor(o[:].rearrange("p (h t) -> p h t", h=NH), i[:].rearrange("p (h t) -> p h t", h=NH), b2, ALU.add),
             r=[r_AT, rC], w=[r_PT])
        k.op("vector", lambda v, o=Pow[:], i=Asb[:]: v.tensor_copy(o[:], i[:]),
             r=[r_A], w=[r_Pow])
        k.op("vector", lambda v, o=PowT[:], i=ATsb[:]: v.tensor_copy(o[:], i[:]),
             r=[r_AT], w=[r_PowT])
        for rnd in range(7):
            pa, rpa = PA[0], r_PA[0]
            pb2, rpb2 = PA[1], r_PA[1]
            if rnd >= 1:
                for h in range(NH):
                    pb, rpb = PB[h % 2], r_PB[h % 2]
                    k.op("tensor", lambda e, p=pb[:, (h // 2) * 128:(h // 2) * 128 + 128],
                         w=Pow[:, h * 128:(h + 1) * 128], i=PT[:, h * 128:(h + 1) * 128]:
                         e.matmul(p, w, i, start=True, stop=True, skip_group_check=True),
                         r=[r_Pow, r_PT], w=[rpb])
            if rnd < 6:
                for h in range(NH):
                    k.op("tensor", lambda e, p=pa[:, (h % 8) * 128:(h % 8) * 128 + 128],
                         w=PowT[:, h * 128:(h + 1) * 128], i=Pow[:, h * 128:(h + 1) * 128]:
                         e.matmul(p, w, i, start=True, stop=True, skip_group_check=True),
                         r=[r_Pow, r_PowT], w=[rpa])
            if rnd < 5:
                for h in range(NH):
                    k.op("tensor", lambda e, p=pb2[:, (h % 8) * 128:(h % 8) * 128 + 128],
                         w=Pow[:, h * 128:(h + 1) * 128], i=PowT[:, h * 128:(h + 1) * 128]:
                         e.matmul(p, w, i, start=True, stop=True, skip_group_check=True),
                         r=[r_Pow, r_PowT], w=[rpb2])
            if rnd >= 1:
                for h in range(NH):
                    pb, rpb = PB[h % 2], r_PB[h % 2]
                    k.op("vector", lambda v, o=PT[:, h * 128:(h + 1) * 128],
                         a=PT[:, h * 128:(h + 1) * 128],
                         i=pb[:, (h // 2) * 128:(h // 2) * 128 + 128]:
                         v.tensor_add(o, a, i), r=[rpb, r_PT], w=[r_PT, rpb], small=True)
            if rnd < 6:
                k.op("scalar", lambda e, o=Pow[:, 0:NH * 128], i=pa[:, 0:NH * 128]:
                     e.activation(o, i, AF.Copy), r=[rpa], w=[r_Pow, rpa])
            if rnd < 5:
                k.op("vector", lambda v, o=PowT[:, 0:NH * 128], i=pb2[:, 0:NH * 128]:
                     v.tensor_copy(o, i), r=[rpb2], w=[r_PowT, rpb2])
        k.op("vector", lambda v, o=rstdc[:, NH:2 * NH], i=egc[:, 0:NH], a=bcols[:]:
             v.tensor_mul(o, i, a[:]), r=[r_egc, r_bcols], w=[r_rstdc], small=True)
        for h in range(NH):
            k.op("vector", lambda v, o=Kbp[:, h * 128:(h + 1) * 128],
                 i=ktok[:, h * 128:(h + 1) * 128], s1=rstdc[:, NH + h:NH + h + 1]:
                 v.tensor_scalar_mul(o, i, s1), r=[r_ktok, r_rstdc], w=[r_Kbp])
            k.op("vector", lambda v, o=Kde[:, h * 128:(h + 1) * 128],
                 i=ktok[:, h * 128:(h + 1) * 128], s1=egc[:, NH + h:NH + h + 1]:
                 v.tensor_scalar_mul(o, i, s1), r=[r_ktok, r_egc], w=[r_Kde])
        for h in range(NH):
            pa, rpa = PA[0], r_PA[0]
            k.op("tensor", lambda e, p=pa[:, (h % 8) * 128:(h % 8) * 128 + 128],
                 w=Kbp[:, h * 128:(h + 1) * 128], i=PT[:, h * 128:(h + 1) * 128]:
                 e.matmul(p, w, i, start=True, stop=True, skip_group_check=True),
                 r=[r_Kbp, r_PT], w=[rpa])
        k.op("vector", lambda v, o=Wneg[:, 0:NH * 128], i=PA[0][:, 0:NH * 128]:
             v.tensor_scalar_mul(o, i, -1.0), r=[r_PA[0]], w=[r_Wneg, r_PA[0]])
        for h in range(NH):
            k.op("vector", lambda v, o=Vb[:, h * 256:(h + 1) * 256],
                 i=vtok[:, h * 256:(h + 1) * 256], s1=bcols[:, h:h + 1]:
                 v.tensor_scalar_mul(o, i, s1), r=[r_vtok, r_bcols], w=[r_Vb])
        for h in range(NH):
            pa, rpa = PA[h // 4], r_PA[h // 4]
            psl = pa[:, (h % 4) * 256:(h % 4) * 256 + 256]
            k.op("tensor", lambda e, p=psl, w=PT[:, h * 128:(h + 1) * 128],
                 i=Vb[:, h * 256:(h + 1) * 256], sp=(tc == 0):
                 e.matmul(p, w, i, start=True, stop=sp, skip_group_check=True),
                 r=[r_PT, r_Vb], w=[rpa])
            if tc > 0:
                k.op("tensor", lambda e, p=psl, w=Wneg[:, h * 128:(h + 1) * 128],
                     i=S_bf[:, h * 256:(h + 1) * 256]:
                     e.matmul(p, w, i, start=False, stop=True, skip_group_check=True),
                     r=[r_Wneg, r_Sbf], w=[rpa])
        for j in range(2):
            k.op("vector", lambda v, o=Usb[:, j * 1024:min((j + 1) * 1024, NH * 256)],
                 i=PA[j][:, 0:min(1024, NH * 256 - j * 1024)]:
                 v.tensor_copy(o, i), r=[r_PA[j]], w=[r_U, r_PA[j]])
        for h in range(NH):
            pa, rpa = PA[0], r_PA[0]
            k.op("tensor", lambda e, p=pa[:, (h % 8) * 128:(h % 8) * 128 + 128],
                 w=kn[:, h * 128:(h + 1) * 128], i=qn[:, h * 128:(h + 1) * 128]:
                 e.matmul(p, w, i, start=True, stop=True, skip_group_check=True),
                 r=[r_kn, r_qn], w=[rpa])
        k.op("vector", lambda v, o=Attn[:, 0:NH * 128], i=PA[0][:, 0:NH * 128], a=dts[:, 0:NH * 128]:
             v.tensor_tensor(o, i, a, ALU.mult), r=[r_PA[0], r_dts], w=[r_Attn, r_PA[0]])
        for h in range(NH):
            pa, rpa = PA[h // 4], r_PA[h // 4]
            psl = pa[:, (h % 4) * 256:(h % 4) * 256 + 256]
            k.op("tensor", lambda e, p=psl, w=Attn[:, h * 128:(h + 1) * 128],
                 i=Usb[:, h * 256:(h + 1) * 256]:
                 e.matmul(p, w, i, start=True, stop=True, skip_group_check=True),
                 r=[r_Attn, r_U], w=[rpa])
        for j in range(2):
            k.op("vector", lambda v, o=Osb[:, j * 1024:min((j + 1) * 1024, NH * 256)],
                 i=PA[j][:, 0:min(1024, NH * 256 - j * 1024)]:
                 v.tensor_copy(o, i), r=[r_PA[j]], w=[r_O, r_PA[j]])
        if tc > 0:
            for h in range(NH):
                pb2, rpb2 = PB[h % 2], r_PB[h % 2]
                k.op("tensor", lambda e, p=pb2[:, 0:256], w=qn[:, h * 128:(h + 1) * 128],
                     i=S_bf[:, h * 256:(h + 1) * 256]:
                     e.matmul(p, w, i, start=True, stop=True, skip_group_check=True),
                     r=[r_qn, r_Sbf], w=[rpb2])
                k.op("vector", lambda v, o=Osb[:, h * 256:(h + 1) * 256], i=pb2[:, 0:256],
                     s1=egc[:, h:h + 1], a=None:
                     v.scalar_tensor_tensor(o, i, s1, o, ALU.mult, ALU.add),
                     r=[rpb2, r_egc, r_O], w=[r_O, rpb2])
        for h in range(NH):
            pa, rpa = PA[h // 4], r_PA[h // 4]
            psl = pa[:, (h % 4) * 256:(h % 4) * 256 + 256]
            k.op("tensor", lambda e, p=psl, w=Kde[:, h * 128:(h + 1) * 128],
                 i=Usb[:, h * 256:(h + 1) * 256]:
                 e.matmul(p, w, i, start=True, stop=True, skip_group_check=True),
                 r=[r_Kde, r_U], w=[rpa])
        if tc == 0:
            for j in range(2):
                k.op("vector", lambda v, o=S_sb[:, j * 1024:min((j + 1) * 1024, NH * 256)],
                     i=PA[j][:, 0:min(1024, NH * 256 - j * 1024)]:
                     v.tensor_copy(o, i), r=[r_PA[j]], w=[r_S, r_PA[j]])
        else:
            for h in range(NH):
                pa, rpa = PA[h // 4], r_PA[h // 4]
                psl = pa[:, (h % 4) * 256:(h % 4) * 256 + 256]
                k.op("vector", lambda v, o=S_sb[:, h * 256:(h + 1) * 256], i2=psl,
                     s1=egc[:, 2 * NH + h:2 * NH + h + 1]:
                     v.scalar_tensor_tensor(o, o, s1, i2, ALU.mult, ALU.add),
                     r=[rpa, r_egc, r_S], w=[r_S, rpa])
        if tc < nchunks - 1:
            k.op("vector", lambda v, o=S_bf[:], i=S_sb[:]: v.tensor_copy(o[:], i[:]),
                 r=[r_S], w=[r_Sbf])
        for h in range(NH):
            k.op("scalar", lambda e, o=dts[:, h * 128:(h + 1) * 128],
                 i=Osb[:, h * 256:h * 256 + 128]: e.activation(o, i, AF.Square),
                 r=[r_O], w=[r_dts])
            k.op("scalar", lambda e, o=das[:, h * 128:(h + 1) * 128],
                 i=Osb[:, h * 256 + 128:(h + 1) * 256]: e.activation(o, i, AF.Square),
                 r=[r_O], w=[r_das])
        k.op("vector", lambda v, o=dts[:, 0:NH * 128], a=das[:, 0:NH * 128]:
             v.tensor_add(o, dts[:, 0:NH * 128], a), r=[r_dts, r_das], w=[r_dts])
        for h in range(NH):
            k.op("vector", lambda v, o=rstdc[:, h:h + 1], i=dts[:, h * 128:(h + 1) * 128]:
                 v.reduce_sum(o, i, axis=AX.X), r=[r_dts], w=[r_rstdc], small=True)
        k.op("vector", lambda v, o=rstdc[:, 0:NH]: v.tensor_scalar_mul(o, o, 1.0 / DV),
             r=[r_rstdc], w=[r_rstdc], small=True)
        k.op("vector", lambda v, o=rstdc[:, 0:NH]: v.tensor_scalar_add(o, o, 1e-5),
             r=[r_rstdc], w=[r_rstdc], small=True)
        k.op("scalar", lambda e, o=rstdc[:, 0:NH], i=rstdc[:, 0:NH]: e.activation(o, i, AF.Ln),
             r=[r_rstdc], w=[r_rstdc], small=True)
        k.op("scalar", lambda e, o=rstdc[:, 0:NH], i=rstdc[:, 0:NH]:
             e.activation(o, i, AF.Exp, scale=-0.5), r=[r_rstdc], w=[r_rstdc], small=True)
        k.op("scalar", lambda e, o=onst[:, 0:NH * 256], i=onst[:, 0:NH * 256]:
             e.activation(o, i, AF.Silu), r=[r_onst], w=[r_onst])
        k.op("vector", lambda v, o=onst[:].rearrange("p (h d2) -> p h d2", h=NH),
             i=onst[:].rearrange("p (h d2) -> p h d2", h=NH),
             b2=nwb[:].unsqueeze(1).broadcast_to([128, NH, DV]):
             v.tensor_tensor(o, i, b2, ALU.mult), r=[r_onst, r_nwb], w=[r_onst])
        for h in range(NH):
            k.op("vector", lambda v, o=onst[:, h * 256:(h + 1) * 256],
                 i=Osb[:, h * 256:(h + 1) * 256], s1=rstdc[:, h:h + 1],
                 a=onst[:, h * 256:(h + 1) * 256]:
                 v.scalar_tensor_tensor(o, i, s1, a, ALU.mult, ALU.mult),
                 r=[r_O, r_rstdc, r_onst], w=[r_onst])
        for h2 in range(NH * 2):
            k.op("sync", lambda e, o=Usb[:, h2 * 128:(h2 + 1) * 128],
                 i=onst[:, h2 * 128:(h2 + 1) * 128]:
                 e.dma_start_transpose(out=o, in_=i),
                 r=[r_onst], w=[r_U], dma=True)
        k.dma("sync", onT_dst[:, :, tsl],
              Usb[:].rearrange("p (vc t) -> p vc t", vc=NH * 2), r=[r_U], w=[r_onTd])
    L["r_wo1"] = r_wo1
    csc.close()
    k.barrier()
    return _attn1_outproj(k, L)


def _attn1_outproj(k, L):
    g = lambda n: L[n]
    NH, T = g("NH"), g("T")
    wo_d, po_d, onT_d = g("wo_d"), g("po_d"), g("onT_d")
    fused, io = g("fused"), g("io")
    osc = ExitStack()
    qpre, kpre = g("qpre"), g("kpre")
    r_wo = g("r_wo1")

    def wov(vc, oc):
        base = qpre if vc < NH else kpre
        lvc = vc if vc < NH else vc - NH
        return base[:, lvc * 2048 + oc * 128:lvc * 2048 + (oc + 1) * 128]
    onb = [k.sb(f"onb1{i}", [128, (NH * 2) * 512], BF16, ctx=osc) for i in range(2)]
    r_onb = [Reg(f"onb{i}") for i in range(2)]
    postg = [k.sb(f"postg1{i}", [128, 512], BF16 if fused else F32, ctx=osc) for i in range(2)]
    r_postg = [Reg(f"postg{i}") for i in range(2)]
    r_pod = Reg("po_dram")
    ps_o = [k.ps(f"psO1{i}", [128, 512], F32, ctx=osc) for i in range(2)]
    r_pso = [Reg(f"psO{i}") for i in range(2)]
    onsrc = onT_d.rearrange("(vc p) t -> p vc t", p=128)
    podst = rearr_feat(po_d)
    for tt in range(4):
        bo = tt % 2
        k.dma("sync", onb[bo][:].rearrange("p (vc t) -> p vc t", vc=NH * 2),
              onsrc[:, :, tt * 512:(tt + 1) * 512], w=[r_onb[bo]])
        for oc in range(NFC):
            pp, rp = ps_o[oc % 2], r_pso[oc % 2]
            for vc in range(NH * 2):
                k.op("tensor", lambda e, p=pp, w=wov(vc, oc),
                     i=onb[bo][:, vc * 512:(vc + 1) * 512],
                     st=(vc == 0), sp=(vc == NH * 2 - 1):
                     e.matmul(p[:], w, i, start=st, stop=sp, skip_group_check=True),
                     r=[r_wo, r_onb[bo]], w=([rp] if vc in (0, NH * 2 - 1) else []))
            g2 = oc % 2
            k.op("vector", lambda v, o=postg[g2], i=pp: v.tensor_copy(o[:], i[:]),
                 r=[rp], w=[r_postg[g2]])
            k.dma("sync", podst[:, oc, tt * 512:(tt + 1) * 512], postg[g2][:],
                  r=[r_postg[g2]], w=[r_pod])
    if fused:
        r_gpo = Reg("gpo")
        k.op("gpsimd", lambda g2, i=po_d, o=io["Gpo"]:
             g2.collective_compute("AllGather", ALU.bypass, replica_groups=PAIRS,
                                   ins=[i.opt()], outs=[o.opt()]),
             r=[r_pod], w=[r_gpo], small=True)
    k.op("sync", lambda e: e.nop(), r=[r_pod])
    osc.close()
    k.prefix = L.get("pfx0", "")
    return k


def build_attn1_full(nheads=6, nchunks=16):
    k = build_attn1(nheads=nheads, nchunks=nchunks)
    return k


def run_attn1(xs, inputs):
    wq = np.asarray(inputs["L1_attn_Wq"], np.float32)
    wk = np.asarray(inputs["L1_attn_Wk"], np.float32)
    wv = np.asarray(inputs["L1_attn_Wv"], np.float32)
    wb = np.asarray(inputs["L1_attn_Wb"], np.float32)
    wa = np.asarray(inputs["L1_attn_Wa"], np.float32)
    wg = np.asarray(inputs["L1_attn_Wg"], np.float32)
    wo = np.asarray(inputs["L1_attn_Wo"], np.float32)
    cwq = np.asarray(inputs["L1_attn_conv_wq"], np.float32)
    cwk = np.asarray(inputs["L1_attn_conv_wk"], np.float32)
    cwv = np.asarray(inputs["L1_attn_conv_wv"], np.float32)
    alog = np.asarray(inputs["L1_attn_A_log"], np.float32)
    dtb = np.asarray(inputs["L1_attn_dt_bias"], np.float32)
    nw = np.asarray(inputs["L1_attn_norm_w"], np.float32)
    k = build_attn1_full()
    in_maps = []
    for c in range(8):
        s, hg = c // 2, c % 2
        xseq = np.ascontiguousarray(np.concatenate([xs[2 * s], xs[2 * s + 1]], axis=1))
        hq = slice(hg * 6 * 128, (hg + 1) * 6 * 128)
        hv = slice(hg * 6 * 256, (hg + 1) * 6 * 256)
        hh = slice(hg * 6, (hg + 1) * 6)
        in_maps.append(dict(
            xseq=xseq,
            wq=np.ascontiguousarray(wq[:, hq]),
            wk=np.ascontiguousarray(wk[:, hq]),
            wv=np.ascontiguousarray(wv[:, hv]),
            wgg=np.ascontiguousarray(wg[:, hv]),
            wbb=np.ascontiguousarray(wb[:, hh]),
            waa=np.ascontiguousarray(wa[:, hh]),
            wo=np.ascontiguousarray(wo[hv, :]),
            cwq=np.ascontiguousarray(np.transpose(cwq[hq].reshape(6, 128, 4), (1, 0, 2)).reshape(128, 24)),
            cwk=np.ascontiguousarray(np.transpose(cwk[hq].reshape(6, 128, 4), (1, 0, 2)).reshape(128, 24)),
            cwv=np.ascontiguousarray(np.transpose(cwv[hv].reshape(12, 128, 4), (1, 0, 2)).reshape(128, 48)),
            aldt=np.ascontiguousarray(np.stack([alog[hh], dtb[hh]], axis=1)),
            normw=np.ascontiguousarray(nw.reshape(1, 256)),
        ))
    res = k.run(in_maps)
    pos = [np.asarray(res.results[c]["po"]) for c in range(8)]
    pas, pbs = [], []
    for c in range(8):
        s, h = c // 2, c % 2
        tsl = slice(h * 1024, (h + 1) * 1024)
        pas.append(np.ascontiguousarray(pos[2 * s][:, tsl]))
        pbs.append(np.ascontiguousarray(pos[2 * s + 1][:, tsl]))
    return pas, pbs



def build_fused():
    k = K()
    C = Consts(k)
    x0 = k.din("x_t", [D, TOK], F32)
    msel = k.din("msel", [128, 2], F32)
    io_ffn = dict(
        wg=k.din("wg", [D, 8192], I8), wu=k.din("wu", [D, 8192], I8),
        wdn=k.din("wdn", [8192, D], I8),
        sg2=k.din("sg2", [128, 64], F32), su2=k.din("su2", [128, 64], F32),
        sd2=k.din("sd2", [128, 16], F32),
    )
    io_a0 = dict(
        wqt=k.din("wqt", [D, 1024], F32), wkt=k.din("wkt", [D, 1024], F32),
        wvt=k.din("wvt", [D, 1024], F32), wot=k.din("wot", [1024, D], F32),
        gq=k.din("gq", [128, 1], F32), gk=k.din("gk", [128, 1], F32),
    )
    io_moe = dict(
        rw=k.din("rw", [D, 8], F32), guw=k.din("guw", [8, D, 2048], F32),
        dww=k.din("dww", [8, 1024, D], F32), sgw=k.din("sgw", [D, 1024], F32),
        suw=k.din("suw", [D, 1024], F32), sdw=k.din("sdw", [1024, D], F32),
    )
    io_a1 = dict(
        wq=k.din("a1wq", [D, 768], F32), wk=k.din("a1wk", [D, 768], F32),
        wv=k.din("a1wv", [D, 1536], F32), wgg=k.din("a1wg", [D, 1536], F32),
        wbb=k.din("a1wb", [D, 6], F32), waa=k.din("a1wa", [D, 6], F32),
        wo=k.din("a1wo", [1536, D], F32),
        cwq=k.din("a1cwq", [128, 24], F32), cwk=k.din("a1cwk", [128, 24], F32),
        cwv=k.din("a1cwv", [128, 48], F32),
        aldt=k.din("a1aldt", [6, 2], F32), normw=k.din("a1nw", [1, 256], F32),
    )
    out_d = k.dout("xo", [D, TOK], F32)
    xs1 = k.dint("xs1", [D, TOK], F32)
    xs2 = k.dint("xs2", [D, TOK], F32)
    xs3 = k.dint("xs3", [D, TOK], F32)
    x2a = k.dint("x2a", [D, TOK], F32)
    x2b = k.dint("x2b", [D, TOK], F32)
    ccl0i = k.dint("ccl0i", [D, TOK], BF16)
    ccl0o = k.dint("ccl0o", [2, D, TOK], BF16)
    ccl1i = k.dint("ccl1i", [D, TOK], BF16)
    ccl1o = k.dint("ccl1o", [2, D, TOK], BF16)
    po0 = k.dint("po0", [D, 2048], BF16)
    Gpo0 = k.dint("Gpo0", [2, D, 2048], BF16)
    po1 = k.dint("po1", [D, 2048], BF16)
    Gpo1 = k.dint("Gpo1", [2, D, 2048], BF16)

    def phase(fn):
        sc = ExitStack()
        k.scope = sc
        fn()
        sc.close()
        k.scope = None
        k.barrier()

    phase(lambda: emit_ffn0(k, C, dict(io_ffn, x_in=x0, x_out=xs1), "F0a_", False))
    phase(lambda: emit_attn0(k, dict(io_a0, x_in=xs1, ccl_in=ccl0i, ccl_out=ccl0o,
                                     po=po0, Gpo=Gpo0), "A0_", fused=True))
    phase(lambda: emit_ffn0(k, C, dict(io_ffn, x_in=xs1, x_out=xs2, Gpo=Gpo0,
                                       msel=msel, x2=x2a), "F0b_", True))
    phase(lambda: emit_moe(k, C, dict(io_moe, x_in=xs2, x_out=xs3), "M0a_", False))
    phase(lambda: emit_attn1(k, dict(io_a1, x_in=xs3, ccl_in=ccl1i, ccl_out=ccl1o,
                                     po=po1, Gpo=Gpo1), "A1_", fused=True))
    phase(lambda: emit_moe(k, C, dict(io_moe, x_in=xs3, x_out=out_d, Gpo=Gpo1,
                                      msel=msel, x2=x2b), "M0b_", True))
    return k


def _fused_in_maps(inputs):
    xs = _shard_x(np.asarray(inputs["L0_attn_x"], np.float32))
    shared = dict(
        wg=np.asarray(inputs["L0_ffn_w_gate"]),
        wu=np.asarray(inputs["L0_ffn_w_up"]),
        wdn=np.asarray(inputs["L0_ffn_w_down"]),
        sg2=_scale2d(inputs["L0_ffn_s_gate"], 64),
        su2=_scale2d(inputs["L0_ffn_s_up"], 64),
        sd2=_scale2d(inputs["L0_ffn_s_down"], 16),
        rw=np.ascontiguousarray(np.asarray(inputs["L1_ffn_router_w"], np.float32)),
        guw=np.ascontiguousarray(np.asarray(inputs["L1_ffn_gate_up_w"], np.float32)),
        dww=np.ascontiguousarray(np.asarray(inputs["L1_ffn_down_w"], np.float32)),
        sgw=np.ascontiguousarray(np.asarray(inputs["L1_ffn_shared_gate_w"], np.float32)),
        suw=np.ascontiguousarray(np.asarray(inputs["L1_ffn_shared_up_w"], np.float32)),
        sdw=np.ascontiguousarray(np.asarray(inputs["L1_ffn_shared_down_w"], np.float32)),
    )
    wqt = np.ascontiguousarray(np.asarray(inputs["L0_attn_Wq"], np.float32).T)
    wkt = np.ascontiguousarray(np.asarray(inputs["L0_attn_Wk"], np.float32).T)
    wvt = np.ascontiguousarray(np.asarray(inputs["L0_attn_Wv"], np.float32).T)
    wot = np.ascontiguousarray(np.asarray(inputs["L0_attn_Wo"], np.float32).T)
    gq = np.ascontiguousarray(np.asarray(inputs["L0_attn_gq"], np.float32).reshape(128, 1))
    gk = np.ascontiguousarray(np.asarray(inputs["L0_attn_gk"], np.float32).reshape(128, 1))
    wq1 = np.asarray(inputs["L1_attn_Wq"], np.float32)
    wk1 = np.asarray(inputs["L1_attn_Wk"], np.float32)
    wv1 = np.asarray(inputs["L1_attn_Wv"], np.float32)
    wb1 = np.asarray(inputs["L1_attn_Wb"], np.float32)
    wa1 = np.asarray(inputs["L1_attn_Wa"], np.float32)
    wg1 = np.asarray(inputs["L1_attn_Wg"], np.float32)
    wo1 = np.asarray(inputs["L1_attn_Wo"], np.float32)
    cwq = np.asarray(inputs["L1_attn_conv_wq"], np.float32)
    cwk = np.asarray(inputs["L1_attn_conv_wk"], np.float32)
    cwv = np.asarray(inputs["L1_attn_conv_wv"], np.float32)
    alog = np.asarray(inputs["L1_attn_A_log"], np.float32)
    dtb = np.asarray(inputs["L1_attn_dt_bias"], np.float32)
    nw = np.asarray(inputs["L1_attn_norm_w"], np.float32)
    in_maps = []
    for c in range(8):
        hg = c % 2
        cols0 = slice(hg * 1024, (hg + 1) * 1024)
        hq = slice(hg * 768, (hg + 1) * 768)
        hv = slice(hg * 1536, (hg + 1) * 1536)
        hh = slice(hg * 6, (hg + 1) * 6)
        m = dict(shared)
        m.update(
            x_t=xs[c],
            msel=np.ascontiguousarray(np.concatenate(
                [np.full((128, 1), 1.0 - hg, np.float32),
                 np.full((128, 1), float(hg), np.float32)], axis=1)),
            wqt=np.ascontiguousarray(wqt[:, cols0]),
            wkt=np.ascontiguousarray(wkt[:, cols0]),
            wvt=np.ascontiguousarray(wvt[:, cols0]),
            wot=np.ascontiguousarray(wot[cols0, :]),
            gq=gq, gk=gk,
            a1wq=np.ascontiguousarray(wq1[:, hq]),
            a1wk=np.ascontiguousarray(wk1[:, hq]),
            a1wv=np.ascontiguousarray(wv1[:, hv]),
            a1wg=np.ascontiguousarray(wg1[:, hv]),
            a1wb=np.ascontiguousarray(wb1[:, hh]),
            a1wa=np.ascontiguousarray(wa1[:, hh]),
            a1wo=np.ascontiguousarray(wo1[hv, :]),
            a1cwq=np.ascontiguousarray(np.transpose(cwq[hq].reshape(6, 128, 4), (1, 0, 2)).reshape(128, 24)),
            a1cwk=np.ascontiguousarray(np.transpose(cwk[hq].reshape(6, 128, 4), (1, 0, 2)).reshape(128, 24)),
            a1cwv=np.ascontiguousarray(np.transpose(cwv[hv].reshape(12, 128, 4), (1, 0, 2)).reshape(128, 48)),
            a1aldt=np.ascontiguousarray(np.stack([alog[hh], dtb[hh]], axis=1)),
            a1nw=np.ascontiguousarray(nw.reshape(1, 256)),
        )
        in_maps.append(m)
    return in_maps


def kernel_unfused(**inputs):
    xs = _shard_x(np.asarray(inputs["L0_attn_x"], np.float32))
    xs = run_ffn0(xs, inputs)
    pas, pbs = run_attn0(xs, inputs)
    xs = run_ffn0(xs, inputs, partials=(pas, pbs))
    xs = run_moe(xs, inputs)
    pas, pbs = run_attn1(xs, inputs)
    xs = run_moe(xs, inputs, partials=(pas, pbs))
    return _unshard_x(xs)


def kernel(**inputs):
    k = build_fused()
    res = k.run(_fused_in_maps(inputs))
    return _unshard_x([res.results[c]["xo"] for c in range(8)])
